# version 18
# baseline (speedup 1.0000x reference)
import numpy as np
import ml_dtypes
import concourse.bass as bass
import concourse.mybir as mybir
from concourse.bass_utils import run_bass_kernel_spmd

F32 = mybir.dt.float32
BF16 = mybir.dt.bfloat16
AF = mybir.ActivationFunctionType
ALU = mybir.AluOpType
AX = mybir.AxisListType

D = 1024
T = 4096
TC = 512
NTC = 8
KD = 8
H16 = 16
HID = 2048
NEXP = 8
SCALE = 0.125
CSHIFT = -12.0
EPS = 1e-5
GMASK = 60.0

_BF = ml_dtypes.bfloat16

ALL_PHASES = ("ln1", "attna", "proja", "ffn1", "ln3",
              "attnb", "projb", "ffn2")


def _host_tables():
    inv = 1.0 / (10000.0 ** (np.arange(0, 64, 2, dtype=np.float64) / 64))
    freqs = np.arange(T, dtype=np.float64)[:, None] * inv[None, :]
    fr = np.concatenate([freqs, freqs], axis=1)
    cos64 = np.cos(fr)
    sgn = np.where(np.arange(64) < 32, -1.0, 1.0)[None, :]
    sinP = np.sin(fr) * sgn
    cosF = np.ascontiguousarray(np.tile(cos64.T, (2, 1)).astype(np.float32))
    sinF = np.ascontiguousarray(np.tile(sinP.T, (2, 1)).astype(np.float32))
    return cosF, sinF


class Prog:
    ENGS = ("sync", "gpsimd", "vector", "scalar", "tensor")

    def __init__(self, nc):
        self.nc = nc
        self.q = {e: [] for e in self.ENGS}
        self.sem = {}
        self.cnt = {}
        self.lastwait = {e: {} for e in self.ENGS}

    def new_sem(self, name):
        self.sem[name] = self.nc.alloc_semaphore(name)
        self.cnt[name] = 0

    def val(self, name):
        return self.cnt[name]

    def snap(self, *names):
        return [(n, self.cnt[n]) for n in names]

    def op(self, eng, fn, waits=(), inc=None):
        lw = self.lastwait[eng]
        w = []
        for s, v in waits:
            if v > 0 and lw.get(s, 0) < v:
                lw[s] = v
                w.append((self.sem[s], v))
        vinc = None
        if inc is not None:
            self.cnt[inc[0]] += inc[1]
            vinc = (self.sem[inc[0]], inc[1])

        def run(e, fn=fn, w=tuple(w), vinc=vinc):
            for s, v in w:
                e.wait_ge(s, v)
            ins = fn(e)
            if vinc is not None and ins is not None:
                ins.then_inc(vinc[0], vinc[1])
        self.q[eng].append(run)

    def wait(self, eng, waits):
        self.op(eng, lambda e: None, waits=waits)

    def replay(self, blk):
        def mk(name):
            def body(eng):
                for fn in self.q[name]:
                    fn(eng)
            return body
        blk.sync(mk("sync"))
        blk.gpsimd(mk("gpsimd"))
        blk.vector(mk("vector"))
        blk.scalar(mk("scalar"))
        blk.tensor(mk("tensor"))


ECTR = {"vector": "dve", "scalar": "acts", "gpsimd": "pool"}


def build_program(phases=ALL_PHASES):
    nc = bass.Bass("TRN2", target_bir_lowering=False, debug=False)
    P = Prog(nc)
    for s in ("ld", "g", "pe", "dve", "acts", "pool"):
        P.new_sem(s)

    def din(name, shape):
        return nc.dram_tensor(name, list(shape), F32, kind="ExternalInput").ap()

    xT_in = din("xT", (D, T))
    wqaT = din("wqaT", (D, D)); wkaT = din("wkaT", (D, D))
    wvaT = din("wvaT", (D, D)); woaT = din("woaT", (D, D))
    Wqb = din("Wqb", (D, D)); Wkvb = din("Wkvb", (D, D)); Woutb = din("Woutb", (D, D))
    boutb = din("boutb", (1, D))
    winT = din("winT", (D, 2 * HID)); w1T = din("w1T", (D, 2 * HID))
    woutT = din("woutT", (HID, D))
    w2T = din("w2T", (NEXP, HID, D))
    wgeT = din("wgeT", (D, 10))
    gbeb = din("gbeb", (1, 10))
    cosF = din("cosF", (128, T)); sinF = din("sinF", (128, T))

    out_T = nc.dram_tensor("out_T", [D, T], F32, kind="ExternalOutput").ap()

    def dscr(name, shape, dt):
        return nc.dram_tensor(name, list(shape), dt, kind="Internal").ap()

    x_h0 = dscr("x_h0", (D, T), F32)
    x_h1 = dscr("x_h1", (D, T), F32)
    xn_h = dscr("xn_h", (D, T), BF16)
    ao_h = dscr("ao_h", (D, T), BF16)
    e_h = dscr("e_h", (HID, T), BF16)
    hh_h = dscr("hh_h", (HID, T), BF16)
    w2bf_h = dscr("w2bf_h", (NEXP, HID, D), BF16)
    wffbf_h = dscr("wffbf_h", (D, 4 * HID), BF16)
    woutbf_h = dscr("woutbf_h", (HID, D), BF16)
    wse2_h = dscr("wse2_h", (1, NEXP * T), BF16)

    ident_h = nc.inline_tensor(np.eye(128).astype(_BF), name="ident128").ap()

    from contextlib import ExitStack
    root = ExitStack()

    def sbuf(n, s, d):
        return root.enter_context(nc.sbuf_tensor(n, list(s), d))

    ones_col_f = sbuf("ones_col_f", (128, 1), F32)
    ones_col_b = sbuf("ones_col_b", (128, 1), BF16)
    ones_row_f = sbuf("ones_row_f", (1, 128), F32)
    ones_row_b = sbuf("ones_row_b", (1, 128), BF16)
    shiftc = sbuf("shiftc", (128, 1), F32)
    eps_t = sbuf("eps_t", (1, 1), F32)
    ident_s = sbuf("ident_s", (128, 128), BF16)
    stat_s = sbuf("stat_s", (1, 5 * TC), F32)
    gbias_s = sbuf("gbias_s", (128, 10), F32)
    boutb_s = sbuf("boutb_s", (128, KD), F32)
    grow_s = sbuf("grow_s", (1, 10), F32)
    g320_s = sbuf("g320_s", (128, 320), F32)

    for buf, v in ((ones_col_f, 1.0), (ones_col_b, 1.0), (ones_row_f, 1.0),
                   (ones_row_b, 1.0), (shiftc, CSHIFT), (eps_t, EPS)):
        P.op("vector", lambda e, b=buf, vv=v: e.memset(b[:], vv), inc=("dve", 1))
    P.op("sync", lambda e: e.dma_start(ident_s[:], ident_h[:]), inc=("ld", 16))
    P.op("sync", lambda e: e.dma_start(grow_s[:], gbeb[:]), inc=("ld", 16))
    for f in range(KD):
        P.op("sync", lambda e, f=f: e.dma_start(
            boutb_s[:, f:f + 1],
            boutb[0:1, 128 * f:128 * (f + 1)].rearrange("o p -> p o")),
            inc=("ld", 16))
    INIT = P.snap("ld", "dve")
    pgb_g = nc.psum_tensor("pgb", [128, 16], F32)
    with pgb_g as pgb:
        P.op("tensor", lambda e: e.matmul(pgb[:, 0:10], ones_row_f[:], grow_s[:],
                                          start=True, stop=True),
             waits=INIT, inc=("pe", 1))
        P.op("vector", lambda e: e.tensor_copy(gbias_s[:], pgb[:, 0:10]),
             waits=P.snap("pe"), inc=("dve", 1))
    for j in range(32):
        P.op("vector", lambda e, j=j: e.tensor_copy(
            g320_s[:, 10 * j:10 * (j + 1)], gbias_s[:, 0:10]),
            waits=P.snap("dve"), inc=("dve", 1))
    CONST_READY = P.snap("dve", "ld")

    def dma(eng, dst, src, waits=()):
        sem = "ld" if eng == "sync" else "g"
        P.op(eng, lambda e, d=dst, s=src: e.dma_start(d, s),
             waits=waits, inc=(sem, 16))
        return P.val(sem)

    def mm(out, lhsT, rhs, start, stop, waits=(), inc=False, tp=None):
        P.op("tensor",
             lambda e, o=out, l=lhsT, r=rhs, st=start, sp=stop, t=tp:
             e.matmul(o, l, r, start=st, stop=sp, tile_position=t),
             waits=waits, inc=("pe", 1) if inc else None)
        return P.val("pe")

    def ew(eng, fn, waits=()):
        c = ECTR[eng]
        P.op(eng, fn, waits=waits, inc=(c, 1))
        return P.val(c)

    def vcopy(out, in_, waits=()):
        return ew("vector", lambda e, o=out, a=in_: e.tensor_copy(o, a), waits)

    def acopy(out, in_, waits=()):
        return ew("scalar", lambda e, o=out, a=in_: e.activation(o, a, AF.Copy), waits)

    def vtt(out, a, b, op, waits=()):
        return ew("vector", lambda e, o=out, x=a, y=b, p=op: e.tensor_tensor(o, x, y, p), waits)

    def ptt(out, a, b, op, waits=()):
        return ew("gpsimd", lambda e, o=out, x=a, y=b, p=op: e.tensor_tensor(o, x, y, p), waits)

    def vact(out, in_, func, bias=0.0, scale=1.0, accum=None, waits=()):
        return ew("scalar",
                  lambda e, o=out, a=in_, f=func, b=bias, s=scale, ac=accum:
                  e.activation(o, a, f, bias=b, scale=s, accum_out=ac), waits)

    def xchunks(hbm, tc):
        return hbm.rearrange("(k p) t -> p k t", p=128)[:, :, tc * TC:(tc + 1) * TC]

    def conv_w(src, dst_tiles, scratch, cache=None):
        for i, dt_ in enumerate(dst_tiles):
            dma("sync", scratch[:], src[128 * i:128 * (i + 1), :],
                waits=P.snap("dve"))
            dv = vcopy(dt_, scratch[:], waits=P.snap("ld"))
            if cache is not None:
                dma("gpsimd", cache[128 * i:128 * (i + 1), :], dt_,
                    waits=[("dve", dv)])

    def emit_ln(x_src):
        ctx = ExitStack()
        tag = str(len(P.q["sync"]))
        sb = lambda n, s, d: ctx.enter_context(nc.sbuf_tensor(n + "_" + tag, list(s), d))
        lx = sb("ln_x", (128, 2 * KD * TC), F32)
        lsq = sb("ln_sq", (128, 2 * KD * TC), BF16)
        lt = sb("ln_t", (128, 2 * KD * TC), F32)
        lxn = sb("ln_xn", (128, 2 * KD * TC), BF16)
        mrep_s = sb("ln_mrep", (128, TC), F32)
        ln_st = sb("ln_st", (1, 10 * TC), F32)
        pssum2 = ctx.enter_context(nc.psum_tensor("ln_ps_" + tag, [1, 2 * TC], F32))
        psq2 = ctx.enter_context(nc.psum_tensor("ln_pq_" + tag, [1, 2 * TC], F32))
        pm = ctx.enter_context(nc.psum_tensor("ln_pm_" + tag, [128, TC], F32))
        pr = ctx.enter_context(nc.psum_tensor("ln_pr_" + tag, [128, TC], F32))
        with ctx:
            prev = {}
            for tc in range(NTC):
                par = tc % 2
                off = par * KD * TC
                so_ = par * 5 * TC
                SM = ln_st[0:1, so_:so_ + TC]
                MEAN = ln_st[0:1, so_ + TC:so_ + 2 * TC]
                RINV = ln_st[0:1, so_ + 2 * TC:so_ + 3 * TC]
                VAR = ln_st[0:1, so_ + 3 * TC:so_ + 4 * TC]
                SD = ln_st[0:1, so_ + 4 * TC:so_ + 5 * TC]
                pssum = pssum2[0:1, par * TC:(par + 1) * TC]
                psq = psq2[0:1, par * TC:(par + 1) * TC]
                xv = lambda k, off=off: lx[:, off + k * TC:off + (k + 1) * TC]
                w = list(CONST_READY) + prev.get(par, [])
                dma("sync", lx[:, off:off + KD * TC].rearrange(
                    "p (k t) -> p k t", k=KD), xchunks(x_src, tc), waits=w)
                ldv = P.snap("ld")
                sqv = []
                for k in range(KD):
                    sqv.append(vact(lsq[:, off + k * TC:off + (k + 1) * TC],
                                    xv(k), AF.Square,
                                    waits=ldv + prev.get(("sq", par), [])))
                wstat = ldv + prev.get(("stats", par), [])
                for k in range(KD):
                    mm(pssum, ones_col_f[:], xv(k), k == 0, k == KD - 1,
                       waits=wstat, inc=(k == KD - 1))
                for k in range(KD):
                    mm(psq, ones_col_b[:], lsq[:, off + k * TC:off + (k + 1) * TC],
                       k == 0, k == KD - 1,
                       waits=wstat + [("acts", sqv[k])], inc=(k == KD - 1))
                pev = P.snap("pe")
                d0 = ew("vector", lambda e, M=MEAN, ps=pssum:
                        e.tensor_scalar_mul(M, ps, 1.0 / D), waits=pev)
                d0 = ew("vector", lambda e, V=VAR, ps=psq:
                        e.tensor_scalar_mul(V, ps, 1.0 / D), waits=[("dve", d0)])
                d0 = vtt(SM, MEAN, MEAN, ALU.mult, waits=[("dve", d0)])
                d1 = vtt(VAR, VAR, SM, ALU.subtract, waits=[("dve", d0)])
                prev[("stats", par)] = [("dve", d1)]
                a1 = vact(SD, VAR, AF.Ln, bias=eps_t[0:1, 0:1],
                          waits=[("dve", d1)] + CONST_READY)
                a1 = vact(RINV, SD, AF.Exp, scale=-0.5, waits=[("acts", a1)])
                mm(pm[:], ones_row_f[:], MEAN, True, True,
                   waits=[("dve", d1)] + prev.get("reps", []))
                pev2 = mm(pr[:], ones_row_f[:], RINV, True, True,
                          waits=[("acts", a1)], inc=True)
                a2 = acopy(mrep_s[:], pm[:], waits=[("pe", pev2)])
                pl = []
                for k in range(KD):
                    pl.append(ptt(lt[:, off + k * TC:off + (k + 1) * TC],
                                  xv(k), mrep_s[:], ALU.subtract,
                                  waits=[("acts", a2)] + prev.get(("lt", par), [])))
                dv = []
                for k in range(KD):
                    dv.append(vtt(lxn[:, off + k * TC:off + (k + 1) * TC],
                                  lt[:, off + k * TC:off + (k + 1) * TC],
                                  pr[:], ALU.mult, waits=[("pool", pl[k])]))
                gv = dma("gpsimd", xchunks(xn_h, tc),
                         lxn[:, off:off + KD * TC].rearrange("p (k t) -> p k t", k=KD),
                         waits=[("dve", dv[-1])])
                prev[par] = [("pool", pl[-1]), ("acts", sqv[-1]), ("g", gv),
                             ("pe", P.val("pe")), ("dve", dv[-1])]
                prev[("sq", par)] = [("pe", P.val("pe"))]
                prev[("lt", par)] = [("dve", dv[-1])]
                prev["reps"] = [("dve", dv[-1]), ("acts", a2)]
            P.wait("sync", P.snap("g", "pool", "dve", "acts", "pe"))

    def emit_outproj(Wsrc, x_src, x_dst, bias):
        ctx = ExitStack()
        tag = str(len(P.q["sync"]))
        sb = lambda n, s, d: ctx.enter_context(nc.sbuf_tensor(n + "_" + tag, list(s), d))
        wsb = sb("op_w", (128, KD * D), BF16)
        wscr = sb("op_ws", (128, D), F32)
        aosb = sb("op_ao", (128, 2 * KD * TC), BF16)
        xsb = sb("op_x", (128, 2 * KD * TC), F32)
        osb = sb("op_o", (128, 2 * KD * TC), F32)
        lsq = sb("op_sq", (128, 2 * KD * TC), BF16)
        lt = sb("op_lt", (128, 2 * KD * TC), F32)
        lxn = sb("op_xn", (128, 2 * KD * TC), BF16)
        mrep_s = sb("op_mr", (128, TC), F32)
        op_st = sb("op_st", (1, 10 * TC), F32)
        p0 = ctx.enter_context(nc.psum_tensor("op_p0_" + tag, [128, TC], F32))
        p1 = ctx.enter_context(nc.psum_tensor("op_p1_" + tag, [128, TC], F32))
        pssum2 = ctx.enter_context(nc.psum_tensor("op_ps_" + tag, [1, 2 * TC], F32))
        psq2 = ctx.enter_context(nc.psum_tensor("op_pq_" + tag, [1, 2 * TC], F32))
        pm = ctx.enter_context(nc.psum_tensor("op_pm_" + tag, [128, TC], F32))
        pr = ctx.enter_context(nc.psum_tensor("op_pr_" + tag, [128, TC], F32))
        with ctx:
            conv_w(Wsrc, [wsb[:, i * D:(i + 1) * D] for i in range(KD)], wscr)
            CW = P.snap("dve")
            pp = [p0, p1]
            prev = {}
            for tc in range(NTC):
                par = tc % 2
                off = par * KD * TC
                w = list(CW) + prev.get(par, [])
                dma("sync", aosb[:, off:off + KD * TC].rearrange(
                    "p (k t) -> p k t", k=KD), xchunks(ao_h, tc), waits=w)
                ldv = P.snap("ld")
                dma("gpsimd", xsb[:, off:off + KD * TC].rearrange(
                    "p (k t) -> p k t", k=KD), xchunks(x_src, tc), waits=w)
                gv = P.snap("g")
                dvs = []
                for f in range(KD):
                    pb = pp[f % 2]
                    if len(dvs) >= 2:
                        wmm = ldv + [("dve", dvs[-2])]
                    else:
                        wmm = ldv + prev.get(("ps", par, f % 2), [])
                    for k in range(KD):
                        mm(pb[:], wsb[:, k * D + 128 * f:k * D + 128 * (f + 1)],
                           aosb[:, off + k * TC:off + (k + 1) * TC],
                           k == 0, k == KD - 1, waits=wmm, inc=(k == KD - 1))
                    pev = P.snap("pe")
                    o = osb[:, off + f * TC:off + (f + 1) * TC]
                    xin = xsb[:, off + f * TC:off + (f + 1) * TC]
                    if bias is None:
                        dvs.append(vtt(o, pb[:], xin, ALU.add, waits=pev + gv))
                    else:
                        dvs.append(ew("vector",
                                      lambda e, o=o, a=pb[:], s=bias[:, f:f + 1],
                                      b=xin: e.scalar_tensor_tensor(
                                          o, a, s, b, ALU.add, ALU.add),
                                      waits=pev + gv + CONST_READY))
                gv2 = dma("gpsimd", xchunks(x_dst, tc),
                          osb[:, off:off + KD * TC].rearrange("p (k t) -> p k t", k=KD),
                          waits=[("dve", dvs[-1])])
                OK = [("dve", dvs[-1])]
                ov = lambda k: osb[:, off + k * TC:off + (k + 1) * TC]
                so_ = par * 5 * TC
                SM = op_st[0:1, so_:so_ + TC]
                MEAN = op_st[0:1, so_ + TC:so_ + 2 * TC]
                RINV = op_st[0:1, so_ + 2 * TC:so_ + 3 * TC]
                VAR = op_st[0:1, so_ + 3 * TC:so_ + 4 * TC]
                SD = op_st[0:1, so_ + 4 * TC:so_ + 5 * TC]
                pssum = pssum2[0:1, par * TC:(par + 1) * TC]
                psq = psq2[0:1, par * TC:(par + 1) * TC]
                sqv = []
                for k in range(KD):
                    sqv.append(vact(lsq[:, off + k * TC:off + (k + 1) * TC],
                                    ov(k), AF.Square,
                                    waits=OK + prev.get(("sq", par), [])))
                wstat = OK + prev.get(("stats", par), [])
                for k in range(KD):
                    mm(pssum, ones_col_f[:], ov(k), k == 0, k == KD - 1,
                       waits=wstat, inc=(k == KD - 1))
                for k in range(KD):
                    mm(psq, ones_col_b[:],
                       lsq[:, off + k * TC:off + (k + 1) * TC],
                       k == 0, k == KD - 1,
                       waits=wstat + [("acts", sqv[k])], inc=(k == KD - 1))
                pev = P.snap("pe")
                d0 = ew("vector", lambda e, M=MEAN, ps=pssum:
                        e.tensor_scalar_mul(M, ps, 1.0 / D), waits=pev)
                d0 = ew("vector", lambda e, V=VAR, ps=psq:
                        e.tensor_scalar_mul(V, ps, 1.0 / D), waits=[("dve", d0)])
                d0 = vtt(SM, MEAN, MEAN, ALU.mult, waits=[("dve", d0)])
                d1 = vtt(VAR, VAR, SM, ALU.subtract, waits=[("dve", d0)])
                prev[("stats", par)] = [("dve", d1)]
                a1 = vact(SD, VAR, AF.Ln, bias=eps_t[0:1, 0:1],
                          waits=[("dve", d1), ("acts", sqv[-1])] + CONST_READY)
                a1 = vact(RINV, SD, AF.Exp, scale=-0.5, waits=[("acts", a1)])
                mm(pm[:], ones_row_f[0:1, :], MEAN, True, True,
                   waits=[("dve", d1)] + prev.get("reps", []))
                pev2 = mm(pr[:], ones_row_f[0:1, :], RINV, True, True,
                          waits=[("acts", a1)], inc=True)
                a2 = acopy(mrep_s[:], pm[:], waits=[("pe", pev2)])
                pl = []
                for k in range(KD):
                    pl.append(ptt(lt[:, off + k * TC:off + (k + 1) * TC],
                                  ov(k), mrep_s[:], ALU.subtract,
                                  waits=[("acts", a2)] + prev.get(("lt", par), [])))
                dv = []
                for k in range(KD):
                    dv.append(vtt(lxn[:, off + k * TC:off + (k + 1) * TC],
                                  lt[:, off + k * TC:off + (k + 1) * TC],
                                  pr[:], ALU.mult, waits=[("pool", pl[k])]))
                gv3 = dma("gpsimd", xchunks(xn_h, tc),
                          lxn[:, off:off + KD * TC].rearrange(
                              "p (k t) -> p k t", k=KD),
                          waits=[("dve", dv[-1])])
                prev[par] = [("dve", dv[-1]), ("g", gv3), ("pe", P.val("pe")),
                             ("pool", pl[-1]), ("acts", sqv[-1])]
                prev[("sq", par)] = [("pe", P.val("pe"))]
                prev[("lt", par)] = [("dve", dv[-1])]
                prev["reps"] = [("dve", dv[-1]), ("acts", a2)]
                prev[("ps", par, 0)] = [("dve", dvs[-2])]
                prev[("ps", par, 1)] = [("dve", dvs[-1])]
            P.wait("sync", P.snap("g", "dve", "pe", "acts", "pool"))

    CONV = {"done": []}

    def conv_jobs():
        jobs = []
        for i in range(KD):
            for j in range(8):
                src_m = winT if j < 4 else w1T
                jc = 1024 * (j % 4)
                jobs.append((src_m[128 * i:128 * (i + 1), jc:jc + 1024],
                             wffbf_h[128 * i:128 * (i + 1),
                                     1024 * j:1024 * (j + 1)]))
        for ex in range(NEXP):
            for k in range(H16):
                jobs.append((w2T[ex, 128 * k:128 * (k + 1), :],
                             w2bf_h[ex, 128 * k:128 * (k + 1), :]))
        for i in range(H16):
            jobs.append((woutT[128 * i:128 * (i + 1), :],
                         woutbf_h[128 * i:128 * (i + 1), :]))
        return jobs

    def run_scores(nchunks, sp2, ptsb, qkops, pvops, epiops, feeds, chunk_w):
        expv = {}
        pvv = {}
        smv = {}
        normd = []
        prevstg = []
        total = nchunks * 256
        for jg in range(total + 2):
            c, r = divmod(jg, 256)
            fl = feeds.get(c + 1)
            if fl is not None and r >= 32 and (r - 32) % 4 == 0:
                ui = (r - 32) // 4
                if ui < len(fl):
                    fl[ui]()
            if jg < total:
                w = [("acts", expv[jg - 2])] if jg >= 2 else []
                if r < 2:
                    w = w + chunk_w.get(c, [])
                smv[jg] = qkops(jg, w)
            j2 = jg - 2
            if j2 >= 0:
                if j2 % 32 == 0 and j2 > 0:
                    normd, prevstg = epiops(j2 // 32 - 1, pvv[j2 - 1],
                                            normd, prevstg)
                w = [("acts", expv[j2])]
                if j2 % 32 == 0:
                    w = w + normd
                    if j2 % 256 == 0:
                        w = w + chunk_w.get(j2 // 256, [])
                pvv[j2] = pvops(j2, w, j2 % 32 == 0, j2 % 32 == 31)
            if jg < total:
                g = pvv[jg - 2] if jg >= 2 else smv[jg]
                ip = jg % 2
                expv[jg] = vact(ptsb[:, ip * 1024:(ip + 1) * 1024], sp2[ip][:],
                                AF.Exp, bias=shiftc[:, 0:1], scale=SCALE,
                                waits=[("pe", g)] + CONST_READY)
        epiops(total // 32 - 1, pvv[total - 1], normd, prevstg)

    def mk_epilogue(oAB, rp, repsb, aostg, store_fn, end_cb):
        est = {"cp": 0, "rppe": {}}
        lnrow = stat_s[0:1, 2048:2560]

        def epiops(e, pvlast, normd, prevstg):
            spar = e % 2
            rAB = stat_s[0:1, spar * 1024:(spar + 1) * 1024]
            w = [("pe", pvlast)]
            if e >= 2:
                w.append(("pe", est["rppe"][e - 2]))
            rv = []
            for hh in range(2):
                aL = vact(lnrow, oAB[64:65, hh * TC:(hh + 1) * TC], AF.Ln,
                          waits=(w if hh == 0 else [("acts", rv[-1])]))
                rv.append(vact(rAB[0:1, hh * TC:(hh + 1) * TC], lnrow,
                               AF.Exp, scale=-1.0, waits=[("acts", aL)]))
            wrp0 = [("acts", rv[0])] + ([("dve", est["cp"])] if est["cp"] else [])
            mm(rp[0:64, :], ones_row_f[0:1, 0:64], rAB[0:1, 0:TC],
               True, True, waits=wrp0)
            rpe = mm(rp[64:128, :], ones_row_f[0:1, 0:64],
                     rAB[0:1, TC:1024], True, True, inc=True,
                     waits=[("acts", rv[1])])
            est["rppe"][e] = rpe
            cp = vcopy(repsb[:], rp[:], waits=[("pe", rpe)])
            est["cp"] = cp
            n1 = vtt(aostg[0:64, e % 2 * TC:(e % 2 + 1) * TC],
                     oAB[0:64, 0:TC], repsb[0:64, :], ALU.mult,
                     waits=[("dve", cp)] + prevstg)
            n2 = vtt(aostg[64:128, e % 2 * TC:(e % 2 + 1) * TC],
                     oAB[0:64, TC:1024], repsb[64:128, :], ALU.mult,
                     waits=[("dve", n1)])
            gq = store_fn(e, spar, n2)
            end_cb(e, pvlast)
            return [("dve", n2)], [("g", gq)]
        return epiops

    def emit_attn_a():
        ctx = ExitStack()
        tag = str(len(P.q["sync"]))
        sb = lambda n, s, d: ctx.enter_context(nc.sbuf_tensor(n + "_" + tag, list(s), d))
        wsb = sb("aa_w", (128, 3 * KD * D), BF16)
        wscr = sb("aa_ws", (128, D), F32)
        xnsb = sb("aa_xn", (128, 2 * KD * TC), BF16)
        qksb = sb("aa_qk", (128, 2 * 2 * T), BF16)
        vpsb = sb("aa_vp", (128, 2 * 4 * 1040), BF16)
        ptsb = sb("aa_pt", (128, 2 * 1024), BF16)
        repsb = sb("aa_rep", (128, TC), F32)
        aostg = sb("aa_stg", (128, 2 * TC), BF16)
        cvt_f = sb("aa_cvf", (128, 2 * 1024), F32)
        cvt_b = sb("aa_cvb", (128, 2 * 1024), BF16)
        s0 = ctx.enter_context(nc.psum_tensor("aa_s0_" + tag, [128, 1024], F32))
        s1 = ctx.enter_context(nc.psum_tensor("aa_s1_" + tag, [128, 1024], F32))
        oAB = ctx.enter_context(nc.psum_tensor("aa_o_" + tag, [65, 1024], F32))
        pj = ctx.enter_context(nc.psum_tensor("aa_pj_" + tag, [128, TC], F32))
        rp = ctx.enter_context(nc.psum_tensor("aa_rp_" + tag, [128, TC], F32))
        with ctx:
            for wi, wsrc in enumerate((wqaT, wkaT, wvaT)):
                conv_w(wsrc, [wsb[:, (wi * KD + i) * D:(wi * KD + i + 1) * D]
                              for i in range(KD)], wscr)
            ew("vector", lambda e: e.memset(vpsb[:], 1.0))
            CW = P.snap("dve")
            sp2 = [s0, s1]
            st = {"pair_end": {}, "ev": None}
            chunk_w = {}
            est = {"cp": 0, "rppe": {}}

            def proj_units(p):
                par = p % 2
                xno = par * KD * TC
                qko = par * 2 * T
                vpo = par * 4 * 1040
                state = {"ld": []}
                units = []

                def u_load():
                    gpe = st["pair_end"].get(p - 2, 0)
                    dma("sync", xnsb[:, xno:xno + KD * TC].rearrange(
                        "p (k t) -> p k t", k=KD), xchunks(xn_h, p),
                        waits=[("pe", gpe)] + CW)
                    state["ld"] = P.snap("ld")
                units.append(u_load)

                def mk_qkf(wi, f):
                    def u():
                        gpe = st["pair_end"].get(p - 2, 0)
                        wmm = state["ld"] + CW + \
                            ([st["ev"]] if st["ev"] else [])
                        for k in range(KD):
                            mm(pj[:], wsb[:, (wi * KD + k) * D + 128 * f:
                                          (wi * KD + k) * D + 128 * (f + 1)],
                               xnsb[:, xno + k * TC:xno + (k + 1) * TC],
                               k == 0, k == KD - 1, waits=wmm,
                               inc=(k == KD - 1))
                        pev = P.snap("pe")
                        dsto = qko + wi * T
                        last = 0
                        for cpar in range(2):
                            for th in range(2):
                                src = pj[64 * cpar:64 * (cpar + 1),
                                         256 * th:256 * (th + 1)]
                                dst = qksb[64 * th:64 * (th + 1),
                                           dsto + (2 * f + cpar) * 256:
                                           dsto + (2 * f + cpar) * 256 + 256]
                                last = vcopy(dst, src,
                                             waits=pev + [("pe", gpe)])
                        st["ev"] = ("dve", last)
                    return u
                for wi in range(2):
                    for f in range(KD):
                        units.append(mk_qkf(wi, f))

                def mk_v(ttl, n):
                    def u():
                        gpe = st["pair_end"].get(p - 2, 0)
                        wmm = state["ld"] + CW + \
                            ([st["ev"]] if st["ev"] else [])
                        for k in range(KD):
                            mm(pj[:], xnsb[:, xno + k * TC + 128 * ttl:
                                           xno + k * TC + 128 * (ttl + 1)],
                               wsb[:, (2 * KD + k) * D + n * TC:
                                   (2 * KD + k) * D + (n + 1) * TC],
                               k == 0, k == KD - 1, waits=wmm,
                               inc=(k == KD - 1))
                        pev = P.snap("pe")
                        base = vpo + ttl * 1040 + n * 8 * 65
                        dst = vpsb[:, base:base + 8 * 65].rearrange(
                            "p (c d) -> p c d", c=8)[:, :, 0:64]
                        srcap = pj[:].rearrange("p (c d) -> p c d", c=8)
                        v = vcopy(dst, srcap, waits=pev + [("pe", gpe)])
                        st["ev"] = ("dve", v)
                        if (ttl, n) == (3, 1):
                            chunk_w[p] = [("dve", v)]
                    return u
                for ttl in range(4):
                    for n in range(2):
                        units.append(mk_v(ttl, n))
                return units

            def qkops(jg, w):
                p, r = divmod(jg, 256)
                qc, kt = divmod(r, 32)
                qko = (p % 2) * 2 * T
                sbk = sp2[jg % 2]
                mm(sbk[:, 0:TC],
                   qksb[0:64, qko + T + kt * 128:qko + T + (kt + 1) * 128],
                   qksb[0:64, qko + qc * TC:qko + (qc + 1) * TC],
                   True, True, waits=w, tp=(0, 0))
                return mm(sbk[:, TC:1024],
                          qksb[64:128, qko + T + kt * 128:qko + T + (kt + 1) * 128],
                          qksb[64:128, qko + qc * TC:qko + (qc + 1) * TC],
                          True, True, tp=(64, 0), inc=True)

            def pvops(jg, w, start, stop):
                p, r = divmod(jg, 256)
                qc, kt = divmod(r, 32)
                vpo = (p % 2) * 4 * 1040
                ip = jg % 2
                vA = vpsb[:, vpo + (kt % 2) * 1040 + (kt // 2) * 65:
                          vpo + (kt % 2) * 1040 + (kt // 2) * 65 + 65]
                vB = vpsb[:, vpo + (2 + kt % 2) * 1040 + (kt // 2) * 65:
                          vpo + (2 + kt % 2) * 1040 + (kt // 2) * 65 + 65]
                mm(oAB[:, 0:TC], vA, ptsb[:, ip * 1024:ip * 1024 + TC],
                   start, stop, waits=w)
                return mm(oAB[:, TC:1024], vB,
                          ptsb[:, ip * 1024 + TC:(ip + 1) * 1024],
                          start, stop, inc=True)

            def store_a(e, spar, n2):
                p, qc = divmod(e, 8)
                gq = 0
                for hh in range(2):
                    hgl = 2 * p + hh
                    src = aostg[64 * hh:64 * (hh + 1),
                                spar * TC:(spar + 1) * TC].rearrange(
                        "d (c s) -> d c s", c=2)
                    dst = ao_h.rearrange("(c dd) t -> dd c t", dd=64)[
                        :, 2 * qc:2 * qc + 2, 256 * hgl:256 * (hgl + 1)]
                    gq = dma("gpsimd", dst, src, waits=[("dve", n2)])
                return gq

            def end_a(e, pvlast):
                p, qc = divmod(e, 8)
                if qc == 7:
                    st["pair_end"][p] = pvlast

            epiops = mk_epilogue(oAB, rp, repsb, aostg, store_a, end_a)

            cst = {}

            def mk_cu(i, src, dst):
                par = i % 2

                def u():
                    f_ap = cvt_f[:, par * 1024:(par + 1) * 1024]
                    b_ap = cvt_b[:, par * 1024:(par + 1) * 1024]
                    dma("sync", f_ap, src,
                        waits=[("dve", cst.get(("c", par), 0))])
                    cv = vcopy(b_ap, f_ap,
                               waits=P.snap("ld") + [("g", cst.get(("s", par), 0))])
                    cst[("c", par)] = cv
                    gv = dma("gpsimd", dst, b_ap, waits=[("dve", cv)])
                    cst[("s", par)] = gv
                return u

            cunits = [mk_cu(i, s, d) for i, (s, d) in enumerate(conv_jobs())]

            for u in proj_units(0):
                u()
            feeds = {}
            for p in range(1, 8):
                pu = proj_units(p)
                take, cunits = cunits[:30], cunits[30:]
                merged = []
                for a, b in zip(pu, take):
                    merged += [a, b]
                merged += pu[len(take):] + take[len(pu):]
                feeds[p] = merged
            run_scores(8, sp2, ptsb, qkops, pvops, epiops, feeds, chunk_w)
            assert not cunits
            P.wait("sync", P.snap("g", "pe", "dve", "acts"))
            CONV["done"] = P.snap("ld", "g", "dve")

    def emit_ffnA(call):
        ctx = ExitStack()
        tag = str(len(P.q["sync"]))
        sb = lambda n, s, d: ctx.enter_context(nc.sbuf_tensor(n + "_" + tag, list(s), d))
        wsb = sb("fa_w", (128, 2 * 4 * D), BF16)
        wscr = sb("fa_ws", (128, D), F32)
        xnsb = sb("fa_xn", (128, KD * T), BF16)
        sasb = sb("fa_sa", (128, 2 * 2 * TC), BF16)
        ehsb = sb("fa_eh", (128, 2 * 2 * TC), BF16)
        gwsb = sb("fa_gw", (128, 80), BF16)
        glb = sb("fa_gl", (128, 576), F32)
        eqb = sb("fa_eq", (128, 256), F32)
        wtb = sb("fa_wtb", (128, 256), BF16)
        wtstg = sb("fa_wts", (128, 256), BF16)
        pa = ctx.enter_context(nc.psum_tensor("fa_p_" + tag, [128, 2 * TC], F32))
        pa2 = ctx.enter_context(nc.psum_tensor("fa_p2_" + tag, [128, 2 * TC], F32))
        pb_ = ctx.enter_context(nc.psum_tensor("fa_pb_" + tag, [128, 2 * TC], F32))
        pb2 = ctx.enter_context(nc.psum_tensor("fa_pb2_" + tag, [128, 2 * TC], F32))

        def xnkt(k, tc):
            return xnsb[:, k * T + tc * TC:k * T + (tc + 1) * TC]

        def xnjt(k, j):
            return xnsb[:, k * T + 128 * j:k * T + 128 * (j + 1)]

        with ctx:
            for i in range(KD):
                dma("sync", wscr[:, 0:10], wgeT[128 * i:128 * (i + 1), :],
                    waits=P.snap("dve"))
                vcopy(gwsb[:, 10 * i:10 * (i + 1)], wscr[:, 0:10],
                      waits=P.snap("ld"))
            for tc in range(NTC):
                dma("gpsimd", xnsb[:].rearrange("p (k t) -> p k t", k=KD)[
                    :, :, tc * TC:(tc + 1) * TC], xchunks(xn_h, tc))
            gv = P.snap("g", "dve")
            NJ = 32
            GLb = glb[:, 0:10 * NJ]
            GL3 = GLb.rearrange("p (j c) -> p j c", j=NJ)
            Dlt = glb[:, 320:320 + NJ]
            SEL = glb[:, 352:352 + NJ]
            GP = glb[:, 384:384 + NJ]
            EMX = glb[:, 416:416 + NJ]
            ESM = glb[:, 448:448 + NJ]
            FCT = glb[:, 480:480 + NJ]
            V1 = glb[:, 512:512 + NJ]
            V2 = glb[:, 544:544 + NJ]
            EQ3 = eqb[:].rearrange("p (j c) -> p j c", j=NJ)
            WT3 = wtb[:].rearrange("p (j c) -> p j c", j=NJ)
            j3 = lambda ap: ap.rearrange("p (j o) -> p j o", j=NJ)
            for j in range(NJ):
                for k in range(KD):
                    mm(pb_[:, 10 * j:10 * (j + 1)], xnjt(k, j),
                       gwsb[:, 10 * k:10 * (k + 1)], k == 0, k == KD - 1,
                       waits=gv, inc=(j == NJ - 1 and k == KD - 1))
            d = vcopy(GLb, pb_[:, 0:10 * NJ], waits=P.snap("pe"))
            d = vtt(GLb, GLb, g320_s[:], ALU.add,
                    waits=[("dve", d)] + CONST_READY)
            d = vtt(j3(Dlt), GL3[:, :, 1:2], GL3[:, :, 0:1], ALU.subtract,
                    waits=[("dve", d)])
            d = ew("vector", lambda e: e.tensor_single_scalar(
                SEL, Dlt, 0.0, ALU.is_gt), waits=[("dve", d)])
            a = ew("scalar", lambda e: e.activation(GP, Dlt, AF.Abs),
                   waits=[("dve", d), ("acts", P.val("acts"))])
            a = vact(GP, GP, AF.Sigmoid, waits=[("acts", a)])
            d = ew("vector", lambda e: e.tensor_scalar_mul(SEL, SEL, GMASK),
                   waits=[("dve", d)])
            dm = d
            for j in range(NJ):
                dm = ew("vector", lambda e, j=j: e.tensor_scalar_sub(
                    GL3[:, j:j + 1, 2:6], GL3[:, j:j + 1, 2:6], SEL[:, j:j + 1]),
                    waits=[("dve", d)])
            d = ew("vector", lambda e: e.tensor_scalar(SEL, SEL, -1.0, GMASK,
                                                       ALU.mult, ALU.add),
                   waits=[("dve", dm)])
            dm = d
            for j in range(NJ):
                dm = ew("vector", lambda e, j=j: e.tensor_scalar_sub(
                    GL3[:, j:j + 1, 6:10], GL3[:, j:j + 1, 6:10], SEL[:, j:j + 1]),
                    waits=[("dve", d)])
            d = ew("vector", lambda e: e.reduce_max(
                j3(EMX), GL3[:, :, 2:10], axis=AX.X, negate=True),
                waits=[("dve", dm)])
            av = a
            for j in range(NJ):
                av = vact(GL3[:, j:j + 1, 2:10], GL3[:, j:j + 1, 2:10], AF.Exp,
                          bias=EMX[:, j:j + 1], accum=ESM[:, j:j + 1],
                          waits=[("dve", d), ("acts", a)])
            d = ew("vector", lambda e: e.reciprocal(FCT, ESM),
                   waits=[("acts", av)])
            d = vtt(FCT, FCT, GP, ALU.mult, waits=[("dve", d)])
            dm = d
            for j in range(NJ):
                dm = ew("vector", lambda e, j=j: e.tensor_scalar_mul(
                    GL3[:, j:j + 1, 2:10], GL3[:, j:j + 1, 2:10], FCT[:, j:j + 1]),
                    waits=[("dve", d)])
            d = ew("vector", lambda e: e.reduce_max(
                j3(V1), GL3[:, :, 2:10], axis=AX.X), waits=[("dve", dm)])
            dm = d
            for j in range(NJ):
                dm = ew("vector", lambda e, j=j: e.tensor_single_scalar(
                    EQ3[:, j:j + 1, :], GL3[:, j:j + 1, 2:10], V1[:, j:j + 1],
                    ALU.not_equal), waits=[("dve", d)])
            d = vtt(EQ3[:, :, :], GL3[:, :, 2:10], EQ3[:, :, :], ALU.mult,
                    waits=[("dve", dm)])
            d = ew("vector", lambda e: e.reduce_max(
                j3(V2), EQ3[:, :, :], axis=AX.X), waits=[("dve", d)])
            dm = d
            for j in range(NJ):
                dm = ew("vector", lambda e, j=j: e.tensor_single_scalar(
                    EQ3[:, j:j + 1, :], GL3[:, j:j + 1, 2:10], V2[:, j:j + 1], ALU.is_ge),
                    waits=[("dve", d)])
            for h in range(2):
                d = vtt(wtb[:, 128 * h:128 * (h + 1)].rearrange(
                            "p (e j) -> p j e", e=NEXP),
                        GL3[:, 16 * h:16 * (h + 1), 2:10],
                        EQ3[:, 16 * h:16 * (h + 1), :], ALU.mult,
                        waits=[("dve", dm)])
                P.op("tensor",
                     lambda e, o=pb2[:].bitcast(BF16)[0:128, 128 * h:128 * (h + 1)],
                     a=wtb[:, 128 * h:128 * (h + 1)], i=ident_s[:]:
                     e.transpose(o, a, i),
                     waits=[("dve", d)] + CONST_READY, inc=("pe", 1))
                te = P.val("pe")
                dv = vcopy(wtstg[:, 128 * h:128 * (h + 1)],
                           pb2[:].bitcast(BF16)[0:128, 128 * h:128 * (h + 1)],
                           waits=[("pe", te)])
                for ex in range(NEXP):
                    dma("gpsimd",
                        wse2_h[0:1, :].rearrange("o (e j c) -> o e j c",
                                                 e=NEXP, j=NJ)[
                            :, ex, 16 * h:16 * (h + 1), :],
                        wtstg[16 * ex:16 * (ex + 1),
                              128 * h:128 * (h + 1)],
                        waits=[("dve", dv)])
            GATE_DONE = [("dve", P.val("dve"))]

            prevw = {}
            for hc in range(H16):
                par = hc % 2
                wo = par * 4 * D

                def wcol(part, k, wo=wo):
                    return wsb[:, wo + part * D + 128 * k:wo + part * D + 128 * (k + 1)]

                wpar = prevw.get(par, [])
                for part in range(4):
                    colo = (part % 2) * HID + 128 * hc
                    cache_colo = (0 if part < 2 else 2 * HID) + colo
                    dma("sync",
                        wsb[:, wo + part * D:wo + (part + 1) * D].rearrange(
                            "p (k c) -> p k c", k=KD),
                        wffbf_h.rearrange("(k p) c -> p k c", p=128)[
                            :, :, cache_colo:cache_colo + 128],
                        waits=wpar + CONV["done"])
                WL = P.snap("ld", "dve")
                for tc in range(NTC):
                    tpar = tc % 2
                    pA_ = pa if tpar == 0 else pb_
                    pB_ = pa2 if tpar == 0 else pb2
                    wmm = WL + gv + prevw.get(("ps", tpar), []) + \
                        (GATE_DONE if (hc == 0 and tpar == 1) else [])
                    for part in range(4):
                        dst = (pA_ if part < 2 else pB_)[
                            :, (part % 2) * TC:(part % 2 + 1) * TC]
                        for k in range(KD):
                            mm(dst, wcol(part, k), xnkt(k, tc),
                               k == 0, k == KD - 1, waits=wmm, inc=(k == KD - 1))
                    pev = P.snap("pe")
                    so = tpar * 2 * TC
                    a1 = vact(sasb[:, so:so + TC], pA_[:, 0:TC], AF.Silu, waits=pev)
                    a2 = vact(sasb[:, so + TC:so + 2 * TC], pB_[:, 0:TC], AF.Silu,
                              waits=[("acts", a1)])
                    d1 = vtt(ehsb[:, so:so + TC], sasb[:, so:so + TC],
                             pA_[:, TC:2 * TC], ALU.mult, waits=[("acts", a1)])
                    d2 = vtt(ehsb[:, so + TC:so + 2 * TC], sasb[:, so + TC:so + 2 * TC],
                             pB_[:, TC:2 * TC], ALU.mult, waits=[("acts", a2)])
                    dma("gpsimd",
                        e_h[128 * hc:128 * (hc + 1), tc * TC:(tc + 1) * TC],
                        ehsb[:, so:so + TC], waits=[("dve", d1)])
                    dma("gpsimd",
                        hh_h[128 * hc:128 * (hc + 1), tc * TC:(tc + 1) * TC],
                        ehsb[:, so + TC:so + 2 * TC], waits=[("dve", d2)])
                    prevw[("ps", tpar)] = [("dve", d2), ("acts", a2),
                                           ("g", P.val("g"))]
                prevw[par] = [("pe", P.val("pe"))]
            P.wait("sync", P.snap("g", "pe", "dve", "acts"))

    def emit_ffnB(call, x_src, x_dst):
        ctx = ExitStack()
        tag = str(len(P.q["sync"]))
        sb = lambda n, s, d: ctx.enter_context(nc.sbuf_tensor(n + "_" + tag, list(s), d))
        wosb = sb("fb_wo", (128, H16 * D), BF16)
        wscr = sb("fb_ws", (128, D), F32)
        w2sb = sb("fb_w2", (128, 2 * 4 * D), BF16)
        esb = sb("fb_e", (128, H16 * TC), BF16)
        hsb = sb("fb_h", (128, H16 * TC), BF16)
        scs = sb("fb_scs", (128, 2 * H16 * TC), BF16)
        repsb = sb("fb_rep", (128, NEXP * TC), BF16)
        wserow = sb("fb_wr", (1, 2 * TC), BF16)
        xsb = sb("fb_x", (128, KD * TC), F32)
        osb = sb("fb_o", (128, KD * TC), F32)
        prt = ctx.enter_context(nc.psum_tensor("fb_p_" + tag, [128, 8 * TC], F32))
        with ctx:
            for i in range(H16):
                dma("sync", wosb[:, i * D:(i + 1) * D],
                    woutbf_h[128 * i:128 * (i + 1), :], waits=CONV["done"])
            CW = P.snap("dve", "ld")
            prev = {}
            w2done = {}
            for tc in range(NTC):
                w = list(CW) + prev.get("cons", [])
                dma("sync", esb[:].rearrange("p (k t) -> p k t", k=H16),
                    e_h.rearrange("(k p) t -> p k t", p=128)[
                        :, :, tc * TC:(tc + 1) * TC], waits=w)
                dma("sync", hsb[:].rearrange("p (k t) -> p k t", k=H16),
                    hh_h.rearrange("(k p) t -> p k t", p=128)[
                        :, :, tc * TC:(tc + 1) * TC])
                dma("gpsimd", xsb[:].rearrange("p (k t) -> p k t", k=KD),
                    xchunks(x_src, tc), waits=w)
                ldv = P.snap("ld")
                gvx = P.snap("g")
                reps = []
                for ex in range(NEXP):
                    rpar = ex % 2
                    dma("sync", wserow[0:1, rpar * TC:(rpar + 1) * TC],
                        wse2_h[0:1, ex * T + tc * TC:ex * T + (tc + 1) * TC],
                        waits=[("pe", P.val("pe"))])
                    ldw = P.snap("ld")
                    rw = ([("acts", reps[-1])] if reps
                          else prev.get("evac", []) + CONST_READY)
                    te = mm(prt[:, 0:TC], ones_row_b[:],
                            wserow[0:1, rpar * TC:(rpar + 1) * TC],
                            True, True, waits=rw + ldw, inc=True)
                    reps.append(vact(repsb[:, ex * TC:(ex + 1) * TC],
                                     prt[:, 0:TC], AF.Copy, waits=[("pe", te)]))
                REPS = [("acts", reps[-1])]
                for pch in range(KD):
                    wmm = ldv + REPS + prev.get("evac", [])
                    for k in range(H16):
                        mm(prt[:, pch * TC:(pch + 1) * TC],
                           wosb[:, k * D + 128 * pch:k * D + 128 * (pch + 1)],
                           esb[:, k * TC:(k + 1) * TC],
                           k == 0, False, waits=wmm, inc=(k == H16 - 1))
                prevex = {}
                for ex in range(NEXP):
                    spar = ex % 2
                    sco = spar * H16 * TC
                    scv = []
                    wprev = prevex.get(spar, [])
                    for k in range(H16):
                        eng = "vector" if k % 2 == 0 else "gpsimd"
                        fn = (lambda e, o=scs[:, sco + k * TC:sco + (k + 1) * TC],
                              a=hsb[:, k * TC:(k + 1) * TC],
                              b=repsb[:, ex * TC:(ex + 1) * TC]:
                              e.tensor_tensor(o, a, b, ALU.mult))
                        scv.append((ECTR[eng],
                                    ew(eng, fn, waits=REPS + ldv + wprev)))
                    for gq in range(4):
                        gpar = (ex * 4 + gq) % 2
                        wo2 = gpar * 4 * D
                        wld = w2done.get(gpar, [])
                        dma("sync",
                            w2sb[:, wo2:wo2 + 4 * D].rearrange(
                                "p (k c) -> p k c", k=4),
                            w2bf_h[ex].rearrange("(q p) c -> p q c", p=128)[
                                :, gq * 4:(gq + 1) * 4, :],
                            waits=wld + CONV["done"])
                        wgrp = P.snap("ld")
                        for kk in range(4):
                            k = gq * 4 + kk
                            laststop = (ex == NEXP - 1 and k == H16 - 1)
                            for pch in range(KD):
                                mm(prt[:, pch * TC:(pch + 1) * TC],
                                   w2sb[:, wo2 + kk * D + 128 * pch:
                                        wo2 + kk * D + 128 * (pch + 1)],
                                   scs[:, sco + k * TC:sco + (k + 1) * TC],
                                   False, laststop,
                                   waits=wgrp + [scv[k]],
                                   inc=(pch == KD - 1 and kk == 3))
                        w2done[gpar] = [("pe", P.val("pe"))]
                    prevex[spar] = [("pe", P.val("pe"))]
                pev = P.snap("pe")
                evs = []
                for pch in range(KD):
                    evs.append(vtt(osb[:, pch * TC:(pch + 1) * TC],
                                   prt[:, pch * TC:(pch + 1) * TC],
                                   xsb[:, pch * TC:(pch + 1) * TC], ALU.add,
                                   waits=pev + gvx))
                gv2 = dma("gpsimd", xchunks(x_dst, tc),
                          osb[:].rearrange("p (k t) -> p k t", k=KD),
                          waits=[("dve", evs[-1])])
                prev["cons"] = [("dve", evs[-1]), ("pe", P.val("pe")),
                                ("g", gv2), ("pool", P.val("pool"))]
                prev["evac"] = [("dve", evs[-1])]
            P.wait("sync", P.snap("g", "pe", "dve", "acts", "pool"))

    def emit_attn_b():
        ctx = ExitStack()
        tag = str(len(P.q["sync"]))
        sb = lambda n, s, d: ctx.enter_context(nc.sbuf_tensor(n + "_" + tag, list(s), d))
        wsb = sb("ab_w", (128, KD * D), BF16)
        xnsb = sb("ab_xn", (128, KD * TC), BF16)
        ktsb = sb("ab_kt", (128, KD * T), BF16)
        vpsb = sb("ab_vp", (128, 32 * 1040), BF16)
        qrsb = sb("ab_qct", (128, 2 * KD * TC), BF16)
        cssb = sb("ab_cs", (128, 2 * TC), F32)
        scr = sb("ab_scr", (128, D), F32)
        rfsb = scr[:, 0:TC]
        rwsb = scr[:, TC:2 * TC]
        t1sb = sb("ab_t1", (128, D), F32)
        t2sb = sb("ab_t2", (128, D), F32)
        wscr = t1sb
        ptsb = sb("ab_pt", (128, 2 * 1024), BF16)
        repsb = sb("ab_rep", (128, TC), F32)
        aostg = sb("ab_stg", (128, 2 * TC), BF16)
        s0 = ctx.enter_context(nc.psum_tensor("ab_s0_" + tag, [128, 1024], F32))
        s1 = ctx.enter_context(nc.psum_tensor("ab_s1_" + tag, [128, 1024], F32))
        oAB = ctx.enter_context(nc.psum_tensor("ab_o_" + tag, [65, 1024], F32))
        pj = ctx.enter_context(nc.psum_tensor("ab_pj_" + tag, [128, TC], F32))
        rp = ctx.enter_context(nc.psum_tensor("ab_rp_" + tag, [128, TC], F32))

        def rot_featmajor(dstbf, ldcs, src):
            a = acopy(rfsb[:], src, waits=[("pe", P.val("pe")),
                                           ("pool", P.val("pool")),
                                           ("dve", P.val("dve"))])
            gsw = 0
            for qd in range(4):
                sq = qd ^ 1
                gsw = dma("gpsimd", rwsb[32 * qd:32 * (qd + 1), :],
                          rfsb[32 * sq:32 * (sq + 1), :], waits=[("acts", a)])
            p1 = ptt(t1sb[:, 0:TC], rfsb[:], cssb[:, 0:TC], ALU.mult,
                     waits=[("acts", a)] + ldcs)
            d1 = vtt(t2sb[:, 0:TC], rwsb[:], cssb[:, TC:2 * TC], ALU.mult,
                     waits=[("g", gsw)] + ldcs)
            d2 = vtt(dstbf, t1sb[:, 0:TC], t2sb[:, 0:TC], ALU.add,
                     waits=[("pool", p1), ("dve", d1)])
            return d2

        with ctx:
            conv_w(Wkvb, [wsb[:, i * D:(i + 1) * D] for i in range(KD)], wscr)
            ew("vector", lambda e: e.memset(vpsb[:], 1.0))
            CW = P.snap("dve")
            banksK = [s0[:, 0:TC], s0[:, TC:2 * TC], s1[:, 0:TC],
                      s1[:, TC:2 * TC], pj[:], rp[:]]
            banksT = [b.bitcast(BF16)[:, 0:128] for b in banksK]
            NBK = 6
            bank_free = {}
            ib = 0
            prev = {}
            for tc in range(NTC):
                w = list(CW) + prev.get("c", [])
                dma("gpsimd", xnsb[:].rearrange("p (k t) -> p k t", k=KD),
                    xchunks(xn_h, tc), waits=w)
                gxn = P.snap("g")
                dma("sync", cssb[:, 0:TC], cosF[:, tc * TC:(tc + 1) * TC], waits=w)
                dma("sync", cssb[:, TC:2 * TC], sinF[:, tc * TC:(tc + 1) * TC])
                ldcs = P.snap("ld")
                xnk = lambda k: xnsb[:, k * TC:(k + 1) * TC]
                rotd = {}
                for f in range(KD):
                    b = ib % NBK
                    ib += 1
                    wmm = gxn + bank_free.get(b, [])
                    for k in range(KD):
                        mm(banksK[b], wsb[:, k * D + 128 * f:k * D + 128 * (f + 1)],
                           xnk(k), k == 0, k == KD - 1, waits=wmm,
                           inc=(k == KD - 1))
                    rotd[f] = rot_featmajor(
                        ktsb[:, f * T + tc * TC:f * T + (tc + 1) * TC],
                        ldcs, banksK[b])
                    bank_free[b] = [("acts", P.val("acts"))]
                for tl in range(4):
                    kt = tc * 4 + tl
                    for f in range(KD):
                        b = ib % NBK
                        ib += 1
                        obf = banksT[b]
                        wtr = [("dve", rotd[f])] + bank_free.get(b, [])
                        P.op("tensor",
                             lambda e, o=obf,
                             a=ktsb[:, f * T + kt * 128:f * T + (kt + 1) * 128],
                             i=ident_s[:]: e.transpose(o, a, i),
                             waits=wtr + CONST_READY, inc=("pe", 1))
                        te = P.val("pe")
                        base = kt * 1040 + 2 * f * 65
                        dst = vpsb[:, base:base + 130].rearrange(
                            "p (c d) -> p c d", c=2)[:, :, 0:64]
                        dv = vcopy(dst, obf.rearrange("p (c d) -> p c d", c=2),
                                   waits=[("pe", te)])
                        bank_free[b] = [("dve", dv)]
                prev["c"] = [("dve", P.val("dve")), ("pe", P.val("pe")),
                             ("pool", P.val("pool")), ("acts", P.val("acts")),
                             ("g", P.val("g"))]
            P.wait("sync", P.snap("pe", "dve", "acts", "pool"))
            conv_w(Wqb, [wsb[:, i * D:(i + 1) * D] for i in range(KD)], wscr)
            KVD = P.snap("dve", "pool", "pe", "acts", "g", "ld")
            sp2 = [s0, s1]
            st = {"chunk_end": {}, "proj_pe": {}}
            chunk_w = {}
            est = {"cp": 0, "rppe": {}}
            rst = {"cpv": 0, "gsw": 0, "p1": 0, "d2": 0}

            def rot_q(dstbf, ldcs, base, guard_pe):
                cpv = vcopy(rfsb[:], pj[:],
                            waits=P.snap("pe") + base +
                            [("pool", rst["p1"]), ("g", rst["gsw"])])
                gsw = 0
                for qd in range(4):
                    sq = qd ^ 1
                    gsw = dma("gpsimd", rwsb[32 * qd:32 * (qd + 1), :],
                              rfsb[32 * sq:32 * (sq + 1), :],
                              waits=[("dve", cpv)])
                p1 = ptt(t1sb[:, 0:TC], rfsb[:], cssb[:, 0:TC], ALU.mult,
                         waits=[("dve", cpv), ("dve", rst["d2"])] + ldcs + base)
                d1 = vtt(t2sb[:, 0:TC], rwsb[:], cssb[:, TC:2 * TC], ALU.mult,
                         waits=[("g", gsw)] + ldcs)
                d2 = vtt(dstbf, t1sb[:, 0:TC], t2sb[:, 0:TC], ALU.add,
                         waits=[("pool", p1), ("pe", guard_pe)] + base)
                rst.update(cpv=cpv, gsw=gsw, p1=p1, d2=d2)
                return d2

            def prologue_units(qc, base):
                qo = (qc % 2) * KD * TC
                state = {"gld": [], "cs": []}
                units = []

                def u_load():
                    dma("gpsimd", xnsb[:].rearrange("p (k t) -> p k t", k=KD),
                        xchunks(xn_h, qc),
                        waits=[("pe", st["proj_pe"].get(qc - 1, 0))] + CW + base)
                    state["gld"] = P.snap("g")
                    dma("sync", cssb[:, 0:TC], cosF[:, qc * TC:(qc + 1) * TC],
                        waits=[("pool", rst["p1"]), ("dve", rst["d2"])] + base)
                    dma("sync", cssb[:, TC:2 * TC],
                        sinF[:, qc * TC:(qc + 1) * TC])
                    state["cs"] = P.snap("ld")
                units.append(u_load)

                def mk_rot(f):
                    def u():
                        wmm = state["gld"] + base + [("dve", rst["cpv"])]
                        for k in range(KD):
                            mm(pj[:], wsb[:, k * D + 128 * f:k * D + 128 * (f + 1)],
                               xnsb[:, k * TC:(k + 1) * TC], k == 0, k == KD - 1,
                               waits=wmm, inc=(k == KD - 1))
                        d2 = rot_q(qrsb[:, qo + f * TC:qo + (f + 1) * TC],
                                   state["cs"], base,
                                   st["chunk_end"].get(qc - 2, 0))
                        if f == KD - 1:
                            chunk_w[qc] = [("dve", d2)] + base
                            st["proj_pe"][qc] = P.val("pe")
                    return u
                for f in range(KD):
                    units.append(mk_rot(f))
                return units

            def qkops(jg, w):
                qc, r = divmod(jg, 256)
                f, kt = divmod(r, 32)
                qo = (qc % 2) * KD * TC
                sbk = sp2[jg % 2]
                mm(sbk[:, 0:TC],
                   ktsb[0:64, f * T + kt * 128:f * T + (kt + 1) * 128],
                   qrsb[0:64, qo + f * TC:qo + (f + 1) * TC],
                   True, True, waits=w, tp=(0, 0))
                return mm(sbk[:, TC:1024],
                          ktsb[64:128, f * T + kt * 128:f * T + (kt + 1) * 128],
                          qrsb[64:128, qo + f * TC:qo + (f + 1) * TC],
                          True, True, tp=(64, 0), inc=True)

            def pvops(jg, w, start, stop):
                qc, r = divmod(jg, 256)
                f, kt = divmod(r, 32)
                ip = jg % 2
                h0 = 2 * f
                vA = vpsb[:, kt * 1040 + h0 * 65:kt * 1040 + h0 * 65 + 65]
                vB = vpsb[:, kt * 1040 + (h0 + 1) * 65:
                          kt * 1040 + (h0 + 1) * 65 + 65]
                mm(oAB[:, 0:TC], vA, ptsb[:, ip * 1024:ip * 1024 + TC],
                   start, stop, waits=w)
                return mm(oAB[:, TC:1024], vB,
                          ptsb[:, ip * 1024 + TC:(ip + 1) * 1024],
                          start, stop, inc=True)

            def store_b(e, spar, n2):
                qc, f = divmod(e, 8)
                return dma("gpsimd",
                           ao_h.rearrange("(k p) t -> p k t", p=128)[
                               :, f, qc * TC:(qc + 1) * TC],
                           aostg[:, spar * TC:(spar + 1) * TC],
                           waits=[("dve", n2)])

            def end_b(e, pvlast):
                qc, f = divmod(e, 8)
                if f == 7:
                    st["chunk_end"][qc] = pvlast

            epiops = mk_epilogue(oAB, rp, repsb, aostg, store_b, end_b)

            for u in prologue_units(0, KVD):
                u()
            feeds = {qc: prologue_units(qc, []) for qc in range(1, 8)}
            run_scores(8, sp2, ptsb, qkops, pvops, epiops, feeds, chunk_w)
            P.wait("sync", P.snap("g", "pe", "dve", "acts", "pool"))

    seq = {
        "ln1": lambda: emit_ln(xT_in),
        "attna": emit_attn_a,
        "proja": lambda: emit_outproj(woaT, xT_in, x_h0, None),
        "ffn1": lambda: (emit_ffnA(0), emit_ffnB(0, x_h0, x_h1)),
        "ln3": lambda: emit_ln(x_h1),
        "attnb": emit_attn_b,
        "projb": lambda: emit_outproj(Woutb, x_h1, x_h0, boutb_s),
        "ffn2": lambda: (emit_ffnA(1), emit_ffnB(1, x_h0, out_T)),
    }
    for ph in phases:
        seq[ph]()

    P.wait("sync", P.snap("ld", "g", "pe", "dve", "acts", "pool"))

    with nc.Block() as blk:
        P.replay(blk)
    root.close()
    return nc


_CACHED = {}


def _get_program():
    if "nc" not in _CACHED:
        _CACHED["nc"] = build_program()
    return _CACHED["nc"]


def kernel(**inputs):
    cosF_t, sinF_t = _host_tables()
    f32 = np.float32
    asnp = lambda a: np.asarray(a)
    common = {
        "wqaT": np.ascontiguousarray(asnp(inputs["L0_attn_a_wq"]).T.astype(f32)),
        "wkaT": np.ascontiguousarray(asnp(inputs["L0_attn_a_wk"]).T.astype(f32)),
        "wvaT": np.ascontiguousarray(asnp(inputs["L0_attn_a_wv"]).T.astype(f32)),
        "woaT": np.ascontiguousarray(asnp(inputs["L0_attn_a_wo"]).T.astype(f32)),
        "Wqb": np.ascontiguousarray(asnp(inputs["L0_attn_b_Wq"]).astype(f32)),
        "Wkvb": np.ascontiguousarray(asnp(inputs["L0_attn_b_Wkv"]).astype(f32)),
        "Woutb": np.ascontiguousarray(asnp(inputs["L0_attn_b_Wout"]).astype(f32)),
        "boutb": np.ascontiguousarray(asnp(inputs["L0_attn_b_bout"]).astype(f32).reshape(1, D)),
        "winT": np.ascontiguousarray(asnp(inputs["L0_ffn_w_in"]).T.astype(f32)),
        "w1T": np.ascontiguousarray(asnp(inputs["L0_ffn_w1"]).T.astype(f32)),
        "woutT": np.ascontiguousarray(asnp(inputs["L0_ffn_w_out"]).T.astype(f32)),
        "w2T": np.ascontiguousarray(asnp(inputs["L0_ffn_w2"]).transpose(0, 2, 1).astype(f32)),
        "wgeT": np.ascontiguousarray(np.concatenate(
            [asnp(inputs["L0_ffn_wg"]).T, asnp(inputs["L0_ffn_we"]).T], axis=1).astype(f32)),
        "gbeb": np.ascontiguousarray(np.concatenate(
            [asnp(inputs["L0_ffn_gb"]), asnp(inputs["L0_ffn_eb"])]).astype(f32).reshape(1, 10)),
        "cosF": cosF_t, "sinF": sinF_t,
    }
    x = asnp(inputs["L0_attn_a_x"])
    in_maps = []
    for b in range(8):
        m = dict(common)
        m["xT"] = np.ascontiguousarray(x[b].T.astype(f32))
        in_maps.append(m)
    nc = _get_program()
    res = run_bass_kernel_spmd(nc, in_maps, core_ids=list(range(8)))
    out = np.stack([np.ascontiguousarray(r["out_T"].T) for r in res.results], axis=0)
    return out


# version 20
# speedup vs baseline: 1.0348x; 1.0348x over previous
import numpy as np
import ml_dtypes
import concourse.bass as bass
import concourse.mybir as mybir
from concourse.bass_utils import run_bass_kernel_spmd

F32 = mybir.dt.float32
BF16 = mybir.dt.bfloat16
AF = mybir.ActivationFunctionType
ALU = mybir.AluOpType
AX = mybir.AxisListType

D = 1024
T = 4096
TC = 512
NTC = 8
KD = 8
H16 = 16
HID = 2048
NEXP = 8
SCALE = 0.125
CSHIFT = -12.0
EPS = 1e-5
GMASK = 60.0

_BF = ml_dtypes.bfloat16

ALL_PHASES = ("ln1", "attna", "proja", "ffn1", "ln3",
              "attnb", "projb", "ffn2")


def _host_tables():
    inv = 1.0 / (10000.0 ** (np.arange(0, 64, 2, dtype=np.float64) / 64))
    freqs = np.arange(T, dtype=np.float64)[:, None] * inv[None, :]
    fr = np.concatenate([freqs, freqs], axis=1)
    cos64 = np.cos(fr)
    sgn = np.where(np.arange(64) < 32, -1.0, 1.0)[None, :]
    sinP = np.sin(fr) * sgn
    cosF = np.ascontiguousarray(np.tile(cos64.T, (2, 1)).astype(np.float32))
    sinF = np.ascontiguousarray(np.tile(sinP.T, (2, 1)).astype(np.float32))
    return cosF, sinF


class Prog:
    ENGS = ("sync", "gpsimd", "vector", "scalar", "tensor")

    def __init__(self, nc):
        self.nc = nc
        self.q = {e: [] for e in self.ENGS}
        self.sem = {}
        self.cnt = {}
        self.lastwait = {e: {} for e in self.ENGS}

    def new_sem(self, name):
        self.sem[name] = self.nc.alloc_semaphore(name)
        self.cnt[name] = 0

    def val(self, name):
        return self.cnt[name]

    def snap(self, *names):
        return [(n, self.cnt[n]) for n in names]

    def op(self, eng, fn, waits=(), inc=None):
        lw = self.lastwait[eng]
        w = []
        for s, v in waits:
            if v > 0 and lw.get(s, 0) < v:
                lw[s] = v
                w.append((self.sem[s], v))
        vinc = None
        if inc is not None:
            self.cnt[inc[0]] += inc[1]
            vinc = (self.sem[inc[0]], inc[1])

        def run(e, fn=fn, w=tuple(w), vinc=vinc):
            for s, v in w:
                e.wait_ge(s, v)
            ins = fn(e)
            if vinc is not None and ins is not None:
                ins.then_inc(vinc[0], vinc[1])
        self.q[eng].append(run)

    def wait(self, eng, waits):
        self.op(eng, lambda e: None, waits=waits)

    def replay(self, blk):
        def mk(name):
            def body(eng):
                for fn in self.q[name]:
                    fn(eng)
            return body
        blk.sync(mk("sync"))
        blk.gpsimd(mk("gpsimd"))
        blk.vector(mk("vector"))
        blk.scalar(mk("scalar"))
        blk.tensor(mk("tensor"))


ECTR = {"vector": "dve", "scalar": "acts", "gpsimd": "pool"}


def build_program(phases=ALL_PHASES):
    nc = bass.Bass("TRN2", target_bir_lowering=False, debug=False)
    P = Prog(nc)
    for s in ("ld", "g", "pe", "dve", "acts", "pool"):
        P.new_sem(s)

    def din(name, shape):
        return nc.dram_tensor(name, list(shape), F32, kind="ExternalInput").ap()

    xT_in = din("xT", (D, T))
    wqaT = din("wqaT", (D, D)); wkaT = din("wkaT", (D, D))
    wvaT = din("wvaT", (D, D)); woaT = din("woaT", (D, D))
    Wqb = din("Wqb", (D, D)); Wkvb = din("Wkvb", (D, D)); Woutb = din("Woutb", (D, D))
    boutb = din("boutb", (1, D))
    winT = din("winT", (D, 2 * HID)); w1T = din("w1T", (D, 2 * HID))
    woutT = din("woutT", (HID, D))
    w2T = din("w2T", (NEXP, HID, D))
    wgeT = din("wgeT", (D, 10))
    gbeb = din("gbeb", (1, 10))
    cosF = din("cosF", (128, T)); sinF = din("sinF", (128, T))

    out_T = nc.dram_tensor("out_T", [D, T], F32, kind="ExternalOutput").ap()

    def dscr(name, shape, dt):
        return nc.dram_tensor(name, list(shape), dt, kind="Internal").ap()

    x_h0 = dscr("x_h0", (D, T), F32)
    x_h1 = dscr("x_h1", (D, T), F32)
    xn_h = dscr("xn_h", (D, T), BF16)
    ao_h = dscr("ao_h", (D, T), BF16)
    e_h = dscr("e_h", (HID, T), BF16)
    hh_h = dscr("hh_h", (HID, T), BF16)
    w2bf_h = dscr("w2bf_h", (NEXP, HID, D), BF16)
    wffbf_h = dscr("wffbf_h", (D, 4 * HID), BF16)
    woutbf_h = dscr("woutbf_h", (HID, D), BF16)
    wqbf_h = dscr("wqbf_h", (D, D), BF16)
    wse2_h = dscr("wse2_h", (1, NEXP * T), BF16)

    ident_h = nc.inline_tensor(np.eye(128).astype(_BF), name="ident128").ap()

    from contextlib import ExitStack
    root = ExitStack()

    def sbuf(n, s, d):
        return root.enter_context(nc.sbuf_tensor(n, list(s), d))

    ones_col_f = sbuf("ones_col_f", (128, 1), F32)
    ones_col_b = sbuf("ones_col_b", (128, 1), BF16)
    ones_row_f = sbuf("ones_row_f", (1, 128), F32)
    ones_row_b = sbuf("ones_row_b", (1, 128), BF16)
    shiftc = sbuf("shiftc", (128, 1), F32)
    eps_t = sbuf("eps_t", (1, 1), F32)
    ident_s = sbuf("ident_s", (128, 128), BF16)
    stat_s = sbuf("stat_s", (1, 4 * TC), F32)
    gbias_s = sbuf("gbias_s", (128, 10), F32)
    boutb_s = sbuf("boutb_s", (128, KD), F32)
    grow_s = sbuf("grow_s", (1, 10), F32)
    g320_s = sbuf("g320_s", (128, 320), F32)

    for buf, v in ((ones_col_f, 1.0), (ones_col_b, 1.0), (ones_row_f, 1.0),
                   (ones_row_b, 1.0), (shiftc, CSHIFT), (eps_t, EPS)):
        P.op("vector", lambda e, b=buf, vv=v: e.memset(b[:], vv), inc=("dve", 1))
    P.op("sync", lambda e: e.dma_start(ident_s[:], ident_h[:]), inc=("ld", 16))
    P.op("sync", lambda e: e.dma_start(grow_s[:], gbeb[:]), inc=("ld", 16))
    for f in range(KD):
        P.op("sync", lambda e, f=f: e.dma_start(
            boutb_s[:, f:f + 1],
            boutb[0:1, 128 * f:128 * (f + 1)].rearrange("o p -> p o")),
            inc=("ld", 16))
    INIT = P.snap("ld", "dve")
    pgb_g = nc.psum_tensor("pgb", [128, 16], F32)
    with pgb_g as pgb:
        P.op("tensor", lambda e: e.matmul(pgb[:, 0:10], ones_row_f[:], grow_s[:],
                                          start=True, stop=True),
             waits=INIT, inc=("pe", 1))
        P.op("vector", lambda e: e.tensor_copy(gbias_s[:], pgb[:, 0:10]),
             waits=P.snap("pe"), inc=("dve", 1))
    for j in range(32):
        P.op("vector", lambda e, j=j: e.tensor_copy(
            g320_s[:, 10 * j:10 * (j + 1)], gbias_s[:, 0:10]),
            waits=P.snap("dve"), inc=("dve", 1))
    CONST_READY = P.snap("dve", "ld")

    def dma(eng, dst, src, waits=()):
        sem = "ld" if eng == "sync" else "g"
        P.op(eng, lambda e, d=dst, s=src: e.dma_start(d, s),
             waits=waits, inc=(sem, 16))
        return P.val(sem)

    def mm(out, lhsT, rhs, start, stop, waits=(), inc=False, tp=None):
        P.op("tensor",
             lambda e, o=out, l=lhsT, r=rhs, st=start, sp=stop, t=tp:
             e.matmul(o, l, r, start=st, stop=sp, tile_position=t),
             waits=waits, inc=("pe", 1) if inc else None)
        return P.val("pe")

    def ew(eng, fn, waits=()):
        c = ECTR[eng]
        P.op(eng, fn, waits=waits, inc=(c, 1))
        return P.val(c)

    def vcopy(out, in_, waits=()):
        return ew("vector", lambda e, o=out, a=in_: e.tensor_copy(o, a), waits)

    def acopy(out, in_, waits=()):
        return ew("scalar", lambda e, o=out, a=in_: e.activation(o, a, AF.Copy), waits)

    def vtt(out, a, b, op, waits=()):
        return ew("vector", lambda e, o=out, x=a, y=b, p=op: e.tensor_tensor(o, x, y, p), waits)

    def ptt(out, a, b, op, waits=()):
        return ew("gpsimd", lambda e, o=out, x=a, y=b, p=op: e.tensor_tensor(o, x, y, p), waits)

    def vact(out, in_, func, bias=0.0, scale=1.0, accum=None, waits=()):
        return ew("scalar",
                  lambda e, o=out, a=in_, f=func, b=bias, s=scale, ac=accum:
                  e.activation(o, a, f, bias=b, scale=s, accum_out=ac), waits)

    def xchunks(hbm, tc):
        return hbm.rearrange("(k p) t -> p k t", p=128)[:, :, tc * TC:(tc + 1) * TC]

    def conv_w(src, dst_tiles, scratch, cache=None):
        for i, dt_ in enumerate(dst_tiles):
            dma("sync", scratch[:], src[128 * i:128 * (i + 1), :],
                waits=P.snap("dve"))
            dv = vcopy(dt_, scratch[:], waits=P.snap("ld"))
            if cache is not None:
                dma("gpsimd", cache[128 * i:128 * (i + 1), :], dt_,
                    waits=[("dve", dv)])

    def emit_ln(x_src):
        ctx = ExitStack()
        tag = str(len(P.q["sync"]))
        sb = lambda n, s, d: ctx.enter_context(nc.sbuf_tensor(n + "_" + tag, list(s), d))
        lx = sb("ln_x", (128, 2 * KD * TC), F32)
        lsq = sb("ln_sq", (128, 2 * KD * TC), BF16)
        lt = sb("ln_t", (128, 2 * KD * TC), F32)
        lxn = sb("ln_xn", (128, 2 * KD * TC), BF16)
        mrep_s = sb("ln_mrep", (128, TC), F32)
        ln_st = sb("ln_st", (1, 10 * TC), F32)
        pssum2 = ctx.enter_context(nc.psum_tensor("ln_ps_" + tag, [1, 2 * TC], F32))
        psq2 = ctx.enter_context(nc.psum_tensor("ln_pq_" + tag, [1, 2 * TC], F32))
        pm = ctx.enter_context(nc.psum_tensor("ln_pm_" + tag, [128, TC], F32))
        pr = ctx.enter_context(nc.psum_tensor("ln_pr_" + tag, [128, TC], F32))
        with ctx:
            prev = {}
            for tc in range(NTC):
                par = tc % 2
                off = par * KD * TC
                so_ = par * 5 * TC
                SM = ln_st[0:1, so_:so_ + TC]
                MEAN = ln_st[0:1, so_ + TC:so_ + 2 * TC]
                RINV = ln_st[0:1, so_ + 2 * TC:so_ + 3 * TC]
                VAR = ln_st[0:1, so_ + 3 * TC:so_ + 4 * TC]
                SD = ln_st[0:1, so_ + 4 * TC:so_ + 5 * TC]
                pssum = pssum2[0:1, par * TC:(par + 1) * TC]
                psq = psq2[0:1, par * TC:(par + 1) * TC]
                xv = lambda k, off=off: lx[:, off + k * TC:off + (k + 1) * TC]
                w = list(CONST_READY) + prev.get(par, [])
                dma("sync", lx[:, off:off + KD * TC].rearrange(
                    "p (k t) -> p k t", k=KD), xchunks(x_src, tc), waits=w)
                ldv = P.snap("ld")
                sqv = []
                for k in range(KD):
                    sqv.append(vact(lsq[:, off + k * TC:off + (k + 1) * TC],
                                    xv(k), AF.Square,
                                    waits=ldv + prev.get(("sq", par), [])))
                wstat = ldv + prev.get(("stats", par), [])
                for k in range(KD):
                    mm(pssum, ones_col_f[:], xv(k), k == 0, k == KD - 1,
                       waits=wstat, inc=(k == KD - 1))
                for k in range(KD):
                    mm(psq, ones_col_b[:], lsq[:, off + k * TC:off + (k + 1) * TC],
                       k == 0, k == KD - 1,
                       waits=wstat + [("acts", sqv[k])], inc=(k == KD - 1))
                pev = P.snap("pe")
                d0 = ew("vector", lambda e, M=MEAN, ps=pssum:
                        e.tensor_scalar_mul(M, ps, 1.0 / D), waits=pev)
                d0 = ew("vector", lambda e, V=VAR, ps=psq:
                        e.tensor_scalar_mul(V, ps, 1.0 / D), waits=[("dve", d0)])
                d0 = vtt(SM, MEAN, MEAN, ALU.mult, waits=[("dve", d0)])
                d1 = vtt(VAR, VAR, SM, ALU.subtract, waits=[("dve", d0)])
                prev[("stats", par)] = [("dve", d1)]
                a1 = vact(SD, VAR, AF.Ln, bias=eps_t[0:1, 0:1],
                          waits=[("dve", d1)] + CONST_READY)
                a1 = vact(RINV, SD, AF.Exp, scale=-0.5, waits=[("acts", a1)])
                mm(pm[:], ones_row_f[:], MEAN, True, True,
                   waits=[("dve", d1)] + prev.get("reps", []))
                pev2 = mm(pr[:], ones_row_f[:], RINV, True, True,
                          waits=[("acts", a1)], inc=True)
                a2 = acopy(mrep_s[:], pm[:], waits=[("pe", pev2)])
                pl = []
                for k in range(KD):
                    eng = "gpsimd" if k % 2 == 0 else "vector"
                    v = ew(eng, lambda e, o=lt[:, off + k * TC:off + (k + 1) * TC],
                           a=xv(k), b=mrep_s[:]:
                           e.tensor_tensor(o, a, b, ALU.subtract),
                           waits=[("acts", a2)] + prev.get(("lt", par), []))
                    pl.append((ECTR[eng], v))
                dv = []
                for k in range(KD):
                    dv.append(vtt(lxn[:, off + k * TC:off + (k + 1) * TC],
                                  lt[:, off + k * TC:off + (k + 1) * TC],
                                  pr[:], ALU.mult, waits=[pl[k]]))
                gv = dma("gpsimd", xchunks(xn_h, tc),
                         lxn[:, off:off + KD * TC].rearrange("p (k t) -> p k t", k=KD),
                         waits=[("dve", dv[-1])])
                prev[par] = [pl[-2], pl[-1], ("acts", sqv[-1]), ("g", gv),
                             ("pe", P.val("pe")), ("dve", dv[-1])]
                prev[("sq", par)] = [("pe", P.val("pe"))]
                prev[("lt", par)] = [("dve", dv[-1])]
                prev["reps"] = [("dve", dv[-1]), ("acts", a2)]
            P.wait("sync", P.snap("g", "pool", "dve", "acts", "pe"))

    def emit_outproj(Wsrc, x_src, x_dst, bias):
        ctx = ExitStack()
        tag = str(len(P.q["sync"]))
        sb = lambda n, s, d: ctx.enter_context(nc.sbuf_tensor(n + "_" + tag, list(s), d))
        wsb = sb("op_w", (128, KD * D), BF16)
        wscr = sb("op_ws", (128, D), F32)
        aosb = sb("op_ao", (128, 2 * KD * TC), BF16)
        xsb = sb("op_x", (128, 2 * KD * TC), F32)
        osb = sb("op_o", (128, 2 * KD * TC), F32)
        lsq = sb("op_sq", (128, 2 * KD * TC), BF16)
        lt = sb("op_lt", (128, 2 * KD * TC), F32)
        lxn = sb("op_xn", (128, 2 * KD * TC), BF16)
        mrep_s = sb("op_mr", (128, TC), F32)
        op_st = sb("op_st", (1, 10 * TC), F32)
        p0 = ctx.enter_context(nc.psum_tensor("op_p0_" + tag, [128, TC], F32))
        p1 = ctx.enter_context(nc.psum_tensor("op_p1_" + tag, [128, TC], F32))
        pssum2 = ctx.enter_context(nc.psum_tensor("op_ps_" + tag, [1, 2 * TC], F32))
        psq2 = ctx.enter_context(nc.psum_tensor("op_pq_" + tag, [1, 2 * TC], F32))
        pm = ctx.enter_context(nc.psum_tensor("op_pm_" + tag, [128, TC], F32))
        pr = ctx.enter_context(nc.psum_tensor("op_pr_" + tag, [128, TC], F32))
        with ctx:
            conv_w(Wsrc, [wsb[:, i * D:(i + 1) * D] for i in range(KD)], wscr)
            CW = P.snap("dve")
            pp = [p0, p1]
            prev = {}
            for tc in range(NTC):
                par = tc % 2
                off = par * KD * TC
                w = list(CW) + prev.get(par, [])
                dma("sync", aosb[:, off:off + KD * TC].rearrange(
                    "p (k t) -> p k t", k=KD), xchunks(ao_h, tc), waits=w)
                ldv = P.snap("ld")
                dma("gpsimd", xsb[:, off:off + KD * TC].rearrange(
                    "p (k t) -> p k t", k=KD), xchunks(x_src, tc), waits=w)
                gv = P.snap("g")
                dvs = []
                for f in range(KD):
                    pb = pp[f % 2]
                    if len(dvs) >= 2:
                        wmm = ldv + [("dve", dvs[-2])]
                    else:
                        wmm = ldv + prev.get(("ps", par, f % 2), [])
                    for k in range(KD):
                        mm(pb[:], wsb[:, k * D + 128 * f:k * D + 128 * (f + 1)],
                           aosb[:, off + k * TC:off + (k + 1) * TC],
                           k == 0, k == KD - 1, waits=wmm, inc=(k == KD - 1))
                    pev = P.snap("pe")
                    o = osb[:, off + f * TC:off + (f + 1) * TC]
                    xin = xsb[:, off + f * TC:off + (f + 1) * TC]
                    if bias is None:
                        dvs.append(vtt(o, pb[:], xin, ALU.add, waits=pev + gv))
                    else:
                        dvs.append(ew("vector",
                                      lambda e, o=o, a=pb[:], s=bias[:, f:f + 1],
                                      b=xin: e.scalar_tensor_tensor(
                                          o, a, s, b, ALU.add, ALU.add),
                                      waits=pev + gv + CONST_READY))
                gv2 = dma("gpsimd", xchunks(x_dst, tc),
                          osb[:, off:off + KD * TC].rearrange("p (k t) -> p k t", k=KD),
                          waits=[("dve", dvs[-1])])
                OK = [("dve", dvs[-1])]
                ov = lambda k: osb[:, off + k * TC:off + (k + 1) * TC]
                so_ = par * 5 * TC
                SM = op_st[0:1, so_:so_ + TC]
                MEAN = op_st[0:1, so_ + TC:so_ + 2 * TC]
                RINV = op_st[0:1, so_ + 2 * TC:so_ + 3 * TC]
                VAR = op_st[0:1, so_ + 3 * TC:so_ + 4 * TC]
                SD = op_st[0:1, so_ + 4 * TC:so_ + 5 * TC]
                pssum = pssum2[0:1, par * TC:(par + 1) * TC]
                psq = psq2[0:1, par * TC:(par + 1) * TC]
                sqv = []
                for k in range(KD):
                    sqv.append(vact(lsq[:, off + k * TC:off + (k + 1) * TC],
                                    ov(k), AF.Square,
                                    waits=OK + prev.get(("sq", par), [])))
                wstat = OK + prev.get(("stats", par), [])
                for k in range(KD):
                    mm(pssum, ones_col_f[:], ov(k), k == 0, k == KD - 1,
                       waits=wstat, inc=(k == KD - 1))
                for k in range(KD):
                    mm(psq, ones_col_b[:],
                       lsq[:, off + k * TC:off + (k + 1) * TC],
                       k == 0, k == KD - 1,
                       waits=wstat + [("acts", sqv[k])], inc=(k == KD - 1))
                pev = P.snap("pe")
                d0 = ew("vector", lambda e, M=MEAN, ps=pssum:
                        e.tensor_scalar_mul(M, ps, 1.0 / D), waits=pev)
                d0 = ew("vector", lambda e, V=VAR, ps=psq:
                        e.tensor_scalar_mul(V, ps, 1.0 / D), waits=[("dve", d0)])
                d0 = vtt(SM, MEAN, MEAN, ALU.mult, waits=[("dve", d0)])
                d1 = vtt(VAR, VAR, SM, ALU.subtract, waits=[("dve", d0)])
                prev[("stats", par)] = [("dve", d1)]
                a1 = vact(SD, VAR, AF.Ln, bias=eps_t[0:1, 0:1],
                          waits=[("dve", d1), ("acts", sqv[-1])] + CONST_READY)
                a1 = vact(RINV, SD, AF.Exp, scale=-0.5, waits=[("acts", a1)])
                mm(pm[:], ones_row_f[0:1, :], MEAN, True, True,
                   waits=[("dve", d1)] + prev.get("reps", []))
                pev2 = mm(pr[:], ones_row_f[0:1, :], RINV, True, True,
                          waits=[("acts", a1)], inc=True)
                a2 = acopy(mrep_s[:], pm[:], waits=[("pe", pev2)])
                pl = []
                for k in range(KD):
                    eng = "gpsimd" if k % 2 == 0 else "vector"
                    v = ew(eng, lambda e, o=lt[:, off + k * TC:off + (k + 1) * TC],
                           a=ov(k), b=mrep_s[:]:
                           e.tensor_tensor(o, a, b, ALU.subtract),
                           waits=[("acts", a2)] + prev.get(("lt", par), []))
                    pl.append((ECTR[eng], v))
                dv = []
                for k in range(KD):
                    dv.append(vtt(lxn[:, off + k * TC:off + (k + 1) * TC],
                                  lt[:, off + k * TC:off + (k + 1) * TC],
                                  pr[:], ALU.mult, waits=[pl[k]]))
                gv3 = dma("gpsimd", xchunks(xn_h, tc),
                          lxn[:, off:off + KD * TC].rearrange(
                              "p (k t) -> p k t", k=KD),
                          waits=[("dve", dv[-1])])
                prev[par] = [("dve", dv[-1]), ("g", gv3), ("pe", P.val("pe")),
                             pl[-2], pl[-1], ("acts", sqv[-1])]
                prev[("sq", par)] = [("pe", P.val("pe"))]
                prev[("lt", par)] = [("dve", dv[-1])]
                prev["reps"] = [("dve", dv[-1]), ("acts", a2)]
                prev[("ps", par, 0)] = [("dve", dvs[-2])]
                prev[("ps", par, 1)] = [("dve", dvs[-1])]
            P.wait("sync", P.snap("g", "dve", "pe", "acts", "pool"))

    CONV = {"done": []}

    def conv_jobs():
        jobs = []
        for i in range(KD):
            for j in range(8):
                src_m = winT if j < 4 else w1T
                jc = 1024 * (j % 4)
                jobs.append((src_m[128 * i:128 * (i + 1), jc:jc + 1024],
                             wffbf_h[128 * i:128 * (i + 1),
                                     1024 * j:1024 * (j + 1)]))
        for ex in range(NEXP):
            for k in range(H16):
                jobs.append((w2T[ex, 128 * k:128 * (k + 1), :],
                             w2bf_h[ex, 128 * k:128 * (k + 1), :]))
        for i in range(H16):
            jobs.append((woutT[128 * i:128 * (i + 1), :],
                         woutbf_h[128 * i:128 * (i + 1), :]))
        for i in range(KD):
            jobs.append((Wqb[128 * i:128 * (i + 1), :],
                         wqbf_h[128 * i:128 * (i + 1), :]))
        return jobs

    def run_scores(nchunks, sp2, ptsb, qkops, pvops, epiops, feeds, chunk_w):
        expv = {}
        pvv = {}
        smv = {}
        normd = []
        prevstg = []
        total = nchunks * 256
        for jg in range(total + 2):
            c, r = divmod(jg, 256)
            fl = feeds.get(c + 1)
            if fl is not None and r >= 32 and (r - 32) % 4 == 0:
                ui = (r - 32) // 4
                if ui < len(fl):
                    fl[ui]()
            if jg < total:
                w = [("acts", expv[jg - 2])] if jg >= 2 else []
                if r < 2:
                    w = w + chunk_w.get(c, [])
                smv[jg] = qkops(jg, w)
            j2 = jg - 2
            if j2 >= 0:
                if j2 % 32 == 0 and j2 > 0:
                    normd, prevstg = epiops(j2 // 32 - 1, pvv[j2 - 1],
                                            normd, prevstg)
                w = [("acts", expv[j2])]
                if j2 % 32 == 0:
                    w = w + normd
                    if j2 % 256 == 0:
                        w = w + chunk_w.get(j2 // 256, [])
                pvv[j2] = pvops(j2, w, j2 % 32 == 0, j2 % 32 == 31)
            if jg < total:
                g = pvv[jg - 2] if jg >= 2 else smv[jg]
                ip = jg % 2
                expv[jg] = vact(ptsb[:, ip * 1024:(ip + 1) * 1024], sp2[ip][:],
                                AF.Exp, bias=shiftc[:, 0:1], scale=SCALE,
                                waits=[("pe", g)] + CONST_READY)
        epiops(total // 32 - 1, pvv[total - 1], normd, prevstg)

    def mk_epilogue(oAB, rp, repsb, aostg, store_fn, end_cb):
        est = {"cp": 0, "rppe": {}}

        def epiops(e, pvlast, normd, prevstg):
            spar = e % 2
            lnrow = repsb[0:1, 0:TC]
            rAB = stat_s[0:1, spar * 1024:(spar + 1) * 1024]
            w = [("pe", pvlast)] + normd
            if e >= 2:
                w.append(("pe", est["rppe"][e - 2]))
            rv = []
            for hh in range(2):
                aL = vact(lnrow, oAB[64:65, hh * TC:(hh + 1) * TC], AF.Ln,
                          waits=(w if hh == 0 else [("acts", rv[-1])]))
                rv.append(vact(rAB[0:1, hh * TC:(hh + 1) * TC], lnrow,
                               AF.Exp, scale=-1.0, waits=[("acts", aL)]))
            wrp0 = [("acts", rv[0])] + ([("dve", est["cp"])] if est["cp"] else [])
            mm(rp[0:64, :], ones_row_f[0:1, 0:64], rAB[0:1, 0:TC],
               True, True, waits=wrp0)
            rpe = mm(rp[64:128, :], ones_row_f[0:1, 0:64],
                     rAB[0:1, TC:1024], True, True, inc=True,
                     waits=[("acts", rv[1])])
            est["rppe"][e] = rpe
            cp = vcopy(repsb[:], rp[:], waits=[("pe", rpe)])
            est["cp"] = cp
            n1 = vtt(aostg[0:64, e % 2 * TC:(e % 2 + 1) * TC],
                     oAB[0:64, 0:TC], repsb[0:64, :], ALU.mult,
                     waits=[("dve", cp)] + prevstg)
            n2 = vtt(aostg[64:128, e % 2 * TC:(e % 2 + 1) * TC],
                     oAB[0:64, TC:1024], repsb[64:128, :], ALU.mult,
                     waits=[("dve", n1)])
            gq = store_fn(e, spar, n2)
            end_cb(e, pvlast)
            return [("dve", n2)], [("g", gq)]
        return epiops

    def emit_attn_a():
        ctx = ExitStack()
        tag = str(len(P.q["sync"]))
        sb = lambda n, s, d: ctx.enter_context(nc.sbuf_tensor(n + "_" + tag, list(s), d))
        wsb = sb("aa_w", (128, 3 * KD * D), BF16)
        wscr = sb("aa_ws", (128, D), F32)
        xnsb = sb("aa_xn", (128, 2 * KD * TC), BF16)
        qksb = sb("aa_qk", (128, 2 * 2 * T), BF16)
        vpsb = sb("aa_vp", (128, 2 * 4 * 1040), BF16)
        ptsb = sb("aa_pt", (128, 2 * 1024), BF16)
        repsb = sb("aa_rep", (128, TC), F32)
        aostg = sb("aa_stg", (128, 2 * TC), BF16)
        cvt_f = sb("aa_cvf", (128, 2 * 1024), F32)
        cvt_b = sb("aa_cvb", (128, 2 * 1024), BF16)
        s0 = ctx.enter_context(nc.psum_tensor("aa_s0_" + tag, [128, 1024], F32))
        s1 = ctx.enter_context(nc.psum_tensor("aa_s1_" + tag, [128, 1024], F32))
        oAB = ctx.enter_context(nc.psum_tensor("aa_o_" + tag, [65, 1024], F32))
        pj = ctx.enter_context(nc.psum_tensor("aa_pj_" + tag, [128, TC], F32))
        rp = ctx.enter_context(nc.psum_tensor("aa_rp_" + tag, [128, TC], F32))
        with ctx:
            for wi, wsrc in enumerate((wqaT, wkaT, wvaT)):
                conv_w(wsrc, [wsb[:, (wi * KD + i) * D:(wi * KD + i + 1) * D]
                              for i in range(KD)], wscr)
            ew("vector", lambda e: e.memset(vpsb[:], 1.0))
            CW = P.snap("dve")
            sp2 = [s0, s1]
            st = {"pair_end": {}, "ev": None}
            chunk_w = {}
            est = {"cp": 0, "rppe": {}}

            def proj_units(p):
                par = p % 2
                xno = par * KD * TC
                qko = par * 2 * T
                vpo = par * 4 * 1040
                state = {"ld": []}
                units = []

                def u_load():
                    gpe = st["pair_end"].get(p - 2, 0)
                    dma("sync", xnsb[:, xno:xno + KD * TC].rearrange(
                        "p (k t) -> p k t", k=KD), xchunks(xn_h, p),
                        waits=[("pe", gpe)] + CW)
                    state["ld"] = P.snap("ld")
                units.append(u_load)

                def mk_qkf(wi, f):
                    def u():
                        gpe = st["pair_end"].get(p - 2, 0)
                        wmm = state["ld"] + CW + \
                            ([st["ev"]] if st["ev"] else [])
                        for k in range(KD):
                            mm(pj[:], wsb[:, (wi * KD + k) * D + 128 * f:
                                          (wi * KD + k) * D + 128 * (f + 1)],
                               xnsb[:, xno + k * TC:xno + (k + 1) * TC],
                               k == 0, k == KD - 1, waits=wmm,
                               inc=(k == KD - 1))
                        pev = P.snap("pe")
                        dsto = qko + wi * T
                        last = 0
                        for cpar in range(2):
                            for th in range(2):
                                src = pj[64 * cpar:64 * (cpar + 1),
                                         256 * th:256 * (th + 1)]
                                dst = qksb[64 * th:64 * (th + 1),
                                           dsto + (2 * f + cpar) * 256:
                                           dsto + (2 * f + cpar) * 256 + 256]
                                last = vcopy(dst, src,
                                             waits=pev + [("pe", gpe)])
                        st["ev"] = ("dve", last)
                    return u
                for wi in range(2):
                    for f in range(KD):
                        units.append(mk_qkf(wi, f))

                def mk_v(ttl, n):
                    def u():
                        gpe = st["pair_end"].get(p - 2, 0)
                        wmm = state["ld"] + CW + \
                            ([st["ev"]] if st["ev"] else [])
                        for k in range(KD):
                            mm(pj[:], xnsb[:, xno + k * TC + 128 * ttl:
                                           xno + k * TC + 128 * (ttl + 1)],
                               wsb[:, (2 * KD + k) * D + n * TC:
                                   (2 * KD + k) * D + (n + 1) * TC],
                               k == 0, k == KD - 1, waits=wmm,
                               inc=(k == KD - 1))
                        pev = P.snap("pe")
                        base = vpo + ttl * 1040 + n * 8 * 65
                        dst = vpsb[:, base:base + 8 * 65].rearrange(
                            "p (c d) -> p c d", c=8)[:, :, 0:64]
                        srcap = pj[:].rearrange("p (c d) -> p c d", c=8)
                        v = vcopy(dst, srcap, waits=pev + [("pe", gpe)])
                        st["ev"] = ("dve", v)
                        if (ttl, n) == (3, 1):
                            chunk_w[p] = [("dve", v)]
                    return u
                for ttl in range(4):
                    for n in range(2):
                        units.append(mk_v(ttl, n))
                return units

            def qkops(jg, w):
                p, r = divmod(jg, 256)
                qc, kt = divmod(r, 32)
                qko = (p % 2) * 2 * T
                sbk = sp2[jg % 2]
                mm(sbk[:, 0:TC],
                   qksb[0:64, qko + T + kt * 128:qko + T + (kt + 1) * 128],
                   qksb[0:64, qko + qc * TC:qko + (qc + 1) * TC],
                   True, True, waits=w, tp=(0, 0))
                return mm(sbk[:, TC:1024],
                          qksb[64:128, qko + T + kt * 128:qko + T + (kt + 1) * 128],
                          qksb[64:128, qko + qc * TC:qko + (qc + 1) * TC],
                          True, True, tp=(64, 0), inc=True)

            def pvops(jg, w, start, stop):
                p, r = divmod(jg, 256)
                qc, kt = divmod(r, 32)
                vpo = (p % 2) * 4 * 1040
                ip = jg % 2
                vA = vpsb[:, vpo + (kt % 2) * 1040 + (kt // 2) * 65:
                          vpo + (kt % 2) * 1040 + (kt // 2) * 65 + 65]
                vB = vpsb[:, vpo + (2 + kt % 2) * 1040 + (kt // 2) * 65:
                          vpo + (2 + kt % 2) * 1040 + (kt // 2) * 65 + 65]
                mm(oAB[:, 0:TC], vA, ptsb[:, ip * 1024:ip * 1024 + TC],
                   start, stop, waits=w)
                return mm(oAB[:, TC:1024], vB,
                          ptsb[:, ip * 1024 + TC:(ip + 1) * 1024],
                          start, stop, inc=True)

            def store_a(e, spar, n2):
                p, qc = divmod(e, 8)
                gq = 0
                for hh in range(2):
                    hgl = 2 * p + hh
                    src = aostg[64 * hh:64 * (hh + 1),
                                spar * TC:(spar + 1) * TC].rearrange(
                        "d (c s) -> d c s", c=2)
                    dst = ao_h.rearrange("(c dd) t -> dd c t", dd=64)[
                        :, 2 * qc:2 * qc + 2, 256 * hgl:256 * (hgl + 1)]
                    gq = dma("gpsimd", dst, src, waits=[("dve", n2)])
                return gq

            def end_a(e, pvlast):
                p, qc = divmod(e, 8)
                if qc == 7:
                    st["pair_end"][p] = pvlast

            epiops = mk_epilogue(oAB, rp, repsb, aostg, store_a, end_a)

            cst = {}

            def mk_cu(i, src, dst):
                par = i % 2

                def u():
                    f_ap = cvt_f[:, par * 1024:(par + 1) * 1024]
                    b_ap = cvt_b[:, par * 1024:(par + 1) * 1024]
                    dma("sync", f_ap, src,
                        waits=[("dve", cst.get(("c", par), 0))])
                    cv = vcopy(b_ap, f_ap,
                               waits=P.snap("ld") + [("g", cst.get(("s", par), 0))])
                    cst[("c", par)] = cv
                    gv = dma("gpsimd", dst, b_ap, waits=[("dve", cv)])
                    cst[("s", par)] = gv
                return u

            cunits = [mk_cu(i, s, d) for i, (s, d) in enumerate(conv_jobs())]

            for u in proj_units(0):
                u()
            feeds = {}
            for p in range(1, 8):
                pu = proj_units(p)
                take, cunits = cunits[:31], cunits[31:]
                merged = []
                for a, b in zip(pu, take):
                    merged += [a, b]
                merged += pu[len(take):] + take[len(pu):]
                feeds[p] = merged
            run_scores(8, sp2, ptsb, qkops, pvops, epiops, feeds, chunk_w)
            assert not cunits
            P.wait("sync", P.snap("g", "pe", "dve", "acts"))
            CONV["done"] = P.snap("ld", "g", "dve")

    def emit_ffnA(call):
        ctx = ExitStack()
        tag = str(len(P.q["sync"]))
        sb = lambda n, s, d: ctx.enter_context(nc.sbuf_tensor(n + "_" + tag, list(s), d))
        wsb = sb("fa_w", (128, 2 * 4 * D), BF16)
        wscr = sb("fa_ws", (128, D), F32)
        xnsb = sb("fa_xn", (128, KD * T), BF16)
        sasb = sb("fa_sa", (128, 2 * 2 * TC), BF16)
        ehsb = sb("fa_eh", (128, 2 * 2 * TC), BF16)
        gwsb = sb("fa_gw", (128, 80), BF16)
        glb = sb("fa_gl", (128, 576), F32)
        eqb = sb("fa_eq", (128, 256), F32)
        wtb = sb("fa_wtb", (128, 256), BF16)
        wtstg = sb("fa_wts", (128, 256), BF16)
        pa = ctx.enter_context(nc.psum_tensor("fa_p_" + tag, [128, 2 * TC], F32))
        pa2 = ctx.enter_context(nc.psum_tensor("fa_p2_" + tag, [128, 2 * TC], F32))
        pb_ = ctx.enter_context(nc.psum_tensor("fa_pb_" + tag, [128, 2 * TC], F32))
        pb2 = ctx.enter_context(nc.psum_tensor("fa_pb2_" + tag, [128, 2 * TC], F32))

        def xnkt(k, tc):
            return xnsb[:, k * T + tc * TC:k * T + (tc + 1) * TC]

        def xnjt(k, j):
            return xnsb[:, k * T + 128 * j:k * T + 128 * (j + 1)]

        with ctx:
            for i in range(KD):
                dma("sync", wscr[:, 0:10], wgeT[128 * i:128 * (i + 1), :],
                    waits=P.snap("dve"))
                vcopy(gwsb[:, 10 * i:10 * (i + 1)], wscr[:, 0:10],
                      waits=P.snap("ld"))
            for tc in range(NTC):
                dma("gpsimd", xnsb[:].rearrange("p (k t) -> p k t", k=KD)[
                    :, :, tc * TC:(tc + 1) * TC], xchunks(xn_h, tc))
            gv = P.snap("g", "dve")
            NJ = 32
            GLb = glb[:, 0:10 * NJ]
            GL3 = GLb.rearrange("p (j c) -> p j c", j=NJ)
            Dlt = glb[:, 320:320 + NJ]
            SEL = glb[:, 352:352 + NJ]
            GP = glb[:, 384:384 + NJ]
            EMX = glb[:, 416:416 + NJ]
            ESM = glb[:, 448:448 + NJ]
            FCT = glb[:, 480:480 + NJ]
            V1 = glb[:, 512:512 + NJ]
            V2 = glb[:, 544:544 + NJ]
            EQ3 = eqb[:].rearrange("p (j c) -> p j c", j=NJ)
            WT3 = wtb[:].rearrange("p (j c) -> p j c", j=NJ)
            j3 = lambda ap: ap.rearrange("p (j o) -> p j o", j=NJ)
            for j in range(NJ):
                for k in range(KD):
                    mm(pb_[:, 10 * j:10 * (j + 1)], xnjt(k, j),
                       gwsb[:, 10 * k:10 * (k + 1)], k == 0, k == KD - 1,
                       waits=gv, inc=(j == NJ - 1 and k == KD - 1))
            d = vcopy(GLb, pb_[:, 0:10 * NJ], waits=P.snap("pe"))
            d = vtt(GLb, GLb, g320_s[:], ALU.add,
                    waits=[("dve", d)] + CONST_READY)
            d = vtt(j3(Dlt), GL3[:, :, 1:2], GL3[:, :, 0:1], ALU.subtract,
                    waits=[("dve", d)])
            d = ew("vector", lambda e: e.tensor_single_scalar(
                SEL, Dlt, 0.0, ALU.is_gt), waits=[("dve", d)])
            a = ew("scalar", lambda e: e.activation(GP, Dlt, AF.Abs),
                   waits=[("dve", d), ("acts", P.val("acts"))])
            a = vact(GP, GP, AF.Sigmoid, waits=[("acts", a)])
            d = ew("vector", lambda e: e.tensor_scalar_mul(SEL, SEL, GMASK),
                   waits=[("dve", d)])
            dm = d
            for j in range(NJ):
                dm = ew("vector", lambda e, j=j: e.tensor_scalar_sub(
                    GL3[:, j:j + 1, 2:6], GL3[:, j:j + 1, 2:6], SEL[:, j:j + 1]),
                    waits=[("dve", d)])
            d = ew("vector", lambda e: e.tensor_scalar(SEL, SEL, -1.0, GMASK,
                                                       ALU.mult, ALU.add),
                   waits=[("dve", dm)])
            dm = d
            for j in range(NJ):
                dm = ew("vector", lambda e, j=j: e.tensor_scalar_sub(
                    GL3[:, j:j + 1, 6:10], GL3[:, j:j + 1, 6:10], SEL[:, j:j + 1]),
                    waits=[("dve", d)])
            d = ew("vector", lambda e: e.reduce_max(
                j3(EMX), GL3[:, :, 2:10], axis=AX.X, negate=True),
                waits=[("dve", dm)])
            av = a
            for j in range(NJ):
                av = vact(GL3[:, j:j + 1, 2:10], GL3[:, j:j + 1, 2:10], AF.Exp,
                          bias=EMX[:, j:j + 1], accum=ESM[:, j:j + 1],
                          waits=[("dve", d), ("acts", a)])
            d = ew("vector", lambda e: e.reciprocal(FCT, ESM),
                   waits=[("acts", av)])
            d = vtt(FCT, FCT, GP, ALU.mult, waits=[("dve", d)])
            dm = d
            for j in range(NJ):
                dm = ew("vector", lambda e, j=j: e.tensor_scalar_mul(
                    GL3[:, j:j + 1, 2:10], GL3[:, j:j + 1, 2:10], FCT[:, j:j + 1]),
                    waits=[("dve", d)])
            d = ew("vector", lambda e: e.reduce_max(
                j3(V1), GL3[:, :, 2:10], axis=AX.X), waits=[("dve", dm)])
            dm = d
            for j in range(NJ):
                dm = ew("vector", lambda e, j=j: e.tensor_single_scalar(
                    EQ3[:, j:j + 1, :], GL3[:, j:j + 1, 2:10], V1[:, j:j + 1],
                    ALU.not_equal), waits=[("dve", d)])
            d = vtt(EQ3[:, :, :], GL3[:, :, 2:10], EQ3[:, :, :], ALU.mult,
                    waits=[("dve", dm)])
            d = ew("vector", lambda e: e.reduce_max(
                j3(V2), EQ3[:, :, :], axis=AX.X), waits=[("dve", d)])
            dm = d
            for j in range(NJ):
                dm = ew("vector", lambda e, j=j: e.tensor_single_scalar(
                    EQ3[:, j:j + 1, :], GL3[:, j:j + 1, 2:10], V2[:, j:j + 1], ALU.is_ge),
                    waits=[("dve", d)])
            for h in range(2):
                d = vtt(wtb[:, 128 * h:128 * (h + 1)].rearrange(
                            "p (e j) -> p j e", e=NEXP),
                        GL3[:, 16 * h:16 * (h + 1), 2:10],
                        EQ3[:, 16 * h:16 * (h + 1), :], ALU.mult,
                        waits=[("dve", dm)])
                P.op("tensor",
                     lambda e, o=pb2[:].bitcast(BF16)[0:128, 128 * h:128 * (h + 1)],
                     a=wtb[:, 128 * h:128 * (h + 1)], i=ident_s[:]:
                     e.transpose(o, a, i),
                     waits=[("dve", d)] + CONST_READY, inc=("pe", 1))
                te = P.val("pe")
                dv = vcopy(wtstg[:, 128 * h:128 * (h + 1)],
                           pb2[:].bitcast(BF16)[0:128, 128 * h:128 * (h + 1)],
                           waits=[("pe", te)])
                for ex in range(NEXP):
                    dma("gpsimd",
                        wse2_h[0:1, :].rearrange("o (e j c) -> o e j c",
                                                 e=NEXP, j=NJ)[
                            :, ex, 16 * h:16 * (h + 1), :],
                        wtstg[16 * ex:16 * (ex + 1),
                              128 * h:128 * (h + 1)],
                        waits=[("dve", dv)])
            GATE_DONE = [("dve", P.val("dve"))]

            prevw = {}
            for hc in range(H16):
                par = hc % 2
                wo = par * 4 * D

                def wcol(part, k, wo=wo):
                    return wsb[:, wo + part * D + 128 * k:wo + part * D + 128 * (k + 1)]

                wpar = prevw.get(par, [])
                for part in range(4):
                    colo = (part % 2) * HID + 128 * hc
                    cache_colo = (0 if part < 2 else 2 * HID) + colo
                    dma("sync",
                        wsb[:, wo + part * D:wo + (part + 1) * D].rearrange(
                            "p (k c) -> p k c", k=KD),
                        wffbf_h.rearrange("(k p) c -> p k c", p=128)[
                            :, :, cache_colo:cache_colo + 128],
                        waits=wpar + CONV["done"])
                WL = P.snap("ld", "dve")
                for tc in range(NTC):
                    tpar = tc % 2
                    pA_ = pa if tpar == 0 else pb_
                    pB_ = pa2 if tpar == 0 else pb2
                    wmm = WL + gv + prevw.get(("ps", tpar), []) + \
                        (GATE_DONE if (hc == 0 and tpar == 1) else [])
                    for part in range(4):
                        dst = (pA_ if part < 2 else pB_)[
                            :, (part % 2) * TC:(part % 2 + 1) * TC]
                        for k in range(KD):
                            mm(dst, wcol(part, k), xnkt(k, tc),
                               k == 0, k == KD - 1, waits=wmm, inc=(k == KD - 1))
                    pev = P.snap("pe")
                    so = tpar * 2 * TC
                    a1 = vact(sasb[:, so:so + TC], pA_[:, 0:TC], AF.Silu, waits=pev)
                    a2 = vact(sasb[:, so + TC:so + 2 * TC], pB_[:, 0:TC], AF.Silu,
                              waits=[("acts", a1)])
                    d1 = vtt(ehsb[:, so:so + TC], sasb[:, so:so + TC],
                             pA_[:, TC:2 * TC], ALU.mult, waits=[("acts", a1)])
                    d2 = vtt(ehsb[:, so + TC:so + 2 * TC], sasb[:, so + TC:so + 2 * TC],
                             pB_[:, TC:2 * TC], ALU.mult, waits=[("acts", a2)])
                    dma("gpsimd",
                        e_h[128 * hc:128 * (hc + 1), tc * TC:(tc + 1) * TC],
                        ehsb[:, so:so + TC], waits=[("dve", d1)])
                    dma("gpsimd",
                        hh_h[128 * hc:128 * (hc + 1), tc * TC:(tc + 1) * TC],
                        ehsb[:, so + TC:so + 2 * TC], waits=[("dve", d2)])
                    prevw[("ps", tpar)] = [("dve", d2), ("acts", a2),
                                           ("g", P.val("g"))]
                prevw[par] = [("pe", P.val("pe"))]
            P.wait("sync", P.snap("g", "pe", "dve", "acts"))

    def emit_ffnB(call, x_src, x_dst):
        ctx = ExitStack()
        tag = str(len(P.q["sync"]))
        sb = lambda n, s, d: ctx.enter_context(nc.sbuf_tensor(n + "_" + tag, list(s), d))
        wosb = sb("fb_wo", (128, H16 * D), BF16)
        wscr = sb("fb_ws", (128, D), F32)
        w2sb = sb("fb_w2", (128, 2 * 4 * D), BF16)
        esb = sb("fb_e", (128, H16 * TC), BF16)
        hsb = sb("fb_h", (128, H16 * TC), BF16)
        scs = sb("fb_scs", (128, 2 * H16 * TC), BF16)
        repsb = sb("fb_rep", (128, NEXP * TC), BF16)
        wserow = sb("fb_wr", (1, NEXP * TC), BF16)
        xsb = sb("fb_x", (128, KD * TC), F32)
        osb = sb("fb_o", (128, KD * TC), F32)
        prt = ctx.enter_context(nc.psum_tensor("fb_p_" + tag, [128, 8 * TC], F32))
        with ctx:
            for i in range(H16):
                dma("sync", wosb[:, i * D:(i + 1) * D],
                    woutbf_h[128 * i:128 * (i + 1), :], waits=CONV["done"])
            CW = P.snap("dve", "ld")
            prev = {}
            w2done = {}
            for tc in range(NTC):
                w = list(CW) + prev.get("cons", [])
                dma("sync", esb[:].rearrange("p (k t) -> p k t", k=H16),
                    e_h.rearrange("(k p) t -> p k t", p=128)[
                        :, :, tc * TC:(tc + 1) * TC], waits=w)
                dma("sync", hsb[:].rearrange("p (k t) -> p k t", k=H16),
                    hh_h.rearrange("(k p) t -> p k t", p=128)[
                        :, :, tc * TC:(tc + 1) * TC])
                dma("gpsimd", xsb[:].rearrange("p (k t) -> p k t", k=KD),
                    xchunks(x_src, tc), waits=w)
                ldv = P.snap("ld")
                gvx = P.snap("g")
                dma("sync",
                    wserow[0:1, :].rearrange("o (e t) -> o e t", e=NEXP),
                    wse2_h[0:1, :].rearrange("o (e t) -> o e t", e=NEXP)[
                        :, :, tc * TC:(tc + 1) * TC],
                    waits=[("pe", P.val("pe"))])
                ldw = P.snap("ld")
                reps = []
                for ex in range(NEXP):
                    te = mm(prt[:, ex * TC:(ex + 1) * TC], ones_row_b[:],
                            wserow[0:1, ex * TC:(ex + 1) * TC],
                            True, True,
                            waits=ldw + prev.get("evac", []) + CONST_READY,
                            inc=True)
                    reps.append(vact(repsb[:, ex * TC:(ex + 1) * TC],
                                     prt[:, ex * TC:(ex + 1) * TC], AF.Copy,
                                     waits=[("pe", te)]))
                REPS = [("acts", reps[-1])]
                for pch in range(KD):
                    wmm = ldv + [("acts", reps[pch])] + prev.get("evac", [])
                    for k in range(H16):
                        mm(prt[:, pch * TC:(pch + 1) * TC],
                           wosb[:, k * D + 128 * pch:k * D + 128 * (pch + 1)],
                           esb[:, k * TC:(k + 1) * TC],
                           k == 0, False, waits=wmm, inc=(k == H16 - 1))
                prevex = {}
                for ex in range(NEXP):
                    spar = ex % 2
                    sco = spar * H16 * TC
                    scv = []
                    wprev = prevex.get(spar, [])
                    for k in range(H16):
                        eng = "vector" if k % 2 == 0 else "gpsimd"
                        fn = (lambda e, o=scs[:, sco + k * TC:sco + (k + 1) * TC],
                              a=hsb[:, k * TC:(k + 1) * TC],
                              b=repsb[:, ex * TC:(ex + 1) * TC]:
                              e.tensor_tensor(o, a, b, ALU.mult))
                        scv.append((ECTR[eng],
                                    ew(eng, fn,
                                       waits=[("acts", reps[ex])] + ldv + wprev)))
                    for gq in range(4):
                        gpar = (ex * 4 + gq) % 2
                        wo2 = gpar * 4 * D
                        wld = w2done.get(gpar, [])
                        dma("sync",
                            w2sb[:, wo2:wo2 + 4 * D].rearrange(
                                "p (k c) -> p k c", k=4),
                            w2bf_h[ex].rearrange("(q p) c -> p q c", p=128)[
                                :, gq * 4:(gq + 1) * 4, :],
                            waits=wld + CONV["done"])
                        wgrp = P.snap("ld")
                        for kk in range(4):
                            k = gq * 4 + kk
                            laststop = (ex == NEXP - 1 and k == H16 - 1)
                            for pch in range(KD):
                                mm(prt[:, pch * TC:(pch + 1) * TC],
                                   w2sb[:, wo2 + kk * D + 128 * pch:
                                        wo2 + kk * D + 128 * (pch + 1)],
                                   scs[:, sco + k * TC:sco + (k + 1) * TC],
                                   False, laststop,
                                   waits=wgrp + [scv[k]],
                                   inc=(pch == KD - 1 and kk == 3))
                        w2done[gpar] = [("pe", P.val("pe"))]
                    prevex[spar] = [("pe", P.val("pe"))]
                pev = P.snap("pe")
                evs = []
                for pch in range(KD):
                    evs.append(vtt(osb[:, pch * TC:(pch + 1) * TC],
                                   prt[:, pch * TC:(pch + 1) * TC],
                                   xsb[:, pch * TC:(pch + 1) * TC], ALU.add,
                                   waits=pev + gvx))
                gv2 = dma("gpsimd", xchunks(x_dst, tc),
                          osb[:].rearrange("p (k t) -> p k t", k=KD),
                          waits=[("dve", evs[-1])])
                prev["cons"] = [("dve", evs[-1]), ("pe", P.val("pe")),
                                ("g", gv2), ("pool", P.val("pool"))]
                prev["evac"] = [("dve", evs[-1])]
            P.wait("sync", P.snap("g", "pe", "dve", "acts", "pool"))

    def emit_attn_b():
        ctx = ExitStack()
        tag = str(len(P.q["sync"]))
        sb = lambda n, s, d: ctx.enter_context(nc.sbuf_tensor(n + "_" + tag, list(s), d))
        wsb = sb("ab_w", (128, KD * D), BF16)
        xnsb = sb("ab_xn", (128, KD * TC), BF16)
        ktsb = sb("ab_kt", (128, KD * T), BF16)
        vpsb = sb("ab_vp", (128, 32 * 1040), BF16)
        qrsb = sb("ab_qct", (128, 2 * KD * TC), BF16)
        cssb = sb("ab_cs", (128, 4 * TC), F32)
        scr = sb("ab_scr", (128, D), F32)
        rfsb = scr[:, 0:TC]
        rwsb = scr[:, TC:2 * TC]
        t1sb = sb("ab_t1", (128, D), F32)
        t2sb = sb("ab_t2", (128, D), F32)
        wscr = t1sb
        ptsb = sb("ab_pt", (128, 2 * 1024), BF16)
        repsb = sb("ab_rep", (128, TC), F32)
        aostg = sb("ab_stg", (128, 2 * TC), BF16)
        s0 = ctx.enter_context(nc.psum_tensor("ab_s0_" + tag, [128, 1024], F32))
        s1 = ctx.enter_context(nc.psum_tensor("ab_s1_" + tag, [128, 1024], F32))
        oAB = ctx.enter_context(nc.psum_tensor("ab_o_" + tag, [65, 1024], F32))
        pj = ctx.enter_context(nc.psum_tensor("ab_pj_" + tag, [128, TC], F32))
        rp = ctx.enter_context(nc.psum_tensor("ab_rp_" + tag, [128, TC], F32))

        def rot_featmajor(dstbf, ldcs, src, cso):
            a = acopy(rfsb[:], src, waits=[("pe", P.val("pe")),
                                           ("pool", P.val("pool")),
                                           ("dve", P.val("dve"))])
            gsw = 0
            for qd in range(4):
                sq = qd ^ 1
                gsw = dma("gpsimd", rwsb[32 * qd:32 * (qd + 1), :],
                          rfsb[32 * sq:32 * (sq + 1), :], waits=[("acts", a)])
            p1 = ptt(t1sb[:, 0:TC], rfsb[:], cssb[:, cso:cso + TC], ALU.mult,
                     waits=[("acts", a)] + ldcs)
            d1 = vtt(t2sb[:, 0:TC], rwsb[:], cssb[:, cso + TC:cso + 2 * TC],
                     ALU.mult, waits=[("g", gsw)] + ldcs)
            d2 = vtt(dstbf, t1sb[:, 0:TC], t2sb[:, 0:TC], ALU.add,
                     waits=[("pool", p1), ("dve", d1)])
            return d2

        with ctx:
            conv_w(Wkvb, [wsb[:, i * D:(i + 1) * D] for i in range(KD)], wscr)
            ew("vector", lambda e: e.memset(vpsb[:], 1.0))
            CW = P.snap("dve")
            banksK = [s0[:, 0:TC], s0[:, TC:2 * TC], s1[:, 0:TC],
                      s1[:, TC:2 * TC], pj[:], rp[:]]
            banksT = [b.bitcast(BF16)[:, 0:128] for b in banksK]
            NBK = 6
            bank_free = {}
            ib = 0
            prev = {}
            for tc in range(NTC):
                w = list(CW) + prev.get("c", [])
                dma("gpsimd", xnsb[:].rearrange("p (k t) -> p k t", k=KD),
                    xchunks(xn_h, tc), waits=w)
                gxn = P.snap("g")
                cso = (tc % 2) * 2 * TC
                dma("sync", cssb[:, cso:cso + TC],
                    cosF[:, tc * TC:(tc + 1) * TC], waits=w)
                dma("sync", cssb[:, cso + TC:cso + 2 * TC],
                    sinF[:, tc * TC:(tc + 1) * TC])
                ldcs = P.snap("ld")
                xnk = lambda k: xnsb[:, k * TC:(k + 1) * TC]
                rotd = {}
                for f in range(KD):
                    b = ib % NBK
                    ib += 1
                    wmm = gxn + bank_free.get(b, [])
                    for k in range(KD):
                        mm(banksK[b], wsb[:, k * D + 128 * f:k * D + 128 * (f + 1)],
                           xnk(k), k == 0, k == KD - 1, waits=wmm,
                           inc=(k == KD - 1))
                    rotd[f] = rot_featmajor(
                        ktsb[:, f * T + tc * TC:f * T + (tc + 1) * TC],
                        ldcs, banksK[b], cso)
                    bank_free[b] = [("acts", P.val("acts"))]
                for tl in range(4):
                    kt = tc * 4 + tl
                    for f in range(KD):
                        b = ib % NBK
                        ib += 1
                        obf = banksT[b]
                        wtr = [("dve", rotd[f])] + bank_free.get(b, [])
                        P.op("tensor",
                             lambda e, o=obf,
                             a=ktsb[:, f * T + kt * 128:f * T + (kt + 1) * 128],
                             i=ident_s[:]: e.transpose(o, a, i),
                             waits=wtr + CONST_READY, inc=("pe", 1))
                        te = P.val("pe")
                        base = kt * 1040 + 2 * f * 65
                        dst = vpsb[:, base:base + 130].rearrange(
                            "p (c d) -> p c d", c=2)[:, :, 0:64]
                        dv = vcopy(dst, obf.rearrange("p (c d) -> p c d", c=2),
                                   waits=[("pe", te)])
                        bank_free[b] = [("dve", dv)]
                prev["c"] = [("dve", P.val("dve")), ("pe", P.val("pe")),
                             ("pool", P.val("pool")), ("acts", P.val("acts")),
                             ("g", P.val("g"))]
            P.wait("sync", P.snap("pe", "dve", "acts", "pool"))
            for i in range(KD):
                dma("sync", wsb[:, i * D:(i + 1) * D],
                    wqbf_h[128 * i:128 * (i + 1), :], waits=CONV["done"])
            KVD = P.snap("dve", "pool", "pe", "acts", "g", "ld")
            sp2 = [s0, s1]
            st = {"chunk_end": {}, "proj_pe": {}}
            chunk_w = {}
            est = {"cp": 0, "rppe": {}}
            rst = {"cpv": 0, "gsw": 0, "p1": 0, "d2": 0}

            def rot_q(dstbf, ldcs, base, guard_pe, cso):
                cpv = vcopy(rfsb[:], pj[:],
                            waits=P.snap("pe") + base +
                            [("pool", rst["p1"]), ("g", rst["gsw"])])
                gsw = 0
                for qd in range(4):
                    sq = qd ^ 1
                    gsw = dma("gpsimd", rwsb[32 * qd:32 * (qd + 1), :],
                              rfsb[32 * sq:32 * (sq + 1), :],
                              waits=[("dve", cpv)])
                p1 = ptt(t1sb[:, 0:TC], rfsb[:], cssb[:, cso:cso + TC], ALU.mult,
                         waits=[("dve", cpv), ("dve", rst["d2"])] + ldcs + base)
                d1 = vtt(t2sb[:, 0:TC], rwsb[:], cssb[:, cso + TC:cso + 2 * TC],
                         ALU.mult, waits=[("g", gsw)] + ldcs)
                d2 = vtt(dstbf, t1sb[:, 0:TC], t2sb[:, 0:TC], ALU.add,
                         waits=[("pool", p1), ("pe", guard_pe)] + base)
                rst.update(cpv=cpv, gsw=gsw, p1=p1, d2=d2)
                return d2

            def prologue_units(qc, base):
                qo = (qc % 2) * KD * TC
                state = {"gld": [], "cs": []}
                units = []

                def u_load():
                    dma("gpsimd", xnsb[:].rearrange("p (k t) -> p k t", k=KD),
                        xchunks(xn_h, qc),
                        waits=[("pe", st["proj_pe"].get(qc - 1, 0))] + CW + base)
                    state["gld"] = P.snap("g")
                    cso = (qc % 2) * 2 * TC
                    dma("sync", cssb[:, cso:cso + TC],
                        cosF[:, qc * TC:(qc + 1) * TC],
                        waits=[("pool", rst["p1"]), ("dve", rst["d2"])] + base)
                    dma("sync", cssb[:, cso + TC:cso + 2 * TC],
                        sinF[:, qc * TC:(qc + 1) * TC])
                    state["cs"] = P.snap("ld")
                units.append(u_load)

                def mk_rot(f):
                    def u():
                        wmm = state["gld"] + base + [("dve", rst["cpv"])]
                        for k in range(KD):
                            mm(pj[:], wsb[:, k * D + 128 * f:k * D + 128 * (f + 1)],
                               xnsb[:, k * TC:(k + 1) * TC], k == 0, k == KD - 1,
                               waits=wmm, inc=(k == KD - 1))
                        d2 = rot_q(qrsb[:, qo + f * TC:qo + (f + 1) * TC],
                                   state["cs"], base,
                                   st["chunk_end"].get(qc - 2, 0),
                                   (qc % 2) * 2 * TC)
                        if f == KD - 1:
                            chunk_w[qc] = [("dve", d2)] + base
                            st["proj_pe"][qc] = P.val("pe")
                    return u
                for f in range(KD):
                    units.append(mk_rot(f))
                return units

            def qkops(jg, w):
                qc, r = divmod(jg, 256)
                f, kt = divmod(r, 32)
                qo = (qc % 2) * KD * TC
                sbk = sp2[jg % 2]
                mm(sbk[:, 0:TC],
                   ktsb[0:64, f * T + kt * 128:f * T + (kt + 1) * 128],
                   qrsb[0:64, qo + f * TC:qo + (f + 1) * TC],
                   True, True, waits=w, tp=(0, 0))
                return mm(sbk[:, TC:1024],
                          ktsb[64:128, f * T + kt * 128:f * T + (kt + 1) * 128],
                          qrsb[64:128, qo + f * TC:qo + (f + 1) * TC],
                          True, True, tp=(64, 0), inc=True)

            def pvops(jg, w, start, stop):
                qc, r = divmod(jg, 256)
                f, kt = divmod(r, 32)
                ip = jg % 2
                h0 = 2 * f
                vA = vpsb[:, kt * 1040 + h0 * 65:kt * 1040 + h0 * 65 + 65]
                vB = vpsb[:, kt * 1040 + (h0 + 1) * 65:
                          kt * 1040 + (h0 + 1) * 65 + 65]
                mm(oAB[:, 0:TC], vA, ptsb[:, ip * 1024:ip * 1024 + TC],
                   start, stop, waits=w)
                return mm(oAB[:, TC:1024], vB,
                          ptsb[:, ip * 1024 + TC:(ip + 1) * 1024],
                          start, stop, inc=True)

            def store_b(e, spar, n2):
                qc, f = divmod(e, 8)
                return dma("gpsimd",
                           ao_h.rearrange("(k p) t -> p k t", p=128)[
                               :, f, qc * TC:(qc + 1) * TC],
                           aostg[:, spar * TC:(spar + 1) * TC],
                           waits=[("dve", n2)])

            def end_b(e, pvlast):
                qc, f = divmod(e, 8)
                if f == 7:
                    st["chunk_end"][qc] = pvlast

            epiops = mk_epilogue(oAB, rp, repsb, aostg, store_b, end_b)

            for u in prologue_units(0, KVD):
                u()
            feeds = {qc: prologue_units(qc, []) for qc in range(1, 8)}
            run_scores(8, sp2, ptsb, qkops, pvops, epiops, feeds, chunk_w)
            P.wait("sync", P.snap("g", "pe", "dve", "acts", "pool"))

    seq = {
        "ln1": lambda: emit_ln(xT_in),
        "attna": emit_attn_a,
        "proja": lambda: emit_outproj(woaT, xT_in, x_h0, None),
        "ffn1": lambda: (emit_ffnA(0), emit_ffnB(0, x_h0, x_h1)),
        "ln3": lambda: emit_ln(x_h1),
        "attnb": emit_attn_b,
        "projb": lambda: emit_outproj(Woutb, x_h1, x_h0, boutb_s),
        "ffn2": lambda: (emit_ffnA(1), emit_ffnB(1, x_h0, out_T)),
    }
    for ph in phases:
        seq[ph]()

    P.wait("sync", P.snap("ld", "g", "pe", "dve", "acts", "pool"))

    with nc.Block() as blk:
        P.replay(blk)
    root.close()
    return nc


_CACHED = {}


def _get_program():
    if "nc" not in _CACHED:
        _CACHED["nc"] = build_program()
    return _CACHED["nc"]


def kernel(**inputs):
    cosF_t, sinF_t = _host_tables()
    f32 = np.float32
    asnp = lambda a: np.asarray(a)
    common = {
        "wqaT": np.ascontiguousarray(asnp(inputs["L0_attn_a_wq"]).T.astype(f32)),
        "wkaT": np.ascontiguousarray(asnp(inputs["L0_attn_a_wk"]).T.astype(f32)),
        "wvaT": np.ascontiguousarray(asnp(inputs["L0_attn_a_wv"]).T.astype(f32)),
        "woaT": np.ascontiguousarray(asnp(inputs["L0_attn_a_wo"]).T.astype(f32)),
        "Wqb": np.ascontiguousarray(asnp(inputs["L0_attn_b_Wq"]).astype(f32)),
        "Wkvb": np.ascontiguousarray(asnp(inputs["L0_attn_b_Wkv"]).astype(f32)),
        "Woutb": np.ascontiguousarray(asnp(inputs["L0_attn_b_Wout"]).astype(f32)),
        "boutb": np.ascontiguousarray(asnp(inputs["L0_attn_b_bout"]).astype(f32).reshape(1, D)),
        "winT": np.ascontiguousarray(asnp(inputs["L0_ffn_w_in"]).T.astype(f32)),
        "w1T": np.ascontiguousarray(asnp(inputs["L0_ffn_w1"]).T.astype(f32)),
        "woutT": np.ascontiguousarray(asnp(inputs["L0_ffn_w_out"]).T.astype(f32)),
        "w2T": np.ascontiguousarray(asnp(inputs["L0_ffn_w2"]).transpose(0, 2, 1).astype(f32)),
        "wgeT": np.ascontiguousarray(np.concatenate(
            [asnp(inputs["L0_ffn_wg"]).T, asnp(inputs["L0_ffn_we"]).T], axis=1).astype(f32)),
        "gbeb": np.ascontiguousarray(np.concatenate(
            [asnp(inputs["L0_ffn_gb"]), asnp(inputs["L0_ffn_eb"])]).astype(f32).reshape(1, 10)),
        "cosF": cosF_t, "sinF": sinF_t,
    }
    x = asnp(inputs["L0_attn_a_x"])
    in_maps = []
    for b in range(8):
        m = dict(common)
        m["xT"] = np.ascontiguousarray(x[b].T.astype(f32))
        in_maps.append(m)
    nc = _get_program()
    res = run_bass_kernel_spmd(nc, in_maps, core_ids=list(range(8)))
    out = np.stack([np.ascontiguousarray(r["out_T"].T) for r in res.results], axis=0)
    return out


# version 21
# speedup vs baseline: 1.0368x; 1.0020x over previous
import numpy as np
import ml_dtypes
import concourse.bass as bass
import concourse.mybir as mybir
from concourse.bass_utils import run_bass_kernel_spmd

F32 = mybir.dt.float32
BF16 = mybir.dt.bfloat16
AF = mybir.ActivationFunctionType
ALU = mybir.AluOpType
AX = mybir.AxisListType

D = 1024
T = 4096
TC = 512
NTC = 8
KD = 8
H16 = 16
HID = 2048
NEXP = 8
SCALE = 0.125
CSHIFT = -12.0
EPS = 1e-5
GMASK = 60.0

_BF = ml_dtypes.bfloat16

ALL_PHASES = ("ln1", "attna", "proja", "ffn1",
              "attnb", "projb", "ffn2")


def _host_tables():
    inv = 1.0 / (10000.0 ** (np.arange(0, 64, 2, dtype=np.float64) / 64))
    freqs = np.arange(T, dtype=np.float64)[:, None] * inv[None, :]
    fr = np.concatenate([freqs, freqs], axis=1)
    cos64 = np.cos(fr)
    sgn = np.where(np.arange(64) < 32, -1.0, 1.0)[None, :]
    sinP = np.sin(fr) * sgn
    cosF = np.ascontiguousarray(np.tile(cos64.T, (2, 1)).astype(np.float32))
    sinF = np.ascontiguousarray(np.tile(sinP.T, (2, 1)).astype(np.float32))
    return cosF, sinF


class Prog:
    ENGS = ("sync", "gpsimd", "vector", "scalar", "tensor")

    def __init__(self, nc):
        self.nc = nc
        self.q = {e: [] for e in self.ENGS}
        self.sem = {}
        self.cnt = {}
        self.lastwait = {e: {} for e in self.ENGS}

    def new_sem(self, name):
        self.sem[name] = self.nc.alloc_semaphore(name)
        self.cnt[name] = 0

    def val(self, name):
        return self.cnt[name]

    def snap(self, *names):
        return [(n, self.cnt[n]) for n in names]

    def op(self, eng, fn, waits=(), inc=None):
        lw = self.lastwait[eng]
        w = []
        for s, v in waits:
            if v > 0 and lw.get(s, 0) < v:
                lw[s] = v
                w.append((self.sem[s], v))
        vinc = None
        if inc is not None:
            self.cnt[inc[0]] += inc[1]
            vinc = (self.sem[inc[0]], inc[1])

        def run(e, fn=fn, w=tuple(w), vinc=vinc):
            for s, v in w:
                e.wait_ge(s, v)
            ins = fn(e)
            if vinc is not None and ins is not None:
                ins.then_inc(vinc[0], vinc[1])
        self.q[eng].append(run)

    def wait(self, eng, waits):
        self.op(eng, lambda e: None, waits=waits)

    def replay(self, blk):
        def mk(name):
            def body(eng):
                for fn in self.q[name]:
                    fn(eng)
            return body
        blk.sync(mk("sync"))
        blk.gpsimd(mk("gpsimd"))
        blk.vector(mk("vector"))
        blk.scalar(mk("scalar"))
        blk.tensor(mk("tensor"))


ECTR = {"vector": "dve", "scalar": "acts", "gpsimd": "pool"}


def build_program(phases=ALL_PHASES):
    nc = bass.Bass("TRN2", target_bir_lowering=False, debug=False)
    P = Prog(nc)
    for s in ("ld", "g", "pe", "dve", "acts", "pool"):
        P.new_sem(s)

    def din(name, shape):
        return nc.dram_tensor(name, list(shape), F32, kind="ExternalInput").ap()

    xT_in = din("xT", (D, T))
    wqaT = din("wqaT", (D, D)); wkaT = din("wkaT", (D, D))
    wvaT = din("wvaT", (D, D)); woaT = din("woaT", (D, D))
    Wqb = din("Wqb", (D, D)); Wkvb = din("Wkvb", (D, D)); Woutb = din("Woutb", (D, D))
    boutb = din("boutb", (1, D))
    winT = din("winT", (D, 2 * HID)); w1T = din("w1T", (D, 2 * HID))
    woutT = din("woutT", (HID, D))
    w2T = din("w2T", (NEXP, HID, D))
    wgeT = din("wgeT", (D, 10))
    gbeb = din("gbeb", (1, 10))
    cosF = din("cosF", (128, T)); sinF = din("sinF", (128, T))

    out_T = nc.dram_tensor("out_T", [D, T], F32, kind="ExternalOutput").ap()

    def dscr(name, shape, dt):
        return nc.dram_tensor(name, list(shape), dt, kind="Internal").ap()

    x_h0 = dscr("x_h0", (D, T), F32)
    x_h1 = dscr("x_h1", (D, T), F32)
    xn_h = dscr("xn_h", (D, T), BF16)
    ao_h = dscr("ao_h", (D, T), BF16)
    e_h = dscr("e_h", (HID, T), BF16)
    hh_h = dscr("hh_h", (HID, T), BF16)
    w2bf_h = dscr("w2bf_h", (NEXP, HID, D), BF16)
    wffbf_h = dscr("wffbf_h", (D, 4 * HID), BF16)
    woutbf_h = dscr("woutbf_h", (HID, D), BF16)
    wqbf_h = dscr("wqbf_h", (D, D), BF16)
    wse2_h = dscr("wse2_h", (1, NEXP * T), BF16)

    ident_h = nc.inline_tensor(np.eye(128).astype(_BF), name="ident128").ap()

    from contextlib import ExitStack
    root = ExitStack()

    def sbuf(n, s, d):
        return root.enter_context(nc.sbuf_tensor(n, list(s), d))

    ones_col_f = sbuf("ones_col_f", (128, 1), F32)
    ones_col_b = sbuf("ones_col_b", (128, 1), BF16)
    ones_row_f = sbuf("ones_row_f", (1, 128), F32)
    ones_row_b = sbuf("ones_row_b", (1, 128), BF16)
    shiftc = sbuf("shiftc", (128, 1), F32)
    eps_t = sbuf("eps_t", (1, 1), F32)
    ident_s = sbuf("ident_s", (128, 128), BF16)
    stat_s = sbuf("stat_s", (1, 4 * TC), F32)
    gbias_s = sbuf("gbias_s", (128, 10), F32)
    boutb_s = sbuf("boutb_s", (128, KD), F32)
    grow_s = sbuf("grow_s", (1, 10), F32)
    g320_s = sbuf("g320_s", (128, 320), F32)

    for buf, v in ((ones_col_f, 1.0), (ones_col_b, 1.0), (ones_row_f, 1.0),
                   (ones_row_b, 1.0), (shiftc, CSHIFT), (eps_t, EPS)):
        P.op("vector", lambda e, b=buf, vv=v: e.memset(b[:], vv), inc=("dve", 1))
    P.op("sync", lambda e: e.dma_start(ident_s[:], ident_h[:]), inc=("ld", 16))
    P.op("sync", lambda e: e.dma_start(grow_s[:], gbeb[:]), inc=("ld", 16))
    for f in range(KD):
        P.op("sync", lambda e, f=f: e.dma_start(
            boutb_s[:, f:f + 1],
            boutb[0:1, 128 * f:128 * (f + 1)].rearrange("o p -> p o")),
            inc=("ld", 16))
    INIT = P.snap("ld", "dve")
    pgb_g = nc.psum_tensor("pgb", [128, 16], F32)
    with pgb_g as pgb:
        P.op("tensor", lambda e: e.matmul(pgb[:, 0:10], ones_row_f[:], grow_s[:],
                                          start=True, stop=True),
             waits=INIT, inc=("pe", 1))
        P.op("vector", lambda e: e.tensor_copy(gbias_s[:], pgb[:, 0:10]),
             waits=P.snap("pe"), inc=("dve", 1))
    for j in range(32):
        P.op("vector", lambda e, j=j: e.tensor_copy(
            g320_s[:, 10 * j:10 * (j + 1)], gbias_s[:, 0:10]),
            waits=P.snap("dve"), inc=("dve", 1))
    CONST_READY = P.snap("dve", "ld")

    def dma(eng, dst, src, waits=()):
        sem = "ld" if eng == "sync" else "g"
        P.op(eng, lambda e, d=dst, s=src: e.dma_start(d, s),
             waits=waits, inc=(sem, 16))
        return P.val(sem)

    def mm(out, lhsT, rhs, start, stop, waits=(), inc=False, tp=None):
        P.op("tensor",
             lambda e, o=out, l=lhsT, r=rhs, st=start, sp=stop, t=tp:
             e.matmul(o, l, r, start=st, stop=sp, tile_position=t),
             waits=waits, inc=("pe", 1) if inc else None)
        return P.val("pe")

    def ew(eng, fn, waits=()):
        c = ECTR[eng]
        P.op(eng, fn, waits=waits, inc=(c, 1))
        return P.val(c)

    def vcopy(out, in_, waits=()):
        return ew("vector", lambda e, o=out, a=in_: e.tensor_copy(o, a), waits)

    def acopy(out, in_, waits=()):
        return ew("scalar", lambda e, o=out, a=in_: e.activation(o, a, AF.Copy), waits)

    def vtt(out, a, b, op, waits=()):
        return ew("vector", lambda e, o=out, x=a, y=b, p=op: e.tensor_tensor(o, x, y, p), waits)

    def ptt(out, a, b, op, waits=()):
        return ew("gpsimd", lambda e, o=out, x=a, y=b, p=op: e.tensor_tensor(o, x, y, p), waits)

    def vact(out, in_, func, bias=0.0, scale=1.0, accum=None, waits=()):
        return ew("scalar",
                  lambda e, o=out, a=in_, f=func, b=bias, s=scale, ac=accum:
                  e.activation(o, a, f, bias=b, scale=s, accum_out=ac), waits)

    def xchunks(hbm, tc):
        return hbm.rearrange("(k p) t -> p k t", p=128)[:, :, tc * TC:(tc + 1) * TC]

    def conv_w(src, dst_tiles, scratch, cache=None):
        for i, dt_ in enumerate(dst_tiles):
            dma("sync", scratch[:], src[128 * i:128 * (i + 1), :],
                waits=P.snap("dve"))
            dv = vcopy(dt_, scratch[:], waits=P.snap("ld"))
            if cache is not None:
                dma("gpsimd", cache[128 * i:128 * (i + 1), :], dt_,
                    waits=[("dve", dv)])

    def emit_ln(x_src):
        ctx = ExitStack()
        tag = str(len(P.q["sync"]))
        sb = lambda n, s, d: ctx.enter_context(nc.sbuf_tensor(n + "_" + tag, list(s), d))
        lx = sb("ln_x", (128, 2 * KD * TC), F32)
        lsq = sb("ln_sq", (128, 2 * KD * TC), BF16)
        lt = sb("ln_t", (128, 2 * KD * TC), F32)
        lxn = sb("ln_xn", (128, 2 * KD * TC), BF16)
        mrep_s = sb("ln_mrep", (128, TC), F32)
        ln_st = sb("ln_st", (1, 10 * TC), F32)
        pssum2 = ctx.enter_context(nc.psum_tensor("ln_ps_" + tag, [1, 2 * TC], F32))
        psq2 = ctx.enter_context(nc.psum_tensor("ln_pq_" + tag, [1, 2 * TC], F32))
        pm = ctx.enter_context(nc.psum_tensor("ln_pm_" + tag, [128, TC], F32))
        pr = ctx.enter_context(nc.psum_tensor("ln_pr_" + tag, [128, TC], F32))
        with ctx:
            prev = {}
            for tc in range(NTC):
                par = tc % 2
                off = par * KD * TC
                so_ = par * 5 * TC
                SM = ln_st[0:1, so_:so_ + TC]
                MEAN = ln_st[0:1, so_ + TC:so_ + 2 * TC]
                RINV = ln_st[0:1, so_ + 2 * TC:so_ + 3 * TC]
                VAR = ln_st[0:1, so_ + 3 * TC:so_ + 4 * TC]
                SD = ln_st[0:1, so_ + 4 * TC:so_ + 5 * TC]
                pssum = pssum2[0:1, par * TC:(par + 1) * TC]
                psq = psq2[0:1, par * TC:(par + 1) * TC]
                xv = lambda k, off=off: lx[:, off + k * TC:off + (k + 1) * TC]
                w = list(CONST_READY) + prev.get(par, [])
                dma("sync", lx[:, off:off + KD * TC].rearrange(
                    "p (k t) -> p k t", k=KD), xchunks(x_src, tc), waits=w)
                ldv = P.snap("ld")
                sqv = []
                for k in range(KD):
                    sqv.append(vact(lsq[:, off + k * TC:off + (k + 1) * TC],
                                    xv(k), AF.Square,
                                    waits=ldv + prev.get(("sq", par), [])))
                wstat = ldv + prev.get(("stats", par), [])
                for k in range(KD):
                    mm(pssum, ones_col_f[:], xv(k), k == 0, k == KD - 1,
                       waits=wstat, inc=(k == KD - 1))
                for k in range(KD):
                    mm(psq, ones_col_b[:], lsq[:, off + k * TC:off + (k + 1) * TC],
                       k == 0, k == KD - 1,
                       waits=wstat + [("acts", sqv[k])], inc=(k == KD - 1))
                pev = P.snap("pe")
                d0 = ew("vector", lambda e, M=MEAN, ps=pssum:
                        e.tensor_scalar_mul(M, ps, 1.0 / D), waits=pev)
                d0 = ew("vector", lambda e, V=VAR, ps=psq:
                        e.tensor_scalar_mul(V, ps, 1.0 / D), waits=[("dve", d0)])
                d0 = vtt(SM, MEAN, MEAN, ALU.mult, waits=[("dve", d0)])
                d1 = vtt(VAR, VAR, SM, ALU.subtract, waits=[("dve", d0)])
                prev[("stats", par)] = [("dve", d1)]
                a1 = vact(SD, VAR, AF.Ln, bias=eps_t[0:1, 0:1],
                          waits=[("dve", d1)] + CONST_READY)
                a1 = vact(RINV, SD, AF.Exp, scale=-0.5, waits=[("acts", a1)])
                mm(pm[:], ones_row_f[:], MEAN, True, True,
                   waits=[("dve", d1)] + prev.get("reps", []))
                pev2 = mm(pr[:], ones_row_f[:], RINV, True, True,
                          waits=[("acts", a1)], inc=True)
                a2 = acopy(mrep_s[:], pm[:], waits=[("pe", pev2)])
                pl = []
                for k in range(KD):
                    eng = "gpsimd" if k % 2 == 0 else "vector"
                    v = ew(eng, lambda e, o=lt[:, off + k * TC:off + (k + 1) * TC],
                           a=xv(k), b=mrep_s[:]:
                           e.tensor_tensor(o, a, b, ALU.subtract),
                           waits=[("acts", a2)] + prev.get(("lt", par), []))
                    pl.append((ECTR[eng], v))
                dv = []
                for k in range(KD):
                    dv.append(vtt(lxn[:, off + k * TC:off + (k + 1) * TC],
                                  lt[:, off + k * TC:off + (k + 1) * TC],
                                  pr[:], ALU.mult, waits=[pl[k]]))
                gv = dma("gpsimd", xchunks(xn_h, tc),
                         lxn[:, off:off + KD * TC].rearrange("p (k t) -> p k t", k=KD),
                         waits=[("dve", dv[-1])])
                prev[par] = [pl[-2], pl[-1], ("acts", sqv[-1]), ("g", gv),
                             ("pe", P.val("pe")), ("dve", dv[-1])]
                prev[("sq", par)] = [("pe", P.val("pe"))]
                prev[("lt", par)] = [("dve", dv[-1])]
                prev["reps"] = [("dve", dv[-1]), ("acts", a2)]
            P.wait("sync", P.snap("g", "pool", "dve", "acts", "pe"))

    def emit_outproj(Wsrc, x_src, x_dst, bias):
        ctx = ExitStack()
        tag = str(len(P.q["sync"]))
        sb = lambda n, s, d: ctx.enter_context(nc.sbuf_tensor(n + "_" + tag, list(s), d))
        wsb = sb("op_w", (128, KD * D), BF16)
        wscr = sb("op_ws", (128, D), F32)
        aosb = sb("op_ao", (128, 2 * KD * TC), BF16)
        xsb = sb("op_x", (128, 2 * KD * TC), F32)
        osb = sb("op_o", (128, 2 * KD * TC), F32)
        lsq = sb("op_sq", (128, 2 * KD * TC), BF16)
        lt = sb("op_lt", (128, 2 * KD * TC), F32)
        lxn = sb("op_xn", (128, 2 * KD * TC), BF16)
        mrep_s = sb("op_mr", (128, TC), F32)
        op_st = sb("op_st", (1, 10 * TC), F32)
        p0 = ctx.enter_context(nc.psum_tensor("op_p0_" + tag, [128, TC], F32))
        p1 = ctx.enter_context(nc.psum_tensor("op_p1_" + tag, [128, TC], F32))
        pssum2 = ctx.enter_context(nc.psum_tensor("op_ps_" + tag, [1, 2 * TC], F32))
        psq2 = ctx.enter_context(nc.psum_tensor("op_pq_" + tag, [1, 2 * TC], F32))
        pm = ctx.enter_context(nc.psum_tensor("op_pm_" + tag, [128, TC], F32))
        pr = ctx.enter_context(nc.psum_tensor("op_pr_" + tag, [128, TC], F32))
        with ctx:
            conv_w(Wsrc, [wsb[:, i * D:(i + 1) * D] for i in range(KD)], wscr)
            CW = P.snap("dve")
            pp = [p0, p1]
            prev = {}
            for tc in range(NTC):
                par = tc % 2
                off = par * KD * TC
                w = list(CW) + prev.get(par, [])
                dma("sync", aosb[:, off:off + KD * TC].rearrange(
                    "p (k t) -> p k t", k=KD), xchunks(ao_h, tc), waits=w)
                ldv = P.snap("ld")
                dma("gpsimd", xsb[:, off:off + KD * TC].rearrange(
                    "p (k t) -> p k t", k=KD), xchunks(x_src, tc), waits=w)
                gv = P.snap("g")
                dvs = []
                for f in range(KD):
                    pb = pp[f % 2]
                    if len(dvs) >= 2:
                        wmm = ldv + [("dve", dvs[-2])]
                    else:
                        wmm = ldv + prev.get(("ps", par, f % 2), [])
                    for k in range(KD):
                        mm(pb[:], wsb[:, k * D + 128 * f:k * D + 128 * (f + 1)],
                           aosb[:, off + k * TC:off + (k + 1) * TC],
                           k == 0, k == KD - 1, waits=wmm, inc=(k == KD - 1))
                    pev = P.snap("pe")
                    o = osb[:, off + f * TC:off + (f + 1) * TC]
                    xin = xsb[:, off + f * TC:off + (f + 1) * TC]
                    if bias is None:
                        dvs.append(vtt(o, pb[:], xin, ALU.add, waits=pev + gv))
                    else:
                        dvs.append(ew("vector",
                                      lambda e, o=o, a=pb[:], s=bias[:, f:f + 1],
                                      b=xin: e.scalar_tensor_tensor(
                                          o, a, s, b, ALU.add, ALU.add),
                                      waits=pev + gv + CONST_READY))
                gv2 = dma("gpsimd", xchunks(x_dst, tc),
                          osb[:, off:off + KD * TC].rearrange("p (k t) -> p k t", k=KD),
                          waits=[("dve", dvs[-1])])
                OK = [("dve", dvs[-1])]
                ov = lambda k: osb[:, off + k * TC:off + (k + 1) * TC]
                so_ = par * 5 * TC
                SM = op_st[0:1, so_:so_ + TC]
                MEAN = op_st[0:1, so_ + TC:so_ + 2 * TC]
                RINV = op_st[0:1, so_ + 2 * TC:so_ + 3 * TC]
                VAR = op_st[0:1, so_ + 3 * TC:so_ + 4 * TC]
                SD = op_st[0:1, so_ + 4 * TC:so_ + 5 * TC]
                pssum = pssum2[0:1, par * TC:(par + 1) * TC]
                psq = psq2[0:1, par * TC:(par + 1) * TC]
                sqv = []
                for k in range(KD):
                    sqv.append(vact(lsq[:, off + k * TC:off + (k + 1) * TC],
                                    ov(k), AF.Square,
                                    waits=OK + prev.get(("sq", par), [])))
                wstat = OK + prev.get(("stats", par), [])
                for k in range(KD):
                    mm(pssum, ones_col_f[:], ov(k), k == 0, k == KD - 1,
                       waits=wstat, inc=(k == KD - 1))
                for k in range(KD):
                    mm(psq, ones_col_b[:],
                       lsq[:, off + k * TC:off + (k + 1) * TC],
                       k == 0, k == KD - 1,
                       waits=wstat + [("acts", sqv[k])], inc=(k == KD - 1))
                pev = P.snap("pe")
                d0 = ew("vector", lambda e, M=MEAN, ps=pssum:
                        e.tensor_scalar_mul(M, ps, 1.0 / D), waits=pev)
                d0 = ew("vector", lambda e, V=VAR, ps=psq:
                        e.tensor_scalar_mul(V, ps, 1.0 / D), waits=[("dve", d0)])
                d0 = vtt(SM, MEAN, MEAN, ALU.mult, waits=[("dve", d0)])
                d1 = vtt(VAR, VAR, SM, ALU.subtract, waits=[("dve", d0)])
                prev[("stats", par)] = [("dve", d1)]
                a1 = vact(SD, VAR, AF.Ln, bias=eps_t[0:1, 0:1],
                          waits=[("dve", d1), ("acts", sqv[-1])] + CONST_READY)
                a1 = vact(RINV, SD, AF.Exp, scale=-0.5, waits=[("acts", a1)])
                mm(pm[:], ones_row_f[0:1, :], MEAN, True, True,
                   waits=[("dve", d1)] + prev.get("reps", []))
                pev2 = mm(pr[:], ones_row_f[0:1, :], RINV, True, True,
                          waits=[("acts", a1)], inc=True)
                a2 = acopy(mrep_s[:], pm[:], waits=[("pe", pev2)])
                pl = []
                for k in range(KD):
                    eng = "gpsimd" if k % 2 == 0 else "vector"
                    v = ew(eng, lambda e, o=lt[:, off + k * TC:off + (k + 1) * TC],
                           a=ov(k), b=mrep_s[:]:
                           e.tensor_tensor(o, a, b, ALU.subtract),
                           waits=[("acts", a2)] + prev.get(("lt", par), []))
                    pl.append((ECTR[eng], v))
                dv = []
                for k in range(KD):
                    dv.append(vtt(lxn[:, off + k * TC:off + (k + 1) * TC],
                                  lt[:, off + k * TC:off + (k + 1) * TC],
                                  pr[:], ALU.mult, waits=[pl[k]]))
                gv3 = dma("gpsimd", xchunks(xn_h, tc),
                          lxn[:, off:off + KD * TC].rearrange(
                              "p (k t) -> p k t", k=KD),
                          waits=[("dve", dv[-1])])
                prev[par] = [("dve", dv[-1]), ("g", gv3), ("pe", P.val("pe")),
                             pl[-2], pl[-1], ("acts", sqv[-1])]
                prev[("sq", par)] = [("pe", P.val("pe"))]
                prev[("lt", par)] = [("dve", dv[-1])]
                prev["reps"] = [("dve", dv[-1]), ("acts", a2)]
                prev[("ps", par, 0)] = [("dve", dvs[-2])]
                prev[("ps", par, 1)] = [("dve", dvs[-1])]
            P.wait("sync", P.snap("g", "dve", "pe", "acts", "pool"))

    CONV = {"done": []}

    def conv_jobs():
        jobs = []
        for i in range(KD):
            for j in range(8):
                src_m = winT if j < 4 else w1T
                jc = 1024 * (j % 4)
                jobs.append((src_m[128 * i:128 * (i + 1), jc:jc + 1024],
                             wffbf_h[128 * i:128 * (i + 1),
                                     1024 * j:1024 * (j + 1)]))
        for ex in range(NEXP):
            for k in range(H16):
                jobs.append((w2T[ex, 128 * k:128 * (k + 1), :],
                             w2bf_h[ex, 128 * k:128 * (k + 1), :]))
        for i in range(H16):
            jobs.append((woutT[128 * i:128 * (i + 1), :],
                         woutbf_h[128 * i:128 * (i + 1), :]))
        for i in range(KD):
            jobs.append((Wqb[128 * i:128 * (i + 1), :],
                         wqbf_h[128 * i:128 * (i + 1), :]))
        return jobs

    def run_scores(nchunks, sp2, ptsb, qkops, pvops, epiops, feeds, chunk_w):
        expv = {}
        pvv = {}
        smv = {}
        normd = []
        prevstg = []
        total = nchunks * 256
        for jg in range(total + 2):
            c, r = divmod(jg, 256)
            fl = feeds.get(c + 1)
            if fl is not None and r >= 32 and (r - 32) % 4 == 0:
                ui = (r - 32) // 4
                if ui < len(fl):
                    fl[ui]()
            if jg < total:
                w = [("acts", expv[jg - 2])] if jg >= 2 else []
                if r < 2:
                    w = w + chunk_w.get(c, [])
                smv[jg] = qkops(jg, w)
            j2 = jg - 2
            if j2 >= 0:
                if j2 % 32 == 0 and j2 > 0:
                    normd, prevstg = epiops(j2 // 32 - 1, pvv[j2 - 1],
                                            normd, prevstg)
                w = [("acts", expv[j2])]
                if j2 % 32 == 0:
                    w = w + normd
                    if j2 % 256 == 0:
                        w = w + chunk_w.get(j2 // 256, [])
                pvv[j2] = pvops(j2, w, j2 % 32 == 0, j2 % 32 == 31)
            if jg < total:
                g = pvv[jg - 2] if jg >= 2 else smv[jg]
                ip = jg % 2
                expv[jg] = vact(ptsb[:, ip * 1024:(ip + 1) * 1024], sp2[ip][:],
                                AF.Exp, bias=shiftc[:, 0:1], scale=SCALE,
                                waits=[("pe", g)] + CONST_READY)
        epiops(total // 32 - 1, pvv[total - 1], normd, prevstg)

    def mk_epilogue(oAB, rp, repsb, aostg, store_fn, end_cb):
        est = {"cp": 0, "rppe": {}}

        def epiops(e, pvlast, normd, prevstg):
            spar = e % 2
            lnrow = repsb[0:1, 0:TC]
            rAB = stat_s[0:1, spar * 1024:(spar + 1) * 1024]
            w = [("pe", pvlast)] + normd
            if e >= 2:
                w.append(("pe", est["rppe"][e - 2]))
            rv = []
            for hh in range(2):
                aL = vact(lnrow, oAB[64:65, hh * TC:(hh + 1) * TC], AF.Ln,
                          waits=(w if hh == 0 else [("acts", rv[-1])]))
                rv.append(vact(rAB[0:1, hh * TC:(hh + 1) * TC], lnrow,
                               AF.Exp, scale=-1.0, waits=[("acts", aL)]))
            wrp0 = [("acts", rv[0])] + ([("dve", est["cp"])] if est["cp"] else [])
            mm(rp[0:64, :], ones_row_f[0:1, 0:64], rAB[0:1, 0:TC],
               True, True, waits=wrp0)
            rpe = mm(rp[64:128, :], ones_row_f[0:1, 0:64],
                     rAB[0:1, TC:1024], True, True, inc=True,
                     waits=[("acts", rv[1])])
            est["rppe"][e] = rpe
            cp = vcopy(repsb[:], rp[:], waits=[("pe", rpe)])
            est["cp"] = cp
            n1 = vtt(aostg[0:64, e % 2 * TC:(e % 2 + 1) * TC],
                     oAB[0:64, 0:TC], repsb[0:64, :], ALU.mult,
                     waits=[("dve", cp)] + prevstg)
            n2 = vtt(aostg[64:128, e % 2 * TC:(e % 2 + 1) * TC],
                     oAB[0:64, TC:1024], repsb[64:128, :], ALU.mult,
                     waits=[("dve", n1)])
            gq = store_fn(e, spar, n2)
            end_cb(e, pvlast)
            return [("dve", n2)], [("g", gq)]
        return epiops

    def emit_attn_a():
        ctx = ExitStack()
        tag = str(len(P.q["sync"]))
        sb = lambda n, s, d: ctx.enter_context(nc.sbuf_tensor(n + "_" + tag, list(s), d))
        wsb = sb("aa_w", (128, 3 * KD * D), BF16)
        wscr = sb("aa_ws", (128, D), F32)
        xnsb = sb("aa_xn", (128, 2 * KD * TC), BF16)
        qksb = sb("aa_qk", (128, 2 * 2 * T), BF16)
        vpsb = sb("aa_vp", (128, 2 * 4 * 1040), BF16)
        ptsb = sb("aa_pt", (128, 2 * 1024), BF16)
        repsb = sb("aa_rep", (128, TC), F32)
        aostg = sb("aa_stg", (128, 2 * TC), BF16)
        cvt_f = sb("aa_cvf", (128, 2 * 1024), F32)
        cvt_b = sb("aa_cvb", (128, 2 * 1024), BF16)
        s0 = ctx.enter_context(nc.psum_tensor("aa_s0_" + tag, [128, 1024], F32))
        s1 = ctx.enter_context(nc.psum_tensor("aa_s1_" + tag, [128, 1024], F32))
        oAB = ctx.enter_context(nc.psum_tensor("aa_o_" + tag, [65, 1024], F32))
        pj = ctx.enter_context(nc.psum_tensor("aa_pj_" + tag, [128, TC], F32))
        rp = ctx.enter_context(nc.psum_tensor("aa_rp_" + tag, [128, TC], F32))
        with ctx:
            for wi, wsrc in enumerate((wqaT, wkaT, wvaT)):
                conv_w(wsrc, [wsb[:, (wi * KD + i) * D:(wi * KD + i + 1) * D]
                              for i in range(KD)], wscr)
            ew("vector", lambda e: e.memset(vpsb[:], 1.0))
            CW = P.snap("dve")
            sp2 = [s0, s1]
            st = {"pair_end": {}, "ev": None}
            chunk_w = {}
            est = {"cp": 0, "rppe": {}}

            def proj_units(p):
                par = p % 2
                xno = par * KD * TC
                qko = par * 2 * T
                vpo = par * 4 * 1040
                state = {"ld": []}
                units = []

                def u_load():
                    gpe = st["pair_end"].get(p - 2, 0)
                    dma("sync", xnsb[:, xno:xno + KD * TC].rearrange(
                        "p (k t) -> p k t", k=KD), xchunks(xn_h, p),
                        waits=[("pe", gpe)] + CW)
                    state["ld"] = P.snap("ld")
                units.append(u_load)

                def mk_qkf(wi, f):
                    def u():
                        gpe = st["pair_end"].get(p - 2, 0)
                        wmm = state["ld"] + CW + \
                            ([st["ev"]] if st["ev"] else [])
                        for k in range(KD):
                            mm(pj[:], wsb[:, (wi * KD + k) * D + 128 * f:
                                          (wi * KD + k) * D + 128 * (f + 1)],
                               xnsb[:, xno + k * TC:xno + (k + 1) * TC],
                               k == 0, k == KD - 1, waits=wmm,
                               inc=(k == KD - 1))
                        pev = P.snap("pe")
                        dsto = qko + wi * T
                        last = 0
                        for cpar in range(2):
                            for th in range(2):
                                src = pj[64 * cpar:64 * (cpar + 1),
                                         256 * th:256 * (th + 1)]
                                dst = qksb[64 * th:64 * (th + 1),
                                           dsto + (2 * f + cpar) * 256:
                                           dsto + (2 * f + cpar) * 256 + 256]
                                last = vcopy(dst, src,
                                             waits=pev + [("pe", gpe)])
                        st["ev"] = ("dve", last)
                    return u
                for wi in range(2):
                    for f in range(KD):
                        units.append(mk_qkf(wi, f))

                def mk_v(ttl, n):
                    def u():
                        gpe = st["pair_end"].get(p - 2, 0)
                        wmm = state["ld"] + CW + \
                            ([st["ev"]] if st["ev"] else [])
                        for k in range(KD):
                            mm(pj[:], xnsb[:, xno + k * TC + 128 * ttl:
                                           xno + k * TC + 128 * (ttl + 1)],
                               wsb[:, (2 * KD + k) * D + n * TC:
                                   (2 * KD + k) * D + (n + 1) * TC],
                               k == 0, k == KD - 1, waits=wmm,
                               inc=(k == KD - 1))
                        pev = P.snap("pe")
                        base = vpo + ttl * 1040 + n * 8 * 65
                        dst = vpsb[:, base:base + 8 * 65].rearrange(
                            "p (c d) -> p c d", c=8)[:, :, 0:64]
                        srcap = pj[:].rearrange("p (c d) -> p c d", c=8)
                        v = vcopy(dst, srcap, waits=pev + [("pe", gpe)])
                        st["ev"] = ("dve", v)
                        if (ttl, n) == (3, 1):
                            chunk_w[p] = [("dve", v)]
                    return u
                for ttl in range(4):
                    for n in range(2):
                        units.append(mk_v(ttl, n))
                return units

            def qkops(jg, w):
                p, r = divmod(jg, 256)
                qc, kt = divmod(r, 32)
                qko = (p % 2) * 2 * T
                sbk = sp2[jg % 2]
                mm(sbk[:, 0:TC],
                   qksb[0:64, qko + T + kt * 128:qko + T + (kt + 1) * 128],
                   qksb[0:64, qko + qc * TC:qko + (qc + 1) * TC],
                   True, True, waits=w, tp=(0, 0))
                return mm(sbk[:, TC:1024],
                          qksb[64:128, qko + T + kt * 128:qko + T + (kt + 1) * 128],
                          qksb[64:128, qko + qc * TC:qko + (qc + 1) * TC],
                          True, True, tp=(64, 0), inc=True)

            def pvops(jg, w, start, stop):
                p, r = divmod(jg, 256)
                qc, kt = divmod(r, 32)
                vpo = (p % 2) * 4 * 1040
                ip = jg % 2
                vA = vpsb[:, vpo + (kt % 2) * 1040 + (kt // 2) * 65:
                          vpo + (kt % 2) * 1040 + (kt // 2) * 65 + 65]
                vB = vpsb[:, vpo + (2 + kt % 2) * 1040 + (kt // 2) * 65:
                          vpo + (2 + kt % 2) * 1040 + (kt // 2) * 65 + 65]
                mm(oAB[:, 0:TC], vA, ptsb[:, ip * 1024:ip * 1024 + TC],
                   start, stop, waits=w)
                return mm(oAB[:, TC:1024], vB,
                          ptsb[:, ip * 1024 + TC:(ip + 1) * 1024],
                          start, stop, inc=True)

            def store_a(e, spar, n2):
                p, qc = divmod(e, 8)
                gq = 0
                for hh in range(2):
                    hgl = 2 * p + hh
                    src = aostg[64 * hh:64 * (hh + 1),
                                spar * TC:(spar + 1) * TC].rearrange(
                        "d (c s) -> d c s", c=2)
                    dst = ao_h.rearrange("(c dd) t -> dd c t", dd=64)[
                        :, 2 * qc:2 * qc + 2, 256 * hgl:256 * (hgl + 1)]
                    gq = dma("gpsimd", dst, src, waits=[("dve", n2)])
                return gq

            def end_a(e, pvlast):
                p, qc = divmod(e, 8)
                if qc == 7:
                    st["pair_end"][p] = pvlast

            epiops = mk_epilogue(oAB, rp, repsb, aostg, store_a, end_a)

            cst = {}

            def mk_cu(i, src, dst):
                par = i % 2

                def u():
                    f_ap = cvt_f[:, par * 1024:(par + 1) * 1024]
                    b_ap = cvt_b[:, par * 1024:(par + 1) * 1024]
                    dma("sync", f_ap, src,
                        waits=[("dve", cst.get(("c", par), 0))])
                    cv = vcopy(b_ap, f_ap,
                               waits=P.snap("ld") + [("g", cst.get(("s", par), 0))])
                    cst[("c", par)] = cv
                    gv = dma("gpsimd", dst, b_ap, waits=[("dve", cv)])
                    cst[("s", par)] = gv
                return u

            cunits = [mk_cu(i, s, d) for i, (s, d) in enumerate(conv_jobs())]

            for u in proj_units(0):
                u()
            feeds = {}
            for p in range(1, 8):
                pu = proj_units(p)
                take, cunits = cunits[:31], cunits[31:]
                merged = []
                for a, b in zip(pu, take):
                    merged += [a, b]
                merged += pu[len(take):] + take[len(pu):]
                feeds[p] = merged
            run_scores(8, sp2, ptsb, qkops, pvops, epiops, feeds, chunk_w)
            assert not cunits
            P.wait("sync", P.snap("g", "pe", "dve", "acts"))
            CONV["done"] = P.snap("ld", "g", "dve")

    def emit_ffnA(call):
        ctx = ExitStack()
        tag = str(len(P.q["sync"]))
        sb = lambda n, s, d: ctx.enter_context(nc.sbuf_tensor(n + "_" + tag, list(s), d))
        wsb = sb("fa_w", (128, 2 * 4 * D), BF16)
        wscr = sb("fa_ws", (128, D), F32)
        xnsb = sb("fa_xn", (128, KD * T), BF16)
        sasb = sb("fa_sa", (128, 2 * 2 * TC), BF16)
        ehsb = sb("fa_eh", (128, 2 * 2 * TC), BF16)
        gwsb = sb("fa_gw", (128, 80), BF16)
        glb = sb("fa_gl", (128, 576), F32)
        eqb = sb("fa_eq", (128, 256), F32)
        wtb = sb("fa_wtb", (128, 256), BF16)
        wtstg = sb("fa_wts", (128, 256), BF16)
        pa = ctx.enter_context(nc.psum_tensor("fa_p_" + tag, [128, 2 * TC], F32))
        pa2 = ctx.enter_context(nc.psum_tensor("fa_p2_" + tag, [128, 2 * TC], F32))
        pb_ = ctx.enter_context(nc.psum_tensor("fa_pb_" + tag, [128, 2 * TC], F32))
        pb2 = ctx.enter_context(nc.psum_tensor("fa_pb2_" + tag, [128, 2 * TC], F32))

        def xnkt(k, tc):
            return xnsb[:, k * T + tc * TC:k * T + (tc + 1) * TC]

        def xnjt(k, j):
            return xnsb[:, k * T + 128 * j:k * T + 128 * (j + 1)]

        with ctx:
            for i in range(KD):
                dma("sync", wscr[:, 0:10], wgeT[128 * i:128 * (i + 1), :],
                    waits=P.snap("dve"))
                vcopy(gwsb[:, 10 * i:10 * (i + 1)], wscr[:, 0:10],
                      waits=P.snap("ld"))
            for tc in range(NTC):
                dma("gpsimd", xnsb[:].rearrange("p (k t) -> p k t", k=KD)[
                    :, :, tc * TC:(tc + 1) * TC], xchunks(xn_h, tc))
            gv = P.snap("g", "dve")
            NJ = 32
            GLb = glb[:, 0:10 * NJ]
            GL3 = GLb.rearrange("p (j c) -> p j c", j=NJ)
            Dlt = glb[:, 320:320 + NJ]
            SEL = glb[:, 352:352 + NJ]
            GP = glb[:, 384:384 + NJ]
            EMX = glb[:, 416:416 + NJ]
            ESM = glb[:, 448:448 + NJ]
            FCT = glb[:, 480:480 + NJ]
            V1 = glb[:, 512:512 + NJ]
            V2 = glb[:, 544:544 + NJ]
            EQ3 = eqb[:].rearrange("p (j c) -> p j c", j=NJ)
            WT3 = wtb[:].rearrange("p (j c) -> p j c", j=NJ)
            j3 = lambda ap: ap.rearrange("p (j o) -> p j o", j=NJ)
            for j in range(NJ):
                for k in range(KD):
                    mm(pb_[:, 10 * j:10 * (j + 1)], xnjt(k, j),
                       gwsb[:, 10 * k:10 * (k + 1)], k == 0, k == KD - 1,
                       waits=gv, inc=(j == NJ - 1 and k == KD - 1))
            d = vcopy(GLb, pb_[:, 0:10 * NJ], waits=P.snap("pe"))
            d = vtt(GLb, GLb, g320_s[:], ALU.add,
                    waits=[("dve", d)] + CONST_READY)
            d = vtt(j3(Dlt), GL3[:, :, 1:2], GL3[:, :, 0:1], ALU.subtract,
                    waits=[("dve", d)])
            d = ew("vector", lambda e: e.tensor_single_scalar(
                SEL, Dlt, 0.0, ALU.is_gt), waits=[("dve", d)])
            a = ew("scalar", lambda e: e.activation(GP, Dlt, AF.Abs),
                   waits=[("dve", d), ("acts", P.val("acts"))])
            a = vact(GP, GP, AF.Sigmoid, waits=[("acts", a)])
            d = ew("vector", lambda e: e.tensor_scalar_mul(SEL, SEL, GMASK),
                   waits=[("dve", d)])
            dm = d
            for j in range(NJ):
                dm = ew("vector", lambda e, j=j: e.tensor_scalar_sub(
                    GL3[:, j:j + 1, 2:6], GL3[:, j:j + 1, 2:6], SEL[:, j:j + 1]),
                    waits=[("dve", d)])
            d = ew("vector", lambda e: e.tensor_scalar(SEL, SEL, -1.0, GMASK,
                                                       ALU.mult, ALU.add),
                   waits=[("dve", dm)])
            dm = d
            for j in range(NJ):
                dm = ew("vector", lambda e, j=j: e.tensor_scalar_sub(
                    GL3[:, j:j + 1, 6:10], GL3[:, j:j + 1, 6:10], SEL[:, j:j + 1]),
                    waits=[("dve", d)])
            d = ew("vector", lambda e: e.reduce_max(
                j3(EMX), GL3[:, :, 2:10], axis=AX.X, negate=True),
                waits=[("dve", dm)])
            av = a
            for j in range(NJ):
                av = vact(GL3[:, j:j + 1, 2:10], GL3[:, j:j + 1, 2:10], AF.Exp,
                          bias=EMX[:, j:j + 1], accum=ESM[:, j:j + 1],
                          waits=[("dve", d), ("acts", a)])
            d = ew("vector", lambda e: e.reciprocal(FCT, ESM),
                   waits=[("acts", av)])
            d = vtt(FCT, FCT, GP, ALU.mult, waits=[("dve", d)])
            dm = d
            for j in range(NJ):
                dm = ew("vector", lambda e, j=j: e.tensor_scalar_mul(
                    GL3[:, j:j + 1, 2:10], GL3[:, j:j + 1, 2:10], FCT[:, j:j + 1]),
                    waits=[("dve", d)])
            d = ew("vector", lambda e: e.reduce_max(
                j3(V1), GL3[:, :, 2:10], axis=AX.X), waits=[("dve", dm)])
            dm = d
            for j in range(NJ):
                dm = ew("vector", lambda e, j=j: e.tensor_single_scalar(
                    EQ3[:, j:j + 1, :], GL3[:, j:j + 1, 2:10], V1[:, j:j + 1],
                    ALU.not_equal), waits=[("dve", d)])
            d = vtt(EQ3[:, :, :], GL3[:, :, 2:10], EQ3[:, :, :], ALU.mult,
                    waits=[("dve", dm)])
            d = ew("vector", lambda e: e.reduce_max(
                j3(V2), EQ3[:, :, :], axis=AX.X), waits=[("dve", d)])
            dm = d
            for j in range(NJ):
                dm = ew("vector", lambda e, j=j: e.tensor_single_scalar(
                    EQ3[:, j:j + 1, :], GL3[:, j:j + 1, 2:10], V2[:, j:j + 1], ALU.is_ge),
                    waits=[("dve", d)])
            for h in range(2):
                d = vtt(wtb[:, 128 * h:128 * (h + 1)].rearrange(
                            "p (e j) -> p j e", e=NEXP),
                        GL3[:, 16 * h:16 * (h + 1), 2:10],
                        EQ3[:, 16 * h:16 * (h + 1), :], ALU.mult,
                        waits=[("dve", dm)])
                P.op("tensor",
                     lambda e, o=pb2[:].bitcast(BF16)[0:128, 128 * h:128 * (h + 1)],
                     a=wtb[:, 128 * h:128 * (h + 1)], i=ident_s[:]:
                     e.transpose(o, a, i),
                     waits=[("dve", d)] + CONST_READY, inc=("pe", 1))
                te = P.val("pe")
                dv = vcopy(wtstg[:, 128 * h:128 * (h + 1)],
                           pb2[:].bitcast(BF16)[0:128, 128 * h:128 * (h + 1)],
                           waits=[("pe", te)])
                for ex in range(NEXP):
                    dma("gpsimd",
                        wse2_h[0:1, :].rearrange("o (e j c) -> o e j c",
                                                 e=NEXP, j=NJ)[
                            :, ex, 16 * h:16 * (h + 1), :],
                        wtstg[16 * ex:16 * (ex + 1),
                              128 * h:128 * (h + 1)],
                        waits=[("dve", dv)])
            GATE_DONE = [("dve", P.val("dve"))]

            prevw = {}
            for hc in range(H16):
                par = hc % 2
                wo = par * 4 * D

                def wcol(part, k, wo=wo):
                    return wsb[:, wo + part * D + 128 * k:wo + part * D + 128 * (k + 1)]

                wpar = prevw.get(par, [])
                for part in range(4):
                    colo = (part % 2) * HID + 128 * hc
                    cache_colo = (0 if part < 2 else 2 * HID) + colo
                    dma("sync",
                        wsb[:, wo + part * D:wo + (part + 1) * D].rearrange(
                            "p (k c) -> p k c", k=KD),
                        wffbf_h.rearrange("(k p) c -> p k c", p=128)[
                            :, :, cache_colo:cache_colo + 128],
                        waits=wpar + CONV["done"])
                WL = P.snap("ld", "dve")
                for tc in range(NTC):
                    tpar = tc % 2
                    pA_ = pa if tpar == 0 else pb_
                    pB_ = pa2 if tpar == 0 else pb2
                    wmm = WL + gv + prevw.get(("ps", tpar), []) + \
                        (GATE_DONE if (hc == 0 and tpar == 1) else [])
                    for part in range(4):
                        dst = (pA_ if part < 2 else pB_)[
                            :, (part % 2) * TC:(part % 2 + 1) * TC]
                        for k in range(KD):
                            mm(dst, wcol(part, k), xnkt(k, tc),
                               k == 0, k == KD - 1, waits=wmm, inc=(k == KD - 1))
                    pev = P.snap("pe")
                    so = tpar * 2 * TC
                    a1 = vact(sasb[:, so:so + TC], pA_[:, 0:TC], AF.Silu, waits=pev)
                    a2 = vact(sasb[:, so + TC:so + 2 * TC], pB_[:, 0:TC], AF.Silu,
                              waits=[("acts", a1)])
                    d1 = vtt(ehsb[:, so:so + TC], sasb[:, so:so + TC],
                             pA_[:, TC:2 * TC], ALU.mult, waits=[("acts", a1)])
                    d2 = vtt(ehsb[:, so + TC:so + 2 * TC], sasb[:, so + TC:so + 2 * TC],
                             pB_[:, TC:2 * TC], ALU.mult, waits=[("acts", a2)])
                    dma("gpsimd",
                        e_h[128 * hc:128 * (hc + 1), tc * TC:(tc + 1) * TC],
                        ehsb[:, so:so + TC], waits=[("dve", d1)])
                    dma("gpsimd",
                        hh_h[128 * hc:128 * (hc + 1), tc * TC:(tc + 1) * TC],
                        ehsb[:, so + TC:so + 2 * TC], waits=[("dve", d2)])
                    prevw[("ps", tpar)] = [("dve", d2), ("acts", a2),
                                           ("g", P.val("g"))]
                prevw[par] = [("pe", P.val("pe"))]
            P.wait("sync", P.snap("g", "pe", "dve", "acts"))

    def emit_ffnB(call, x_src, x_dst, ln=False):
        ctx = ExitStack()
        tag = str(len(P.q["sync"]))
        sb = lambda n, s, d: ctx.enter_context(nc.sbuf_tensor(n + "_" + tag, list(s), d))
        wosb = sb("fb_wo", (128, H16 * D), BF16)
        wscr = sb("fb_ws", (128, D), F32)
        w2sb = sb("fb_w2", (128, 2 * 4 * D), BF16)
        esb = sb("fb_e", (128, H16 * TC), BF16)
        hsb = sb("fb_h", (128, H16 * TC), BF16)
        scs = sb("fb_scs", (128, 2 * H16 * TC), BF16)
        repsb = sb("fb_rep", (128, NEXP * TC), BF16)
        wserow = sb("fb_wr", (1, NEXP * TC), BF16)
        xsb = sb("fb_x", (128, KD * TC), F32)
        osb = sb("fb_o", (128, KD * TC), F32)
        if ln:
            lsq = sb("fb_sq", (128, KD * TC), BF16)
            lxn = sb("fb_xn", (128, KD * TC), BF16)
            mrep_s = sb("fb_mr", (128, TC), F32)
            fb_st = sb("fb_st", (1, 5 * TC), F32)
            lt = xsb
        prt = ctx.enter_context(nc.psum_tensor("fb_p_" + tag, [128, 8 * TC], F32))
        with ctx:
            for i in range(H16):
                dma("sync", wosb[:, i * D:(i + 1) * D],
                    woutbf_h[128 * i:128 * (i + 1), :], waits=CONV["done"])
            CW = P.snap("dve", "ld")
            prev = {}
            w2done = {}
            for tc in range(NTC):
                w = list(CW) + prev.get("cons", [])
                dma("sync", esb[:].rearrange("p (k t) -> p k t", k=H16),
                    e_h.rearrange("(k p) t -> p k t", p=128)[
                        :, :, tc * TC:(tc + 1) * TC], waits=w)
                dma("sync", hsb[:].rearrange("p (k t) -> p k t", k=H16),
                    hh_h.rearrange("(k p) t -> p k t", p=128)[
                        :, :, tc * TC:(tc + 1) * TC])
                dma("gpsimd", xsb[:].rearrange("p (k t) -> p k t", k=KD),
                    xchunks(x_src, tc), waits=w)
                ldv = P.snap("ld")
                gvx = P.snap("g")
                dma("sync",
                    wserow[0:1, :].rearrange("o (e t) -> o e t", e=NEXP),
                    wse2_h[0:1, :].rearrange("o (e t) -> o e t", e=NEXP)[
                        :, :, tc * TC:(tc + 1) * TC],
                    waits=[("pe", P.val("pe"))])
                ldw = P.snap("ld")
                reps = []
                for ex in range(NEXP):
                    wb = prev.get("evac", []) if ex < 4 else \
                        prev.get("evac", []) + prev.get("evac4", [])
                    te = mm(prt[:, ex * TC:(ex + 1) * TC], ones_row_b[:],
                            wserow[0:1, ex * TC:(ex + 1) * TC],
                            True, True,
                            waits=ldw + wb + CONST_READY,
                            inc=True)
                    reps.append(vact(repsb[:, ex * TC:(ex + 1) * TC],
                                     prt[:, ex * TC:(ex + 1) * TC], AF.Copy,
                                     waits=[("pe", te)]))
                REPS = [("acts", reps[-1])]
                for pch in range(KD):
                    wmm = ldv + [("acts", reps[pch])] + prev.get("evac", [])
                    for k in range(H16):
                        mm(prt[:, pch * TC:(pch + 1) * TC],
                           wosb[:, k * D + 128 * pch:k * D + 128 * (pch + 1)],
                           esb[:, k * TC:(k + 1) * TC],
                           k == 0, False, waits=wmm, inc=(k == H16 - 1))
                prevex = {}
                for ex in range(NEXP):
                    spar = ex % 2
                    sco = spar * H16 * TC
                    scv = []
                    wprev = prevex.get(spar, [])
                    for k in range(H16):
                        eng = "vector" if k % 2 == 0 else "gpsimd"
                        fn = (lambda e, o=scs[:, sco + k * TC:sco + (k + 1) * TC],
                              a=hsb[:, k * TC:(k + 1) * TC],
                              b=repsb[:, ex * TC:(ex + 1) * TC]:
                              e.tensor_tensor(o, a, b, ALU.mult))
                        scv.append((ECTR[eng],
                                    ew(eng, fn,
                                       waits=[("acts", reps[ex])] + ldv + wprev)))
                    for gq in range(4):
                        gpar = (ex * 4 + gq) % 2
                        wo2 = gpar * 4 * D
                        wld = w2done.get(gpar, [])
                        dma("sync",
                            w2sb[:, wo2:wo2 + 4 * D].rearrange(
                                "p (k c) -> p k c", k=4),
                            w2bf_h[ex].rearrange("(q p) c -> p q c", p=128)[
                                :, gq * 4:(gq + 1) * 4, :],
                            waits=wld + CONV["done"])
                        wgrp = P.snap("ld")
                        for kk in range(4):
                            k = gq * 4 + kk
                            laststop = (ex == NEXP - 1 and k == H16 - 1)
                            for pch in range(KD):
                                mm(prt[:, pch * TC:(pch + 1) * TC],
                                   w2sb[:, wo2 + kk * D + 128 * pch:
                                        wo2 + kk * D + 128 * (pch + 1)],
                                   scs[:, sco + k * TC:sco + (k + 1) * TC],
                                   False, laststop,
                                   waits=wgrp + [scv[k]],
                                   inc=(pch == KD - 1 and kk == 3))
                        w2done[gpar] = [("pe", P.val("pe"))]
                    prevex[spar] = [("pe", P.val("pe"))]
                pev = P.snap("pe")
                evs = []
                for pch in range(KD):
                    evs.append(vtt(osb[:, pch * TC:(pch + 1) * TC],
                                   prt[:, pch * TC:(pch + 1) * TC],
                                   xsb[:, pch * TC:(pch + 1) * TC], ALU.add,
                                   waits=pev + gvx))
                gv2 = dma("gpsimd", xchunks(x_dst, tc),
                          osb[:].rearrange("p (k t) -> p k t", k=KD),
                          waits=[("dve", evs[-1])])
                if not ln:
                    prev["cons"] = [("dve", evs[-1]), ("pe", P.val("pe")),
                                    ("g", gv2), ("pool", P.val("pool"))]
                    prev["evac"] = [("dve", evs[-1])]
                    continue
                OK = [("dve", evs[-1])]
                ov = lambda k: osb[:, k * TC:(k + 1) * TC]
                SM = fb_st[0:1, 0:TC]
                MEAN = fb_st[0:1, TC:2 * TC]
                RINV = fb_st[0:1, 2 * TC:3 * TC]
                VAR = fb_st[0:1, 3 * TC:4 * TC]
                SD = fb_st[0:1, 4 * TC:5 * TC]
                pssum = prt[0:1, 6 * TC:6 * TC + TC]
                psq = prt[0:1, 7 * TC:7 * TC + TC]
                pm = prt[:, 4 * TC:5 * TC]
                pr = prt[:, 5 * TC:6 * TC]
                sqv = []
                for k in range(KD):
                    sqv.append(vact(lsq[:, k * TC:(k + 1) * TC], ov(k),
                                    AF.Square,
                                    waits=OK + prev.get("lnsq", [])))
                for k in range(KD):
                    mm(pssum, ones_col_f[:], ov(k), k == 0, k == KD - 1,
                       waits=OK, inc=(k == KD - 1))
                for k in range(KD):
                    mm(psq, ones_col_b[:], lsq[:, k * TC:(k + 1) * TC],
                       k == 0, k == KD - 1,
                       waits=OK + [("acts", sqv[k])], inc=(k == KD - 1))
                pev2 = P.snap("pe")
                prev["lnsq"] = [("pe", P.val("pe"))]
                d0 = ew("vector", lambda e, M=MEAN, ps=pssum:
                        e.tensor_scalar_mul(M, ps, 1.0 / D), waits=pev2)
                d0 = ew("vector", lambda e, V=VAR, ps=psq:
                        e.tensor_scalar_mul(V, ps, 1.0 / D), waits=[("dve", d0)])
                d0 = vtt(SM, MEAN, MEAN, ALU.mult, waits=[("dve", d0)])
                d1 = vtt(VAR, VAR, SM, ALU.subtract, waits=[("dve", d0)])
                a1 = vact(SD, VAR, AF.Ln, bias=eps_t[0:1, 0:1],
                          waits=[("dve", d1)] + CONST_READY)
                a1 = vact(RINV, SD, AF.Exp, scale=-0.5, waits=[("acts", a1)])
                mm(pm[:], ones_row_f[0:1, :], MEAN, True, True,
                   waits=[("dve", d1)] + prev.get("lnrep", []))
                pev3 = mm(pr[:], ones_row_f[0:1, :], RINV, True, True,
                          waits=[("acts", a1)], inc=True)
                a2 = acopy(mrep_s[:], pm[:], waits=[("pe", pev3)])
                pl = []
                for k in range(KD):
                    eng = "gpsimd" if k % 2 == 0 else "vector"
                    v = ew(eng, lambda e, o=lt[:, k * TC:(k + 1) * TC],
                           a=ov(k), b=mrep_s[:]:
                           e.tensor_tensor(o, a, b, ALU.subtract),
                           waits=[("acts", a2), ("g", gv2)])
                    pl.append((ECTR[eng], v))
                dv = []
                for k in range(KD):
                    dv.append(vtt(lxn[:, k * TC:(k + 1) * TC],
                                  lt[:, k * TC:(k + 1) * TC],
                                  pr[:], ALU.mult, waits=[pl[k]]))
                gv3 = dma("gpsimd", xchunks(xn_h, tc),
                          lxn[:].rearrange("p (k t) -> p k t", k=KD),
                          waits=[("dve", dv[-1])])
                prev["lnrep"] = [("acts", a2), pl[-2], pl[-1], ("dve", dv[-1])]
                prev["cons"] = [("dve", dv[-1]), ("pe", P.val("pe")),
                                ("g", gv3), pl[-2], pl[-1],
                                ("acts", sqv[-1]), ("pool", P.val("pool"))]
                prev["evac"] = [("dve", evs[-1])]
                prev["evac4"] = [("acts", a2), ("dve", dv[-1])]
            P.wait("sync", P.snap("g", "pe", "dve", "acts", "pool"))

    def emit_attn_b():
        ctx = ExitStack()
        tag = str(len(P.q["sync"]))
        sb = lambda n, s, d: ctx.enter_context(nc.sbuf_tensor(n + "_" + tag, list(s), d))
        wsb = sb("ab_w", (128, KD * D), BF16)
        xnsb = sb("ab_xn", (128, KD * TC), BF16)
        ktsb = sb("ab_kt", (128, KD * T), BF16)
        vpsb = sb("ab_vp", (128, 32 * 1040), BF16)
        qrsb = sb("ab_qct", (128, 2 * KD * TC), BF16)
        cssb = sb("ab_cs", (128, 4 * TC), F32)
        scr = sb("ab_scr", (128, D), F32)
        rfsb = scr[:, 0:TC]
        rwsb = scr[:, TC:2 * TC]
        t1sb = sb("ab_t1", (128, D), F32)
        t2sb = sb("ab_t2", (128, D), F32)
        wscr = t1sb
        ptsb = sb("ab_pt", (128, 2 * 1024), BF16)
        repsb = sb("ab_rep", (128, TC), F32)
        aostg = sb("ab_stg", (128, 2 * TC), BF16)
        s0 = ctx.enter_context(nc.psum_tensor("ab_s0_" + tag, [128, 1024], F32))
        s1 = ctx.enter_context(nc.psum_tensor("ab_s1_" + tag, [128, 1024], F32))
        oAB = ctx.enter_context(nc.psum_tensor("ab_o_" + tag, [65, 1024], F32))
        pj = ctx.enter_context(nc.psum_tensor("ab_pj_" + tag, [128, TC], F32))
        rp = ctx.enter_context(nc.psum_tensor("ab_rp_" + tag, [128, TC], F32))

        def rot_featmajor(dstbf, ldcs, src, cso):
            a = acopy(rfsb[:], src, waits=[("pe", P.val("pe")),
                                           ("pool", P.val("pool")),
                                           ("dve", P.val("dve"))])
            gsw = 0
            for qd in range(4):
                sq = qd ^ 1
                gsw = dma("gpsimd", rwsb[32 * qd:32 * (qd + 1), :],
                          rfsb[32 * sq:32 * (sq + 1), :], waits=[("acts", a)])
            p1 = ptt(t1sb[:, 0:TC], rfsb[:], cssb[:, cso:cso + TC], ALU.mult,
                     waits=[("acts", a)] + ldcs)
            d1 = vtt(t2sb[:, 0:TC], rwsb[:], cssb[:, cso + TC:cso + 2 * TC],
                     ALU.mult, waits=[("g", gsw)] + ldcs)
            d2 = vtt(dstbf, t1sb[:, 0:TC], t2sb[:, 0:TC], ALU.add,
                     waits=[("pool", p1), ("dve", d1)])
            return d2

        with ctx:
            conv_w(Wkvb, [wsb[:, i * D:(i + 1) * D] for i in range(KD)], wscr)
            ew("vector", lambda e: e.memset(vpsb[:], 1.0))
            CW = P.snap("dve")
            banksK = [s0[:, 0:TC], s0[:, TC:2 * TC], s1[:, 0:TC],
                      s1[:, TC:2 * TC], pj[:], rp[:]]
            banksT = [b.bitcast(BF16)[:, 0:128] for b in banksK]
            NBK = 6
            bank_free = {}
            ib = 0
            prev = {}
            for tc in range(NTC):
                w = list(CW) + prev.get("c", [])
                dma("gpsimd", xnsb[:].rearrange("p (k t) -> p k t", k=KD),
                    xchunks(xn_h, tc), waits=w)
                gxn = P.snap("g")
                cso = (tc % 2) * 2 * TC
                dma("sync", cssb[:, cso:cso + TC],
                    cosF[:, tc * TC:(tc + 1) * TC], waits=w)
                dma("sync", cssb[:, cso + TC:cso + 2 * TC],
                    sinF[:, tc * TC:(tc + 1) * TC])
                ldcs = P.snap("ld")
                xnk = lambda k: xnsb[:, k * TC:(k + 1) * TC]
                rotd = {}
                for f in range(KD):
                    b = ib % NBK
                    ib += 1
                    wmm = gxn + bank_free.get(b, [])
                    for k in range(KD):
                        mm(banksK[b], wsb[:, k * D + 128 * f:k * D + 128 * (f + 1)],
                           xnk(k), k == 0, k == KD - 1, waits=wmm,
                           inc=(k == KD - 1))
                    rotd[f] = rot_featmajor(
                        ktsb[:, f * T + tc * TC:f * T + (tc + 1) * TC],
                        ldcs, banksK[b], cso)
                    bank_free[b] = [("acts", P.val("acts"))]
                for tl in range(4):
                    kt = tc * 4 + tl
                    for f in range(KD):
                        b = ib % NBK
                        ib += 1
                        obf = banksT[b]
                        wtr = [("dve", rotd[f])] + bank_free.get(b, [])
                        P.op("tensor",
                             lambda e, o=obf,
                             a=ktsb[:, f * T + kt * 128:f * T + (kt + 1) * 128],
                             i=ident_s[:]: e.transpose(o, a, i),
                             waits=wtr + CONST_READY, inc=("pe", 1))
                        te = P.val("pe")
                        base = kt * 1040 + 2 * f * 65
                        dst = vpsb[:, base:base + 130].rearrange(
                            "p (c d) -> p c d", c=2)[:, :, 0:64]
                        dv = vcopy(dst, obf.rearrange("p (c d) -> p c d", c=2),
                                   waits=[("pe", te)])
                        bank_free[b] = [("dve", dv)]
                prev["c"] = [("dve", P.val("dve")), ("pe", P.val("pe")),
                             ("pool", P.val("pool")), ("acts", P.val("acts")),
                             ("g", P.val("g"))]
            P.wait("sync", P.snap("pe", "dve", "acts", "pool"))
            for i in range(KD):
                dma("sync", wsb[:, i * D:(i + 1) * D],
                    wqbf_h[128 * i:128 * (i + 1), :], waits=CONV["done"])
            KVD = P.snap("dve", "pool", "pe", "acts", "g", "ld")
            sp2 = [s0, s1]
            st = {"chunk_end": {}, "proj_pe": {}}
            chunk_w = {}
            est = {"cp": 0, "rppe": {}}
            rst = {"cpv": 0, "gsw": 0, "p1": 0, "d2": 0}

            def rot_q(dstbf, ldcs, base, guard_pe, cso):
                cpv = vcopy(rfsb[:], pj[:],
                            waits=P.snap("pe") + base +
                            [("pool", rst["p1"]), ("g", rst["gsw"])])
                gsw = 0
                for qd in range(4):
                    sq = qd ^ 1
                    gsw = dma("gpsimd", rwsb[32 * qd:32 * (qd + 1), :],
                              rfsb[32 * sq:32 * (sq + 1), :],
                              waits=[("dve", cpv)])
                p1 = ptt(t1sb[:, 0:TC], rfsb[:], cssb[:, cso:cso + TC], ALU.mult,
                         waits=[("dve", cpv), ("dve", rst["d2"])] + ldcs + base)
                d1 = vtt(t2sb[:, 0:TC], rwsb[:], cssb[:, cso + TC:cso + 2 * TC],
                         ALU.mult, waits=[("g", gsw)] + ldcs)
                d2 = vtt(dstbf, t1sb[:, 0:TC], t2sb[:, 0:TC], ALU.add,
                         waits=[("pool", p1), ("pe", guard_pe)] + base)
                rst.update(cpv=cpv, gsw=gsw, p1=p1, d2=d2)
                return d2

            def prologue_units(qc, base):
                qo = (qc % 2) * KD * TC
                state = {"gld": [], "cs": []}
                units = []

                def u_load():
                    dma("gpsimd", xnsb[:].rearrange("p (k t) -> p k t", k=KD),
                        xchunks(xn_h, qc),
                        waits=[("pe", st["proj_pe"].get(qc - 1, 0))] + CW + base)
                    state["gld"] = P.snap("g")
                    cso = (qc % 2) * 2 * TC
                    dma("sync", cssb[:, cso:cso + TC],
                        cosF[:, qc * TC:(qc + 1) * TC],
                        waits=[("pool", rst["p1"]), ("dve", rst["d2"])] + base)
                    dma("sync", cssb[:, cso + TC:cso + 2 * TC],
                        sinF[:, qc * TC:(qc + 1) * TC])
                    state["cs"] = P.snap("ld")
                units.append(u_load)

                def mk_rot(f):
                    def u():
                        wmm = state["gld"] + base + [("dve", rst["cpv"])]
                        for k in range(KD):
                            mm(pj[:], wsb[:, k * D + 128 * f:k * D + 128 * (f + 1)],
                               xnsb[:, k * TC:(k + 1) * TC], k == 0, k == KD - 1,
                               waits=wmm, inc=(k == KD - 1))
                        d2 = rot_q(qrsb[:, qo + f * TC:qo + (f + 1) * TC],
                                   state["cs"], base,
                                   st["chunk_end"].get(qc - 2, 0),
                                   (qc % 2) * 2 * TC)
                        if f == KD - 1:
                            chunk_w[qc] = [("dve", d2)] + base
                            st["proj_pe"][qc] = P.val("pe")
                    return u
                for f in range(KD):
                    units.append(mk_rot(f))
                return units

            def qkops(jg, w):
                qc, r = divmod(jg, 256)
                f, kt = divmod(r, 32)
                qo = (qc % 2) * KD * TC
                sbk = sp2[jg % 2]
                mm(sbk[:, 0:TC],
                   ktsb[0:64, f * T + kt * 128:f * T + (kt + 1) * 128],
                   qrsb[0:64, qo + f * TC:qo + (f + 1) * TC],
                   True, True, waits=w, tp=(0, 0))
                return mm(sbk[:, TC:1024],
                          ktsb[64:128, f * T + kt * 128:f * T + (kt + 1) * 128],
                          qrsb[64:128, qo + f * TC:qo + (f + 1) * TC],
                          True, True, tp=(64, 0), inc=True)

            def pvops(jg, w, start, stop):
                qc, r = divmod(jg, 256)
                f, kt = divmod(r, 32)
                ip = jg % 2
                h0 = 2 * f
                vA = vpsb[:, kt * 1040 + h0 * 65:kt * 1040 + h0 * 65 + 65]
                vB = vpsb[:, kt * 1040 + (h0 + 1) * 65:
                          kt * 1040 + (h0 + 1) * 65 + 65]
                mm(oAB[:, 0:TC], vA, ptsb[:, ip * 1024:ip * 1024 + TC],
                   start, stop, waits=w)
                return mm(oAB[:, TC:1024], vB,
                          ptsb[:, ip * 1024 + TC:(ip + 1) * 1024],
                          start, stop, inc=True)

            def store_b(e, spar, n2):
                qc, f = divmod(e, 8)
                return dma("gpsimd",
                           ao_h.rearrange("(k p) t -> p k t", p=128)[
                               :, f, qc * TC:(qc + 1) * TC],
                           aostg[:, spar * TC:(spar + 1) * TC],
                           waits=[("dve", n2)])

            def end_b(e, pvlast):
                qc, f = divmod(e, 8)
                if f == 7:
                    st["chunk_end"][qc] = pvlast

            epiops = mk_epilogue(oAB, rp, repsb, aostg, store_b, end_b)

            for u in prologue_units(0, KVD):
                u()
            feeds = {qc: prologue_units(qc, []) for qc in range(1, 8)}
            run_scores(8, sp2, ptsb, qkops, pvops, epiops, feeds, chunk_w)
            P.wait("sync", P.snap("g", "pe", "dve", "acts", "pool"))

    seq = {
        "ln1": lambda: emit_ln(xT_in),
        "attna": emit_attn_a,
        "proja": lambda: emit_outproj(woaT, xT_in, x_h0, None),
        "ffn1": lambda: (emit_ffnA(0), emit_ffnB(0, x_h0, x_h1, ln=True)),
        "attnb": emit_attn_b,
        "projb": lambda: emit_outproj(Woutb, x_h1, x_h0, boutb_s),
        "ffn2": lambda: (emit_ffnA(1), emit_ffnB(1, x_h0, out_T)),
    }
    for ph in phases:
        seq[ph]()

    P.wait("sync", P.snap("ld", "g", "pe", "dve", "acts", "pool"))

    with nc.Block() as blk:
        P.replay(blk)
    root.close()
    return nc


_CACHED = {}


def _get_program():
    if "nc" not in _CACHED:
        _CACHED["nc"] = build_program()
    return _CACHED["nc"]


def kernel(**inputs):
    cosF_t, sinF_t = _host_tables()
    f32 = np.float32
    asnp = lambda a: np.asarray(a)
    common = {
        "wqaT": np.ascontiguousarray(asnp(inputs["L0_attn_a_wq"]).T.astype(f32)),
        "wkaT": np.ascontiguousarray(asnp(inputs["L0_attn_a_wk"]).T.astype(f32)),
        "wvaT": np.ascontiguousarray(asnp(inputs["L0_attn_a_wv"]).T.astype(f32)),
        "woaT": np.ascontiguousarray(asnp(inputs["L0_attn_a_wo"]).T.astype(f32)),
        "Wqb": np.ascontiguousarray(asnp(inputs["L0_attn_b_Wq"]).astype(f32)),
        "Wkvb": np.ascontiguousarray(asnp(inputs["L0_attn_b_Wkv"]).astype(f32)),
        "Woutb": np.ascontiguousarray(asnp(inputs["L0_attn_b_Wout"]).astype(f32)),
        "boutb": np.ascontiguousarray(asnp(inputs["L0_attn_b_bout"]).astype(f32).reshape(1, D)),
        "winT": np.ascontiguousarray(asnp(inputs["L0_ffn_w_in"]).T.astype(f32)),
        "w1T": np.ascontiguousarray(asnp(inputs["L0_ffn_w1"]).T.astype(f32)),
        "woutT": np.ascontiguousarray(asnp(inputs["L0_ffn_w_out"]).T.astype(f32)),
        "w2T": np.ascontiguousarray(asnp(inputs["L0_ffn_w2"]).transpose(0, 2, 1).astype(f32)),
        "wgeT": np.ascontiguousarray(np.concatenate(
            [asnp(inputs["L0_ffn_wg"]).T, asnp(inputs["L0_ffn_we"]).T], axis=1).astype(f32)),
        "gbeb": np.ascontiguousarray(np.concatenate(
            [asnp(inputs["L0_ffn_gb"]), asnp(inputs["L0_ffn_eb"])]).astype(f32).reshape(1, 10)),
        "cosF": cosF_t, "sinF": sinF_t,
    }
    x = asnp(inputs["L0_attn_a_x"])
    in_maps = []
    for b in range(8):
        m = dict(common)
        m["xT"] = np.ascontiguousarray(x[b].T.astype(f32))
        in_maps.append(m)
    nc = _get_program()
    res = run_bass_kernel_spmd(nc, in_maps, core_ids=list(range(8)))
    out = np.stack([np.ascontiguousarray(r["out_T"].T) for r in res.results], axis=0)
    return out
